# version 20
# baseline (speedup 1.0000x reference)
import contextlib

import numpy as np
import ml_dtypes

import concourse.bass as bass
import concourse.mybir as mybir
from concourse.bass_utils import run_bass_kernel_spmd

F32 = mybir.dt.float32
BF16 = mybir.dt.bfloat16
I32 = mybir.dt.int32
AF = mybir.ActivationFunctionType
ALU = mybir.AluOpType

D = 2048
T = 2048
HF = 8192
GN = 64
DCH = 16
TQ = 512
NTQ = 4
TH = 1024
NCHH = 8
NKV = 8
NQ = 16
EPS = 1e-5
FWHT_SCALE = 0.5 / float(np.sqrt(8192.0))
QSCALE = 1.0 / float(np.sqrt(128.0))
GINV = 1.0 / 16.0
POLL_MAX = 500000

ENGS = ["sp", "pe", "dve", "act", "pool"]
_BODY = {"sp": "sync", "pe": "tensor", "dve": "vector", "act": "scalar",
         "pool": "gpsimd"}


PARANOID = False
DEBUG_ATTN = False


class Sched:

    def __init__(self):
        self.ops = {e: [] for e in ENGS}
        self.ccnt = {e: 0 for e in ENGS}
        self.dcnt = {}
        self.down = {}
        self.dclosed = {}
        self.drelax = set()
        self.dflushonly = set()
        self.dflush = {}
        self.last_w = {}
        self.readers = {}
        self.floor = {}

    def op(self, eng, fn, reads=(), writes=(), dma=False, dsem=None):
        if dma:
            assert dsem is not None, "dma ops need a dsem"
            self.down.setdefault(dsem, set()).add(eng)
            assert not self.dclosed.get(dsem), f"dsem {dsem} closed"
        deps = {}
        floor = None
        if PARANOID:
            floor = {("c", e): n for e, n in self.ccnt.items()}
            for ds, n in self.dcnt.items():
                floor[("d", ds)] = n * 16
        else:
            floor = self.floor
        me_stream = ("d", dsem) if dma else ("c", eng)
        for key, tgt in floor.items():
            if key != me_stream and tgt > 0:
                if deps.get(key, 0) < tgt:
                    deps[key] = tgt

        def need(kind, ident, cnt):
            if kind == "c":
                key = ("c", ident)
                if key == me_stream:
                    return
                tgt = cnt
            else:
                ds = ident
                if ds in self.dflush and cnt <= self.dflush[ds][0]:
                    _, feng, fcnt = self.dflush[ds]
                    need("c", feng, fcnt)
                    return
                if ds in self.drelax:
                    cnt = self.dcnt[ds]
                    self.dclosed[ds] = True
                key = ("d", ds)
                if key == me_stream:
                    return
                tgt = cnt * 16
            if deps.get(key, 0) < tgt:
                deps[key] = tgt

        def need_entry(ent):
            kind, ident, cnt = ent
            need(kind, ident, cnt)

        for r in reads:
            w = self.last_w.get(r)
            if w is not None:
                need_entry(w)
        for w_ in writes:
            lw = self.last_w.get(w_)
            if lw is not None:
                need_entry(lw)
            for rd in self.readers.get(w_, ()):
                need_entry(rd)

        for key in list(deps):
            if key[0] == "d":
                deps[key] = 16 * self.dcnt[key[1]]
        if dma:
            self.dcnt[dsem] = self.dcnt.get(dsem, 0) + 1
            me = ("d", dsem, self.dcnt[dsem])
        else:
            self.ccnt[eng] += 1
            me = ("c", eng, self.ccnt[eng])
        self.ops[eng].append((fn, dma, dsem, deps))
        for r in reads:
            self.readers.setdefault(r, []).append(me)
        for w_ in writes:
            self.last_w[w_] = me
            self.readers[w_] = []

    def fence(self):
        by_eng = {}
        for ds, engs in self.down.items():
            n = self.dcnt.get(ds, 0)
            if n > 0 and self.dflush.get(ds, (0,))[0] < n:
                for eng in engs:
                    by_eng.setdefault(eng, []).append((ds, n))
        for eng, lst in by_eng.items():
            def flush(e, lst=tuple(lst)):
                for ds, n in lst:
                    e.wait_ge(self._sems[("d", ds)], 16 * n)
                return e.nop()
            self.op(eng, flush)
            for ds, n in lst:
                self.dflush[ds] = (n, eng, self.ccnt[eng])
        self.floor = {("c", e): n for e, n in self.ccnt.items() if n > 0}

    def emit(self, nc, block, ctx):
        sems = {}
        for e in ENGS:
            sems[("c", e)] = ctx.enter_context(nc.semaphore(f"sc_{e}"))
        for ds in self.down:
            sems[("d", ds)] = ctx.enter_context(nc.semaphore(f"sd_{ds}"))
        self._sems = sems

        def body_for(e):
            ops = self.ops[e]

            def run(eng):
                hw = {}
                for fn, dma, dsem, deps in ops:
                    for key, tgt in deps.items():
                        if hw.get(key, 0) >= tgt:
                            continue
                        eng.wait_ge(sems[key], tgt)
                        hw[key] = tgt
                    instr = fn(eng)
                    if dma:
                        instr.then_inc(sems[("d", dsem)], 16)
                    else:
                        instr.then_inc(sems[("c", e)], 1)
            return run

        for e in ENGS:
            getattr(block, _BODY[e])(body_for(e))
        return sems


def build_nc(n_sub=3):
    nc = bass.Bass("TRN2", target_bir_lowering=False)

    x_in = nc.dram_tensor("x_in", [D, T], F32, kind="ExternalInput").ap()
    y_out = nc.dram_tensor("y_out", [D, T], F32, kind="ExternalOutput").ap()

    Wq = nc.dram_tensor("Wq", [D, D], F32, kind="ExternalInput").ap()
    Wk = nc.dram_tensor("Wk", [D, 1024], F32, kind="ExternalInput").ap()
    Wv = nc.dram_tensor("Wv", [D, 1024], F32, kind="ExternalInput").ap()
    Wg2 = nc.dram_tensor("Wg2", [16, 1024], F32, kind="ExternalInput").ap()
    Wo = nc.dram_tensor("Wo", [D, D], F32, kind="ExternalInput").ap()
    W1S = nc.dram_tensor("W1S", [GN, 128, D], F32, kind="ExternalInput").ap()
    W2T = nc.dram_tensor("W2T", [HF, D], F32, kind="ExternalInput").ap()
    W3S = nc.dram_tensor("W3S", [GN, 128, D], F32, kind="ExternalInput").ap()

    wg1_t = nc.dram_tensor("wg1_t", [128, 256], F32, kind="ExternalInput").ap()
    bq_t = nc.dram_tensor("bq_t", [128, 16], F32, kind="ExternalInput").ap()
    bk_t = nc.dram_tensor("bk_t", [128, 8], F32, kind="ExternalInput").ap()
    bv_t = nc.dram_tensor("bv_t", [128, 8], F32, kind="ExternalInput").ap()
    bg2_t = nc.dram_tensor("bg2_t", [128, 8], F32, kind="ExternalInput").ap()
    gnw_t = nc.dram_tensor("gnw_t", [1, 128], F32, kind="ExternalInput").ap()

    ident_c = nc.dram_tensor("ident_c", [128, 128], BF16, kind="ExternalInput").ap()
    h128_c = nc.dram_tensor("h128_c", [128, 128], BF16, kind="ExternalInput").ap()
    h64x2_c = nc.dram_tensor("h64x2_c", [128, 128], BF16, kind="ExternalInput").ap()
    mask_c = nc.dram_tensor("mask_c", [128, 128], BF16, kind="ExternalInput").ap()
    ones_col_c = nc.dram_tensor("ones_col_c", [128, 1], BF16, kind="ExternalInput").ap()
    ones_row_c = nc.dram_tensor("ones_row_c", [1, 128], BF16, kind="ExternalInput").ap()
    meta_c = nc.dram_tensor("meta_c", [1, 8], I32, kind="ExternalInput").ap()

    W1T_s = nc.dram_tensor("W1T_s", [GN, 128, D], BF16, kind="Internal").ap()
    W3T_s = nc.dram_tensor("W3T_s", [GN, 128, D], BF16, kind="Internal").ap()
    Z_s = nc.dram_tensor("Z_s", [HF, D], BF16, kind="Internal").ap()
    W2pp_s = nc.dram_tensor("W2pp_s", [4, GN, 128, 512], BF16, kind="Internal").ap()
    r1_s = nc.dram_tensor("r1_s", [D, T], F32, kind="Internal").ap()
    r2_s = nc.dram_tensor("r2_s", [D, T], F32, kind="Internal").ap()
    xb_a = nc.dram_tensor("xb_a", [D, T], BF16, kind="Internal").ap()
    xb_b = nc.dram_tensor("xb_b", [D, T], BF16, kind="Internal").ap()
    q_sp = nc.dram_tensor("q_sp", [128, 16384], BF16, kind="Internal").ap()
    o_sp = nc.dram_tensor("o_sp", [128, 16384], BF16, kind="Internal").ap()

    if DEBUG_ATTN:
        dbg_k = nc.dram_tensor("dbg_k", [128, 8192], BF16, kind="ExternalOutput").ap()
        dbg_ku = nc.dram_tensor("dbg_ku", [128, 8192], BF16, kind="ExternalOutput").ap()
        dbg_v = nc.dram_tensor("dbg_v", [128, 8192], BF16, kind="ExternalOutput").ap()
        dbg_fq = nc.dram_tensor("dbg_fq", [128, 8192], BF16, kind="ExternalOutput").ap()
        dbg_qp = nc.dram_tensor("dbg_qp", [128, 16384], BF16, kind="ExternalOutput").ap()
        dbg_o = nc.dram_tensor("dbg_o", [128, 16384], BF16, kind="ExternalOutput").ap()
        dbg_xn = nc.dram_tensor("dbg_xn", [128, 16384], BF16, kind="ExternalOutput").ap()
    sh_state = nc.dram_tensor("sh_state", [NKV * 128, 128], F32, kind="Internal",
                              addr_space="Shared").ap()
    sh_canary = nc.dram_tensor("sh_canary", [1, 16], I32, kind="Internal",
                               addr_space="Shared").ap()

    with contextlib.ExitStack() as ctx:
        sb = lambda name, shape, dt: ctx.enter_context(nc.sbuf_tensor(name, shape, dt))
        psb = lambda name, shape, dt: ctx.enter_context(nc.psum_tensor(name, shape, dt))

        pb = [psb(f"pb{i}", [128, 512], F32) for i in range(6)]
        pbh = [psb(f"pbh{i}", [128, 1024], BF16) for i in range(2)]

        A = sb("A", [128, 32768], BF16)
        B = sb("B", [128, 16384], BF16)
        C = sb("C", [128, 16384], BF16)
        Dn = sb("Dn", [128, 16384], BF16)
        E = sb("E", [128, 4096], F32)

        ident = sb("ident", [128, 128], BF16)
        h128 = sb("h128", [128, 128], BF16)
        h64x2 = sb("h64x2", [128, 128], BF16)
        mask = sb("mask", [128, 128], BF16)
        gnw_bc = sb("gnw_bc", [128, 128], BF16)
        ones_col = sb("ones_col", [128, 1], BF16)
        ones_row = sb("ones_row", [1, 128], BF16)
        gnw_row = sb("gnw_row", [1, 128], F32)
        gnw_rowb = sb("gnw_rowb", [1, 128], BF16)
        meta = sb("meta", [1, 8], I32)
        can_sb = sb("can_sb", [1, 16], I32)
        bq_sb = sb("bq_sb", [128, 16], F32)
        bk_sb = sb("bk_sb", [128, 8], F32)
        bv_sb = sb("bv_sb", [128, 8], F32)
        bg2_sb = sb("bg2_sb", [128, 8], F32)
        wg1b = sb("wg1b", [128, 256], BF16)
        wg2b = sb("wg2b", [16, 1024], BF16)
        zeros128 = sb("zeros128", [128, 128], F32)
        eps_col = sb("eps_col", [128, 1], F32)
        mrbc = sb("mrbc", [128, 4096], BF16)
        rowf_t = sb("rowf_t", [128, 1024], F32)
        rowb = sb("rowb", [1, 1024], BF16)
        S = [sb(f"S{i}", [128, 128], F32) for i in range(NKV)]
        Sbuf = [sb(f"Sb{i}", [128, 128], BF16) for i in range(NKV)]
        elb = sb("elb", [128, 64], F32)
        atb = [sb(f"atb{i}", [128, 128], BF16) for i in range(4)]
        vt_sb = [sb(f"vt{i}", [128, 128], BF16) for i in range(2)]
        ke_sb = [sb(f"ke{i}", [128, 128], BF16) for i in range(2)]
        onb = [sb(f"onb{i}", [128, 128], BF16) for i in range(4)]
        otmp = [sb(f"otmp{i}", [128, 128], BF16) for i in range(4)]
        rms_sq = sb("rms_sq", [128, 128], BF16)
        rms_acc = [sb(f"rmsa{i}", [128, 2], F32) for i in range(2)]
        rms_rstd = [sb(f"rmsr{i}", [128, 8], F32) for i in range(2)]
        dex = sb("dex", [128, 128], F32)
        dacc = sb("dacc", [128, 8], F32)
        lnm2 = sb("lnm2", [1, 512], BF16)

        sgt = [E[:, 0:512], E[:, 512:1024]]
        rcur = [E[:, 1024:1536], E[:, 1536:2048]]
        rnew = [E[:, 2048:2560], E[:, 2560:3072]]
        w3f = [E[:, 1024:2048], E[:, 2048:3072]]
        zf = E[:, 0:512]
        zmin = E[:, 512:1024]
        zab = E[:, 3072:3584]
        zex = E[:, 0:512]
        zln = E[:, 3072:3584]
        wg1f = E[:, 3584:3840]
        wg2f = E[0:16, 1024:2048]
        gk_v = E[:, 1024:2048]
        Bcum = E[:, 2048:3072]
        oc_t = [E[:, 0:256], E[:, 256:512]]

        w1sl = [C[:, 0:2048], C[:, 2048:4096]]
        w3sl = [C[:, 4096:6144], C[:, 6144:8192]]
        w2bl4 = [C[:, 8192:10240], C[:, 10240:12288]]
        usb_v = [Dn[:, 0:2048], Dn[:, 2048:4096]]
        zsl = [Dn[:, 4096:6144], Dn[:, 6144:8192]]
        wld = [Dn[:, 8192 + i * 2048:8192 + (i + 1) * 2048] for i in range(4)]
        gsl1 = [Dn[:, 0:2048], Dn[:, 2048:4096]]
        gsl3 = [Dn[:, 4096:6144], Dn[:, 6144:8192]]
        xnh = C
        kfm = A[:, 0:8192]
        kefm = A[:, 8192:16384]
        vfm = A[:, 16384:24576]
        fq = A[:, 24576:32768]
        xnq = B[:, 0:8192]
        awb = B[:, 8192:12288]
        xbq_t = [B[:, 12288:12800], B[:, 12800:13312]]
        xbn_t = [B[:, 13312:13824], B[:, 13824:14336]]
        xbn2_t = [B[:, 14336:14848], B[:, 14848:15360]]

        def qp_ap(qh):
            if qh < 8:
                return B[:, qh * 1024:(qh + 1) * 1024]
            if qh < 15:
                return A[:, 24576 + (qh - 8) * 1024:24576 + (qh - 7) * 1024]
            return B[:, 15360:16384]
        ofm = Dn
        aws = [awb[:, 0:2048], awb[:, 2048:4096]]
        awbv = Dn[:, 3072:7168]
        ebm_t = Dn[:, 0:1024]
        z1b = Dn[0:16, 1024:2048]
        qbt = [Dn[:, 2048:2560], Dn[:, 2560:3072]]

        s_hp = ctx.enter_context(nc.semaphore("s_hp"))
        s_hs = ctx.enter_context(nc.semaphore("s_hs"))

        sch = Sched()
        block = ctx.enter_context(nc.Block())

        def ld(dst, src, wkeys, eng="sp", rkeys=(), dsem=None):
            sch.op(eng, lambda e, d=dst, s=src: e.dma_start(out=d, in_=s),
                   reads=rkeys, writes=wkeys, dma=True,
                   dsem=dsem if dsem is not None else wkeys[0])

        sch.drelax.add("consts")
        ld(ident[:], ident_c, ["ident"], dsem="consts")
        ld(h128[:], h128_c, ["h128"], dsem="consts")
        ld(h64x2[:], h64x2_c, ["h64x2"], dsem="consts")
        ld(mask[:], mask_c, ["mask"], dsem="consts")
        ld(ones_col[:], ones_col_c, ["ones_col"], dsem="consts")
        ld(ones_row[:], ones_row_c, ["ones_row"], dsem="consts")
        ld(meta[:], meta_c, ["meta"], dsem="consts")
        ld(bq_sb[:], bq_t, ["bq"], dsem="consts")
        ld(bk_sb[:], bk_t, ["bk"], dsem="consts")
        ld(bv_sb[:], bv_t, ["bv"], dsem="consts")
        ld(bg2_sb[:], bg2_t, ["bg2"], dsem="consts")
        ld(gnw_row[:], gnw_t, ["gnw_row"], dsem="consts")

        def memsets(e):
            e.memset(zeros128[:], 0.0)
            e.memset(eps_col[:], EPS)
            e.memset(dacc[:], 1.0)
            for i in range(NKV):
                e.memset(S[i][:], 0.0)
                ins = e.memset(Sbuf[i][:], 0.0)
            return ins
        sch.op("pool", memsets,
               writes=["zeros128", "epsc", "dacc"]
                      + [f"S{i}" for i in range(NKV)]
                      + [f"Sb{i}" for i in range(NKV)])

        sch.op("dve", lambda e: e.tensor_scalar_mul(bq_sb[:], bq_sb[:], QSCALE),
               reads=["bq"], writes=["bq"])
        sch.op("dve", lambda e: e.tensor_copy(gnw_rowb[:], gnw_row[:]),
               reads=["gnw_row"], writes=["gnw_rowb"])
        sch.op("pe", lambda e: e.matmul(pb[0][:, 0:128], ones_row[:],
                                        gnw_rowb[:], start=True, stop=True),
               reads=["ones_row", "gnw_rowb"], writes=["pb0"])
        sch.op("act", lambda e: e.activation(gnw_bc[:], pb[0][:, 0:128], AF.Copy),
               reads=["pb0"], writes=["gnw_bc"])

        def ln_finalize(q, bx, bx2):
            def v1(e):
                e.tensor_scalar(rowf_t[0:1, 0:512], pb[bx][0:1, 0:512],
                                1.0 / 2048.0, None, ALU.mult)
                e.tensor_scalar(rowf_t[0:1, 512:1024], pb[bx2][0:1, 0:512],
                                1.0 / 2048.0, None, ALU.mult)
                return e.tensor_copy(rowb[0:1, 0:512], rowf_t[0:1, 0:512])
            sch.op("dve", v1, reads=[f"pb{bx}", f"pb{bx2}"],
                   writes=["rowf", "rowbm"])
            sch.op("act", lambda e: e.activation(lnm2[0:1, 0:512],
                                                 rowf_t[0:1, 0:512], AF.Square),
                   reads=["rowf"], writes=["rowfm2", "lnm2"])
            sch.op("dve", lambda e: e.tensor_sub(rowf_t[0:1, 0:512],
                                                 rowf_t[0:1, 512:1024],
                                                 lnm2[0:1, 0:512]),
                   reads=["rowf", "rowfm2"], writes=["rowf2"])
            sch.op("act", lambda e: e.activation(rowf_t[0:1, 512:1024],
                                                 rowf_t[0:1, 0:512],
                                                 AF.Sqrt,
                                                 bias=eps_col[0:1, 0:1]),
                   reads=["rowf2", "epsc"], writes=["rowf3"])
            def v3(e):
                e.reciprocal(rowf_t[0:1, 0:512], rowf_t[0:1, 512:1024])
                return e.tensor_copy(rowb[0:1, 512:1024], rowf_t[0:1, 0:512])
            sch.op("dve", v3, reads=["rowf3", "rowbm"], writes=["rowb"])
            def bcast(e):
                e.matmul(pb[0][:, 0:512], ones_row[:], rowb[0:1, 0:512],
                         start=True, stop=True)
                return e.matmul(pb[1][:, 0:512], ones_row[:], rowb[0:1, 512:1024],
                                start=True, stop=True)
            sch.op("pe", bcast, reads=["rowb", "ones_row"], writes=["pb0", "pb1"])
            def evb(e):
                e.activation(mrbc[:, q * 1024:q * 1024 + 512], pb[0][:, 0:512],
                             AF.Copy)
                return e.activation(mrbc[:, q * 1024 + 512:(q + 1) * 1024],
                                    pb[1][:, 0:512], AF.Copy)
            sch.op("act", evb, reads=["pb0", "pb1"], writes=[f"mrbc{q}"])

        def epilogue_tile(bk_i, dch, q, j, r_src, rs_key, r_dst, rd_tag,
                          xb_dst, xb_tag, stats):
            p2 = j % 2
            rk = [f"rcur{p2}"] + ([f"{rs_key}_{dch}_{q}"] if rs_key else [])
            ld(rcur[p2], r_src[dch * 128:(dch + 1) * 128, q * TQ:(q + 1) * TQ],
               [f"rcur{p2}"], rkeys=rk)
            sch.op("dve", lambda e, b=bk_i, p=p2: e.tensor_add(
                rnew[p], pb[b][:, 0:512], rcur[p]),
                   reads=[f"pb{bk_i}", f"rcur{p2}"], writes=[f"rnew{p2}"])
            sch.op("act", lambda e, p=p2, d=dch, qq=q, rd=r_dst: e.dma_start(
                out=rd[d * 128:(d + 1) * 128, qq * TQ:(qq + 1) * TQ],
                in_=rnew[p]),
                   reads=[f"rnew{p2}"], dma=True, dsem=f"strnew{p2}",
                   writes=[f"{rd_tag}_{dch}_{q}"])
            if xb_dst is not None:
                sch.op("dve", lambda e, p=p2: e.tensor_copy(xbn_t[p], rnew[p]),
                       reads=[f"rnew{p2}"], writes=[f"xbn{p2}"])
                sch.op("act", lambda e, p=p2, d=dch, qq=q, xd=xb_dst: e.dma_start(
                    out=xd[d * 128:(d + 1) * 128, qq * TQ:(qq + 1) * TQ],
                    in_=xbn_t[p]),
                       reads=[f"xbn{p2}"], dma=True, dsem=f"stxbn{p2}",
                       writes=[f"{xb_tag}_{dch}_{q}"])
            if stats is not None:
                sx, sx2, first, last = stats
                sch.op("act", lambda e, p=p2: e.activation(xbn2_t[p], xbn_t[p],
                                                           AF.Square),
                       reads=[f"xbn{p2}"], writes=[f"xbn2{p2}"])
                def smm(e, p=p2, f=first, l=last):
                    e.matmul(pb[sx][0:1, 0:512], ones_col[:], xbn_t[p][:],
                             start=f, stop=l)
                    return e.matmul(pb[sx2][0:1, 0:512], ones_col[:],
                                    xbn2_t[p][:], start=f, stop=l)
                sch.op("pe", smm, reads=[f"xbn{p2}", f"xbn2{p2}", "ones_col"],
                       writes=[f"pb{sx}", f"pb{sx2}"])


        Z_r = Z_s.rearrange("(a b) d -> b a d", b=128)
        W2T_r = W2T.rearrange("(a b) d -> b a d", b=128)

        def p4_step(b0):
            p = b0 % 2
            sch.op("pool", lambda e, bb=b0, pp=p: e.dma_start(
                out=usb_v[pp][0:64, :], in_=W2T_r[bb]),
                   writes=[f"usb{p}a"], dma=True,
                   dsem=f"usb{p}")
            sch.op("pool", lambda e, bb=b0, pp=p: e.dma_start(
                out=usb_v[pp][64:128, :], in_=W2T_r[bb + 64]),
                   writes=[f"usb{p}b"], dma=True, dsem=f"usb{p}")
            for dsl in range(4):
                bkx = 4 + dsl % 2
                sch.op("pe", lambda e, pp=p, dd=dsl, b2=bkx: e.matmul(
                    pb[b2][:, 0:512], h64x2[:],
                    usb_v[pp][:, dd * 512:(dd + 1) * 512], start=True, stop=True),
                       reads=[f"usb{p}a", f"usb{p}b", "h64x2"],
                       writes=[f"pb{bkx}"])
                sch.op("act", lambda e, pp=p, dd=dsl, b2=bkx: e.activation(
                    zsl[pp][:, dd * 512:(dd + 1) * 512], pb[b2][:, 0:512],
                    AF.Copy),
                       reads=[f"pb{bkx}"], writes=[f"zsl{p}_{dsl}"])
            zreads = [f"zsl{p}_{d}" for d in range(4)]
            sch.op("sp", lambda e, bb=b0, pp=p: e.dma_start(
                out=Z_r[bb], in_=zsl[pp][0:64, :]),
                   reads=zreads, dma=True, dsem=f"stz{p}", writes=[f"z_{b0}a"])
            sch.op("sp", lambda e, bb=b0, pp=p: e.dma_start(
                out=Z_r[bb + 64], in_=zsl[pp][64:128, :]),
                   reads=zreads, dma=True, dsem=f"stz{p}", writes=[f"z_{b0}b"])

        z_all = [f"z_{b}a" for b in range(64)] + [f"z_{b}b" for b in range(64)]

        def p5_step(g):
            p = g % 2
            sch.op("sp", lambda e, gg=g, pp=p: e.dma_start(
                out=usb_v[pp], in_=Z_s[gg * 128:(gg + 1) * 128, :]),
                   reads=z_all,
                   writes=[f"usb{p}a", f"usb{p}b"], dma=True, dsem=f"usb{p}")
            for dsl in range(4):
                bkx = 4 + dsl % 2
                sch.op("pe", lambda e, pp=p, dd=dsl, b2=bkx: e.matmul(
                    pb[b2][:, 0:512], h128[:],
                    usb_v[pp][:, dd * 512:(dd + 1) * 512], start=True, stop=True),
                       reads=[f"usb{p}a", "h128"], writes=[f"pb{bkx}"])
                sch.op("act", lambda e, pp=p, dd=dsl, b2=bkx: e.activation(
                    zsl[pp][:, dd * 512:(dd + 1) * 512], pb[b2][:, 0:512],
                    AF.Copy, scale=FWHT_SCALE),
                       reads=[f"pb{bkx}"], writes=[f"zsl{p}_{dsl}"])
                sch.op("act", lambda e, pp=p, dd=dsl, gg=g: e.dma_start(
                    out=W2pp_s[dd, gg], in_=zsl[pp][:, dd * 512:(dd + 1) * 512]),
                       reads=[f"zsl{p}_{dsl}"], dma=True, dsem=f"stzsl{p}",
                       writes=[f"w2pp_{dsl}_{g}"])

        ring = [rcur[0], rcur[1], rnew[0], rnew[1]]
        ringk = ["rcur0", "rcur1", "rnew0", "rnew1"]
        for q in range(NTQ):
            for dch in range(DCH):
                i4 = (q * DCH + dch) % 4
                p = (q * DCH + dch) % 2
                ld(ring[i4], x_in[dch * 128:(dch + 1) * 128,
                                  q * TQ:(q + 1) * TQ],
                   [ringk[i4]], dsem=ringk[i4])
                sch.op("dve", lambda e, pp=p, r4=i4: e.tensor_copy(
                    xbn_t[pp], ring[r4]),
                       reads=[ringk[i4]], writes=[f"xbn{p}"])
                sch.op("act", lambda e, pp=p, d=dch, qq=q: e.dma_start(
                    out=xb_a[d * 128:(d + 1) * 128, qq * TQ:(qq + 1) * TQ],
                    in_=xbn_t[pp]),
                       reads=[f"xbn{p}"], dma=True, dsem=f"stxbn{p}",
                       writes=[f"xbA_{dch}_{q}"])
                sch.op("act", lambda e, pp=p: e.activation(xbn2_t[pp], xbn_t[pp],
                                                           AF.Square),
                       reads=[f"xbn{p}"], writes=[f"xbn2{p}"])
                def smm(e, pp=p, f=(dch == 0), l=(dch == DCH - 1)):
                    e.matmul(pb[4][0:1, 0:512], ones_col[:], xbn_t[pp][:],
                             start=f, stop=l)
                    return e.matmul(pb[5][0:1, 0:512], ones_col[:],
                                    xbn2_t[pp][:], start=f, stop=l)
                sch.op("pe", smm, reads=[f"xbn{p}", f"xbn2{p}", "ones_col"],
                       writes=["pb4", "pb5"])
            ln_finalize(q, 4, 5)

        def ffn(idx, r_dst_override=None):
            r_src = x_in if idx == 0 else r2_s
            rs_key = None if idx == 0 else "r2"
            final = (idx == 1)
            r_dst = r_dst_override if r_dst_override is not None else \
                (r1_s if idx == 0 else y_out)
            rd_tag = "y" if (final or r_dst_override is not None) else "r1"
            xb_dst = None if final else xb_b
            prep = (idx == 0)
            for q in range(NTQ):
                for dch in range(DCH):
                    p = dch % 2
                    xk = [f"xbq{p}"]
                    if prep and q == 0 and dch < 2:
                        pass
                    ld(xbq_t[p], xb_a[dch * 128:(dch + 1) * 128,
                                      q * TQ:(q + 1) * TQ],
                       xk, rkeys=[f"xbA_{dch}_{q}"])
                    def xn(e, pp=p, d=dch, qq=q):
                        e.tensor_tensor(xbn_t[pp][:], xbq_t[pp][:],
                                        mrbc[:, qq * 1024:qq * 1024 + 512],
                                        ALU.subtract)
                        return e.tensor_tensor(
                            xnq[:, d * 512:(d + 1) * 512], xbn_t[pp][:],
                            mrbc[:, qq * 1024 + 512:(qq + 1) * 1024], ALU.mult)
                    sch.op("dve", xn, reads=[f"xbq{p}", f"mrbc{q}"],
                           writes=["xnq", f"xbn{p}"])
                for g in range(GN):
                    p = g % 2
                    if prep and q == 0:
                        if g < 32:
                            for j in range(2):
                                p4_step(2 * g + j)
                        else:
                            for j in range(2):
                                p5_step(2 * (g - 32) + j)
                        sch.op("pool", lambda e, gg=g, pp=p: e.dma_start(
                            out=w1sl[pp], in_=W1S[gg]),
                               dma=True, dsem=f"w1sl{p}",
                               writes=[f"w1sl{p}"])
                        sch.op("act", lambda e, gg=g, pp=p: e.dma_start(
                            out=W1T_s[gg], in_=w1sl[pp]),
                               reads=[f"w1sl{p}"], dma=True, dsem="stgsl0",
                               writes=[f"w1t_{g}"])
                        for hf in range(2):
                            sch.op("sp", lambda e, gg=g, hh=hf: e.dma_start(
                                out=w3f[hh],
                                in_=W3S[gg, :, hh * 1024:(hh + 1) * 1024]),
                                   dma=True, dsem=f"w3f{hf}",
                                   writes=([f"w3f{hf}", "rcur0",
                                            "rcur1"] if hf == 0 else
                                           [f"w3f{hf}", "rnew0",
                                            "rnew1"]))
                        def cst(e, pp=p):
                            e.tensor_copy(w3sl[pp][:, 0:1024], w3f[0])
                            return e.tensor_copy(w3sl[pp][:, 1024:2048],
                                                 w3f[1])
                        sch.op("dve", cst,
                               reads=["w3f0", "w3f1", "rcur0",
                                      "rcur1", "rnew0", "rnew1"],
                               writes=[f"w3sl{p}"])
                        sch.op("act", lambda e, gg=g, pp=p: e.dma_start(
                            out=W3T_s[gg], in_=w3sl[pp]),
                               reads=[f"w3sl{p}"], dma=True, dsem="stgsl1",
                               writes=[f"w3t_{g}"])
                        lhs1, lhs3 = w1sl[p], w3sl[p]
                        mmreads = [f"w1sl{p}", f"w3sl{p}", "xnq"]
                        mmwrites = [f"pb{2 * p}", f"pb{2 * p + 1}"]
                    else:
                        ld(w1sl[p], W1T_s[g], [f"w1sl{p}"],
                           rkeys=[f"w1t_{g}"])
                        ld(w3sl[p], W3T_s[g], [f"w3sl{p}"],
                           rkeys=[f"w3t_{g}"])
                        lhs1, lhs3 = w1sl[p], w3sl[p]
                        mmreads = [f"w1sl{p}", f"w3sl{p}", "xnq"]
                        mmwrites = [f"pb{2 * p}", f"pb{2 * p + 1}"]
                    def mm1(e, l1=lhs1, l3=lhs3, pp=p):
                        for d in range(DCH):
                            e.matmul(pb[2 * pp][:, 0:512],
                                     l1[:, d * 128:(d + 1) * 128],
                                     xnq[:, d * 512:(d + 1) * 512],
                                     start=(d == 0), stop=(d == DCH - 1))
                        for d in range(DCH):
                            ins = e.matmul(pb[2 * pp + 1][:, 0:512],
                                           l3[:, d * 128:(d + 1) * 128],
                                           xnq[:, d * 512:(d + 1) * 512],
                                           start=(d == 0), stop=(d == DCH - 1))
                        return ins
                    sch.op("pe", mm1, reads=mmreads, writes=mmwrites)
                    sch.op("act", lambda e, pp=p: e.activation(
                        sgt[pp], pb[2 * pp][:, 0:512], AF.Silu),
                           reads=[f"pb{2 * p}"], writes=[f"sgt{p}"])
                    sch.op("dve", lambda e, pp=p, gg=g: e.tensor_tensor(
                        A[:, gg * 512:(gg + 1) * 512], sgt[pp],
                        pb[2 * pp + 1][:, 0:512], ALU.mult),
                           reads=[f"sgt{p}", f"pb{2 * p + 1}"],
                           writes=[f"hid{g}"])
                for quad in range(4):
                    for g4 in range(GN // 4):
                        p = (quad * (GN // 4) + g4) % 2
                        src = W2pp_s[quad, 4 * g4:4 * g4 + 4].rearrange(
                            "g p d -> p g d")
                        sch.op("sp", lambda e, s2=src, pp=p: e.dma_start(
                            out=w2bl4[pp].rearrange("p (g d) -> p g d", d=512),
                            in_=s2),
                               reads=[f"w2pp_{quad}_{4 * g4 + j}"
                                      for j in range(4)],
                               dma=True, dsem=f"w2bl{p}",
                               writes=[f"w2bl{p}"])
                        def mm2(e, pp=p, g0=4 * g4):
                            for j in range(4):
                                for dl in range(4):
                                    ins = e.matmul(
                                        pb[dl][:, 0:512],
                                        w2bl4[pp][:, j * 512 + dl * 128:
                                                  j * 512 + (dl + 1) * 128],
                                        A[:, (g0 + j) * 512:(g0 + j + 1) * 512],
                                        start=(g0 + j == 0),
                                        stop=(g0 + j == GN - 1))
                            return ins
                        sch.op("pe", mm2,
                               reads=[f"w2bl{p}"] +
                                     [f"hid{4 * g4 + j}" for j in range(4)],
                               writes=["pb0", "pb1", "pb2", "pb3"])
                    for dl in range(4):
                        dch = quad * 4 + dl
                        st = None if final else \
                            (4, 5, dch == 0, dch == DCH - 1)
                        epilogue_tile(dl, dch, q, quad * 4 + dl, r_src, rs_key,
                                      r_dst, rd_tag, xb_dst, "xbB", st)
                if not final:
                    ln_finalize(q, 4, 5)
            sch.fence()

        ffn(0, r_dst_override=(y_out if n_sub == 1 else None))

        def hs_read(e):
            with (e.register("rnonce") as rnonce, e.register("rpar") as rpar,
                  e.register("rn") as rn, e.register("rw") as rw,
                  e.register("rc") as rc, e.register("riters") as riters,
                  e.register("rcond") as rcond, e.register("rtmp") as rtmp):
                e.reg_mov(rnonce, e.value_load(meta[0:1, 0:1]))
                e.reg_mov(rpar, e.value_load(meta[0:1, 1:2]))
                e.reg_alu(rn, rpar, 0, ALU.add)
                with e.While(rn):
                    e.reg_mov(rw, 0)
                    e.reg_mov(riters, POLL_MAX)
                    e.reg_mov(rcond, 1)
                    with e.While(rcond):
                        e.dma_start(out=can_sb[0:1, 0:2],
                                    in_=sh_canary[0:1, 0:2]).then_inc(s_hp, 16)
                        e.reg_add(rw, rw, 16)
                        e.wait_ge(s_hp, rw)
                        e.reg_mov(rc, e.value_load(can_sb[0:1, 0:1]))
                        e.reg_sub(riters, riters, 1)
                        e.reg_alu(rcond, rc, rnonce, ALU.not_equal)
                        e.reg_alu(rtmp, riters, 0, ALU.is_gt)
                        e.reg_alu(rcond, rcond, rtmp, ALU.bitwise_and)
                    for i in range(NKV):
                        e.dma_start(out=S[i][:],
                                    in_=sh_state[i * 128:(i + 1) * 128, :]
                                    ).then_inc(s_hp, 16)
                    e.reg_add(rw, rw, 16 * NKV)
                    e.wait_ge(s_hp, rw)
                    e.reg_alu(rcond, rc, rnonce, ALU.not_equal)
                    e.reg_mov(rc, e.value_load(can_sb[0:1, 1:2]))
                    e.reg_add(rc, rc, 1)
                    e.reg_mov(rtmp, e.value_load(meta[0:1, 2:3]))
                    e.reg_alu(rtmp, rc, rtmp, ALU.not_equal)
                    e.reg_alu(rcond, rcond, rtmp, ALU.bitwise_or)
                    with e.While(rcond):
                        for i in range(NKV):
                            e.memset(S[i][:], float("nan"))
                        e.reg_mov(rcond, 0)
                    e.reg_sub(rn, rn, 1)
                e.reg_mov(rn, 1)
                e.reg_sub(rn, rn, rpar)
                with e.While(rn):
                    for i in range(NKV):
                        e.memset(S[i][:], 0.0)
                    e.reg_sub(rn, rn, 1)
            return e.nop()

        def hs_write(e):
            with (e.register("rpar2") as rpar, e.register("rn2") as rn):
                e.reg_mov(rpar, e.value_load(meta[0:1, 1:2]))
                e.reg_mov(rn, 1)
                e.reg_sub(rn, rn, rpar)
                with e.While(rn):
                    for i in range(NKV):
                        e.dma_start(out=sh_state[i * 128:(i + 1) * 128, :],
                                    in_=S[i][:]).then_inc(s_hs, 16)
                    e.wait_ge(s_hs, 16 * NKV)
                    e.dma_start(out=sh_canary[0:1, 0:2],
                                in_=meta[0:1, 3:5]).then_inc(s_hs, 16)
                    e.wait_ge(s_hs, 16 * (NKV + 1))
                    e.reg_sub(rn, rn, 1)
            return e.nop()

        def attn(r_dst_override=None):
            WkR = Wk.rearrange("(c p) f -> p c f", p=128)
            WvR = Wv.rearrange("(c p) f -> p c f", p=128)
            WqR = Wq.rearrange("(c p) f -> p c f", p=128)
            WoR = Wo.rearrange("(c p) f -> p c f", p=128)
            aws3 = [a.rearrange("p (c f) -> p c f", f=128) for a in aws]
            r_dst = r_dst_override if r_dst_override is not None else r2_s
            rd_tag = "y" if r_dst_override is not None else "r2"

            ld(wg1f, wg1_t, ["wg1f"])
            ld(wg2f[:], Wg2, ["wg2f"])
            sch.op("dve", lambda e: e.tensor_copy(wg1b[:], wg1f),
                   reads=["wg1f"], writes=["wg1b"])
            sch.op("dve", lambda e: e.tensor_copy(wg2b[:], wg2f[:]),
                   reads=["wg2f"], writes=["wg2b"])

            for h in range(2):
                for dch in range(DCH):
                    for ts in range(2):
                        i = dch * 2 + ts
                        p = i % 2
                        q = 2 * h + ts
                        ld(xbq_t[p], xb_b[dch * 128:(dch + 1) * 128,
                                          q * TQ:(q + 1) * TQ],
                           [f"xbq{p}"], rkeys=[f"xbB_{dch}_{q}"])
                        def xn(e, pp=p, d=dch, tt=ts, qq=q):
                            e.tensor_tensor(xbn_t[pp][:], xbq_t[pp][:],
                                            mrbc[:, qq * 1024:qq * 1024 + 512],
                                            ALU.subtract)
                            return e.tensor_tensor(
                                xnh[:, d * 1024 + tt * 512:
                                    d * 1024 + (tt + 1) * 512],
                                xbn_t[pp][:],
                                mrbc[:, qq * 1024 + 512:(qq + 1) * 1024],
                                ALU.mult)
                        sch.op("dve", xn, reads=[f"xbq{p}", f"mrbc{q}"],
                               writes=["xnh", f"xbn{p}"])

                for ts in range(2):
                    def z1mm(e, tt=ts):
                        for d in range(DCH):
                            ins = e.matmul(pb[4 + tt][0:16, 0:512],
                                           wg1b[:, d * 16:(d + 1) * 16],
                                           xnh[:, d * 1024 + tt * 512:
                                               d * 1024 + (tt + 1) * 512],
                                           start=(d == 0), stop=(d == DCH - 1))
                        return ins
                    sch.op("pe", z1mm, reads=["wg1b", "xnh"],
                           writes=[f"pb{4 + ts}"])
                    sch.op("act", lambda e, tt=ts: e.activation(
                        z1b[0:16, tt * 512:(tt + 1) * 512],
                        pb[4 + tt][0:16, 0:512], AF.Copy),
                           reads=[f"pb{4 + ts}"],
                           writes=["z1b", "o_1_0", "o_1_1"])

                awb3 = awb.rearrange("p (c f) -> p c f", f=256)
                awv3b = awbv.rearrange("p (c f) -> p c f", f=256)
                for fch in range(NKV):
                    pp = fch % 2
                    if pp == 0:
                        sch.op("pool", lambda e, f2=fch // 2: e.dma_start(
                            out=awb3, in_=WkR[:, :, f2 * 256:(f2 + 1) * 256]),
                               dma=True, dsem="aw", writes=["aw0", "aw1"])
                    for ts in range(2):
                        def kmm(e, p2=pp, tt=ts):
                            for d in range(DCH):
                                ins = e.matmul(pb[tt][:, 0:512],
                                               awb[:, d * 256 + p2 * 128:
                                                   d * 256 + (p2 + 1) * 128],
                                               xnh[:, d * 1024 + tt * 512:
                                                   d * 1024 + (tt + 1) * 512],
                                               start=(d == 0),
                                               stop=(d == DCH - 1))
                            return ins
                        sch.op("pe", kmm, reads=[f"aw{pp}", "xnh"],
                               writes=[f"pb{ts}"])
                        sch.op("act", lambda e, f=fch, tt=ts: e.activation(
                            kfm[:, f * 1024 + tt * 512:f * 1024 + (tt + 1) * 512],
                            pb[tt][:, 0:512], AF.Relu, bias=bk_sb[:, f:f + 1]),
                               reads=[f"pb{ts}", "bk"], writes=[f"k_{fch}_{ts}"])
                    if pp == 0:
                        sch.op("pool", lambda e, f2=fch // 2: e.dma_start(
                            out=awv3b, in_=WvR[:, :, f2 * 256:(f2 + 1) * 256]),
                               dma=True, dsem="awv",
                               writes=["awv0", "awv1"] +
                                      [f"o_{f}_{t2}" for f in (3, 4, 5, 6)
                                       for t2 in range(2)])
                    for ts in range(2):
                        def vmm(e, p2=pp, tt=ts):
                            for d in range(DCH):
                                ins = e.matmul(pb[2 + tt][:, 0:512],
                                               awbv[:, d * 256 + p2 * 128:
                                                    d * 256 + (p2 + 1) * 128],
                                               xnh[:, d * 1024 + tt * 512:
                                                   d * 1024 + (tt + 1) * 512],
                                               start=(d == 0),
                                               stop=(d == DCH - 1))
                            return ins
                        sch.op("pe", vmm, reads=[f"awv{pp}", "xnh"],
                               writes=[f"pb{2 + ts}"])
                        sch.op("dve", lambda e, f=fch, tt=ts: e.tensor_scalar_add(
                            vfm[:, f * 1024 + tt * 512:f * 1024 + (tt + 1) * 512],
                            pb[2 + tt][:, 0:512], bv_sb[:, f:f + 1]),
                               reads=[f"pb{2 + ts}", "bv"],
                               writes=[f"v_{fch}_{ts}"])
                        sch.op("pe", lambda e, f=fch, tt=ts: e.matmul(
                            pb[4 + tt][:, 0:512],
                            wg2b[:, f * 128:(f + 1) * 128],
                            z1b[0:16, tt * 512:(tt + 1) * 512],
                            start=True, stop=True),
                               reads=["wg2b", "z1b"], writes=[f"pb{4 + ts}"])
                        sch.op("dve", lambda e, f=fch, tt=ts: e.tensor_scalar_add(
                            zf, pb[4 + tt][:, 0:512], bg2_sb[:, f:f + 1]),
                               reads=[f"pb{4 + ts}", "bg2"], writes=["zf"])
                        sch.op("dve", lambda e: e.tensor_scalar_min(
                            zmin, zf, 0.0), reads=["zf"], writes=["zmin"])
                        def zchain(e):
                            e.activation(zab, zf, AF.Abs)
                            e.activation(zex, zab, AF.Exp, scale=-1.0)
                            return e.activation(zln, zex, AF.Ln, bias=1.0)
                        sch.op("act", zchain, reads=["zf"],
                               writes=["zln", "zf"])
                        sch.op("dve", lambda e, tt=ts: e.tensor_sub(
                            gk_v[:, tt * 512:(tt + 1) * 512], zmin, zln),
                               reads=["zmin", "zln"],
                               writes=["gk", "rcur0", "rcur1", "wg2f"])
                    def scan(e):
                        for cc in range(NCHH):
                            ins = e.tensor_tensor_scan(
                                Bcum[:, cc * 128:(cc + 1) * 128],
                                gk_v[:, cc * 128:(cc + 1) * 128],
                                zeros128[:], 0.0, ALU.add, ALU.add)
                        return ins
                    sch.op("dve", scan, reads=["gk", "zeros128"],
                           writes=["Bcum", "rnew0", "rnew1"])
                    sch.op("act", lambda e, f=fch: e.activation(
                        fq[:, f * 1024:(f + 1) * 1024], Bcum, AF.Exp,
                        scale=GINV),
                           reads=["Bcum"], writes=[f"fq{fch}"])
                    sch.op("act", lambda e: e.activation(
                        ebm_t[:], Bcum, AF.Exp, scale=-GINV),
                           reads=["Bcum"], writes=["ebm", "o_0_0", "o_0_1"])
                    def elbf(e, f=fch):
                        for cc in range(NCHH):
                            ins = e.activation(
                                elb[:, f * 8 + cc:f * 8 + cc + 1],
                                Bcum[:, cc * 128 + 127:cc * 128 + 128],
                                AF.Exp, scale=GINV)
                        return ins
                    sch.op("act", elbf, reads=["Bcum"], writes=[f"elb{fch}"])
                    def kmul(e, f=fch):
                        return e.tensor_tensor(
                            kefm[:, f * 1024:(f + 1) * 1024],
                            kfm[:, f * 1024:(f + 1) * 1024], ebm_t[:],
                            ALU.mult)
                    sch.op("dve", kmul,
                           reads=[f"k_{fch}_0", f"k_{fch}_1", "ebm"],
                           writes=[f"ku_{fch}"])
                    def kendm(e, f=fch):
                        for cc in range(NCHH):
                            ins = e.tensor_scalar_mul(
                                kfm[:, f * 1024 + cc * 128:
                                    f * 1024 + (cc + 1) * 128],
                                kefm[:, f * 1024 + cc * 128:
                                     f * 1024 + (cc + 1) * 128],
                                elb[:, f * 8 + cc:f * 8 + cc + 1])
                        return ins
                    sch.op("dve", kendm, reads=[f"ku_{fch}", f"elb{fch}"],
                           writes=[f"ke_{fch}", f"k_{fch}_0", f"k_{fch}_1"])

                if DEBUG_ATTN and h == 0:
                    sch.drelax.add("dbg")
                    sch.drelax.add("dbg2")
                    sch.drelax.add("dbg3")
                    sch.op("act", lambda e: e.dma_start(out=dbg_k, in_=kfm),
                           reads=[f"ke_{f}" for f in range(NKV)], dma=True,
                           dsem="dbg", writes=["dbgk"])
                    sch.op("act", lambda e: e.dma_start(out=dbg_ku, in_=kefm),
                           reads=[f"ku_{f}" for f in range(NKV)], dma=True,
                           dsem="dbg", writes=["dbgku"])
                    sch.op("act", lambda e: e.dma_start(out=dbg_v, in_=vfm),
                           reads=[f"v_{f}_{t2}" for f in range(NKV)
                                  for t2 in range(2)], dma=True,
                           dsem="dbg", writes=["dbgv"])
                    sch.op("act", lambda e: e.dma_start(out=dbg_fq, in_=fq),
                           reads=[f"fq{f}" for f in range(NKV)], dma=True,
                           dsem="dbg", writes=["dbgfq"])
                    sch.op("act", lambda e: e.dma_start(out=dbg_xn, in_=xnh[:]),
                           reads=["xnh"], dma=True, dsem="dbg", writes=["dbgxn"])

                for qh in range(NQ):
                    pp = qh % 2
                    if pp == 0:
                        sch.op("pool", lambda e, q2=qh // 2: e.dma_start(
                            out=awb3, in_=WqR[:, :, q2 * 256:(q2 + 1) * 256]),
                               dma=True, dsem="aw", writes=["aw0", "aw1"])
                    for ts in range(2):
                        def qmm(e, p2=pp, tt=ts):
                            for d in range(DCH):
                                ins = e.matmul(pb[tt][:, 0:512],
                                               awb[:, d * 256 + p2 * 128:
                                                   d * 256 + (p2 + 1) * 128],
                                               xnh[:, d * 1024 + tt * 512:
                                                   d * 1024 + (tt + 1) * 512],
                                               start=(d == 0),
                                               stop=(d == DCH - 1))
                            return ins
                        sch.op("pe", qmm, reads=[f"aw{pp}", "xnh"],
                               writes=[f"pb{ts}"])
                        sch.op("act", lambda e, qq=qh, tt=ts: e.activation(
                            qbt[tt], pb[tt][:, 0:512], AF.Relu,
                            scale=QSCALE, bias=bq_sb[:, qq:qq + 1]),
                               reads=[f"pb{ts}", "bq"],
                               writes=[f"qbt{ts}", f"o_2_{ts}"])
                        qpw = [f"qp_{qh}_{ts}"]
                        if 8 <= qh < 15:
                            qpw.append(f"fq{qh - 8}")
                        sch.op("dve", lambda e, qq=qh, tt=ts: e.tensor_tensor(
                            qp_ap(qq)[:, tt * 512:(tt + 1) * 512],
                            qbt[tt],
                            fq[:, (qq // 2) * 1024 + tt * 512:
                               (qq // 2) * 1024 + (tt + 1) * 512], ALU.mult),
                               reads=[f"qbt{ts}", f"fq{qh // 2}"],
                               writes=qpw)

                if DEBUG_ATTN and h == 0:
                    for qh in range(NQ):
                        sch.op("act", lambda e, qq=qh: e.dma_start(
                            out=dbg_qp[:, qq * 1024:(qq + 1) * 1024],
                            in_=qp_ap(qq)),
                               reads=[f"qp_{qh}_0", f"qp_{qh}_1"], dma=True,
                               dsem="dbg2", writes=[f"dbgqp{qh}"])

                if h == 0:
                    for qh in range(NQ):
                        rk = [f"qp_{qh}_0", f"qp_{qh}_1"]
                        if 8 <= qh < 15:
                            rk.append(f"fq{qh - 8}")
                        sch.op("sp", lambda e, qq=qh: e.dma_start(
                            out=q_sp[:, qq * 1024:(qq + 1) * 1024],
                            in_=qp_ap(qq)),
                               reads=rk, dma=True, dsem="qsp",
                               writes=[f"qsp_{qh}"])

                for c in range(NCHH):
                    for kvh in range(NKV):
                        u = c * NKV + kvh
                        par = u % 2
                        ts_c = c // 4
                        qh0, qh1 = 2 * kvh, 2 * kvh + 1
                        ksl = kfm[:, kvh * 1024 + c * 128:
                                  kvh * 1024 + (c + 1) * 128]
                        kesl = kefm[:, kvh * 1024 + c * 128:
                                    kvh * 1024 + (c + 1) * 128]
                        vsl = vfm[:, kvh * 1024 + c * 128:
                                  kvh * 1024 + (c + 1) * 128]
                        q0sl = qp_ap(qh0)[:, c * 128:(c + 1) * 128]
                        q1sl = qp_ap(qh1)[:, c * 128:(c + 1) * 128]
                        kread = [f"ke_{kvh}", f"ku_{kvh}", f"v_{kvh}_{ts_c}",
                                 f"qp_{qh0}_{ts_c}", f"qp_{qh1}_{ts_c}"]

                        def t1(e, a=vsl, b2=ksl, pr=par):
                            e.transpose(pbh[pr][:, 0:128], a, ident[:])
                            return e.transpose(pbh[pr][:, 128:256], b2,
                                               ident[:])
                        sch.op("pe", t1, reads=kread + ["ident"],
                               writes=[f"pbh{par}T"])

                        def t2(e, ks=kesl, qa=q0sl, qb=q1sl, pr=par):
                            e.matmul(pb[pr][:, 0:128], ks, qa,
                                     start=True, stop=True)
                            return e.matmul(pb[pr][:, 128:256], ks, qb,
                                            start=True, stop=True)
                        sch.op("pe", t2, reads=kread, writes=[f"pb{par}"])

                        def v1(e, pr=par):
                            e.tensor_copy(vt_sb[pr][:], pbh[pr][:, 0:128])
                            e.tensor_copy(ke_sb[pr][:], pbh[pr][:, 128:256])
                            e.tensor_tensor(atb[2 * pr][:], pb[pr][:, 0:128],
                                            mask[:], ALU.mult)
                            return e.tensor_tensor(atb[2 * pr + 1][:],
                                                   pb[pr][:, 128:256],
                                                   mask[:], ALU.mult)
                        sch.op("dve", v1,
                               reads=[f"pbh{par}T", f"pb{par}", "mask"],
                               writes=[f"vt{par}", f"ke2{par}",
                                       f"atb{2 * par}", f"atb{2 * par + 1}"])

                        def omm(e, pr=par, qa=q0sl, qb=q1sl, kv=kvh):
                            e.matmul(pb[2 + pr][:, 0:128], atb[2 * pr][:],
                                     vt_sb[pr][:], start=True, stop=False)
                            e.matmul(pb[2 + pr][:, 0:128], qa, Sbuf[kv][:],
                                     start=False, stop=True)
                            e.matmul(pb[2 + pr][:, 128:256],
                                     atb[2 * pr + 1][:], vt_sb[pr][:],
                                     start=True, stop=False)
                            e.matmul(pb[2 + pr][:, 128:256], qb, Sbuf[kv][:],
                                     start=False, stop=True)
                            return e.matmul(pb[2 + pr][:, 256:384],
                                            ke_sb[pr][:], vt_sb[pr][:],
                                            start=True, stop=True)
                        sch.op("pe", omm,
                               reads=[f"atb{2 * par}", f"atb{2 * par + 1}",
                                      f"vt{par}", f"ke2{par}", f"Sb{kvh}"]
                                     + kread,
                               writes=[f"pb{2 + par}"])

                        def rms_a(e, pr=par):
                            e.activation(rms_sq[:], pb[2 + pr][:, 0:128],
                                         AF.Square,
                                         accum_out=rms_acc[pr][:, 0:1])
                            return e.activation(rms_sq[:],
                                                pb[2 + pr][:, 128:256],
                                                AF.Square,
                                                accum_out=rms_acc[pr][:, 1:2])
                        sch.op("act", rms_a, reads=[f"pb{2 + par}"],
                               writes=[f"rmsacc{par}"])
                        def rms_b(e, pr=par):
                            e.activation(rms_rstd[pr][:, 0:2],
                                         rms_acc[pr][:, 0:2], AF.Ln,
                                         scale=1.0 / 128.0,
                                         bias=eps_col[:, 0:1])
                            return e.activation(rms_rstd[pr][:, 2:4],
                                                rms_rstd[pr][:, 0:2], AF.Exp,
                                                scale=-0.5)
                        sch.op("act", rms_b, reads=[f"rmsacc{par}", "epsc"],
                               writes=[f"rmssq{par}"])
                        def gmul(e, pr=par):
                            e.scalar_tensor_tensor(
                                onb[2 * pr][:], pb[2 + pr][:, 0:128],
                                rms_rstd[pr][:, 2:3], gnw_bc[:],
                                ALU.mult, ALU.mult)
                            return e.scalar_tensor_tensor(
                                onb[2 * pr + 1][:], pb[2 + pr][:, 128:256],
                                rms_rstd[pr][:, 3:4], gnw_bc[:],
                                ALU.mult, ALU.mult)
                        sch.op("dve", gmul,
                               reads=[f"rmssq{par}", f"pb{2 + par}",
                                      "gnw_bc"],
                               writes=[f"onb{2 * par}", f"onb{2 * par + 1}"])
                        def ot(e, pr=par):
                            e.transpose(pbh[pr][:, 256:384], onb[2 * pr][:],
                                        ident[:])
                            return e.transpose(pbh[pr][:, 384:512],
                                               onb[2 * pr + 1][:], ident[:])
                        sch.op("pe", ot,
                               reads=[f"onb{2 * par}", f"onb{2 * par + 1}",
                                      "ident"],
                               writes=[f"pbh{par}O"])

                        def oev_raw(e, pr=par, a=qh0, b2=qh1, cc=c):
                            e.activation(ofm[:, a * 1024 + cc * 128:
                                             a * 1024 + (cc + 1) * 128],
                                         pb[2 + pr][:, 0:128], AF.Copy)
                            return e.activation(
                                ofm[:, b2 * 1024 + cc * 128:
                                    b2 * 1024 + (cc + 1) * 128],
                                pb[2 + pr][:, 128:256], AF.Copy)
                        sch.op("act", oev_raw,
                               reads=[f"pb{2 + par}", f"pbh{par}O"],
                               writes=[f"o_{qh0}_{ts_c}", f"o_{qh1}_{ts_c}"])

                        gcol = kvh * 16 + h * 8 + c
                        def dexop(e, kv=kvh, g=gcol, cc=c):
                            e.tensor_copy(dex[:, g:g + 1],
                                          dacc[:, kv:kv + 1])
                            return e.tensor_scalar_mul(
                                dacc[:, kv:kv + 1], dacc[:, kv:kv + 1],
                                elb[:, kv * 8 + cc:kv * 8 + cc + 1])
                        sch.op("dve", dexop, reads=[f"elb{kvh}", "dacc"],
                               writes=["dacc", "dexc"])

                        def supd(e, pr=par, kv=kvh, cc=c):
                            e.tensor_scalar_mul(
                                S[kv][:], S[kv][:],
                                elb[:, kv * 8 + cc:kv * 8 + cc + 1])
                            e.tensor_add(S[kv][:], S[kv][:],
                                         pb[2 + pr][:, 256:384])
                            return e.tensor_copy(Sbuf[kv][:], S[kv][:])
                        sch.op("dve", supd,
                               reads=[f"elb{kvh}", f"pb{2 + par}", f"S{kvh}"],
                               writes=[f"S{kvh}", f"Sb{kvh}"])

                    if h == 0 and c in (3, 7):
                        tsX = c // 4
                        for qh in range(NQ):
                            sch.op("sp", lambda e, qq=qh, tt=tsX: e.dma_start(
                                out=o_sp[:, qq * 1024 + tt * 512:
                                         qq * 1024 + (tt + 1) * 512],
                                in_=ofm[:, qq * 1024 + tt * 512:
                                        qq * 1024 + (tt + 1) * 512]),
                                   reads=[f"o_{qh}_{tsX}"], dma=True,
                                   dsem="osp", writes=[f"osp_{qh}_{tsX}"])

                if DEBUG_ATTN and h == 0:
                    sch.op("act", lambda e: e.dma_start(out=dbg_o, in_=ofm[:]),
                           reads=[f"o_{f}_{t2}" for f in range(NQ)
                                  for t2 in range(2)], dma=True,
                           dsem="dbg3", writes=["dbgo"])

            sch.op("pool", hs_write,
                   reads=["meta"] + [f"S{i}" for i in range(NKV)],
                   writes=["hs_done"])
            sch.op("pool", hs_read, reads=["meta", "hs_done"],
                   writes=[f"S{i}" for i in range(NKV)])
            def sbc(e):
                for i in range(NKV):
                    ins = e.tensor_copy(Sbuf[i][:], S[i][:])
                return ins
            sch.op("dve", sbc, reads=[f"S{i}" for i in range(NKV)],
                   writes=[f"Sb{i}" for i in range(NKV)])

            for qh in range(NQ):
                claims = [f"qrl_{qh}"]
                if qh < 8:
                    claims += [f"ke_{qh}", f"k_{qh}_0", f"k_{qh}_1"]
                else:
                    claims += [f"ku_{qh - 8}"]
                sch.op("sp", lambda e, qq=qh: e.dma_start(
                    out=A[:, qq * 1024:(qq + 1) * 1024],
                    in_=q_sp[:, qq * 1024:(qq + 1) * 1024]),
                       reads=[f"qsp_{qh}"], dma=True, dsem="qrl",
                       writes=claims)

            def corr_unit(hh, kvh, c, par):
                gc = hh * 8 + c
                ts_c = c // 4
                qh0, qh1 = 2 * kvh, 2 * kvh + 1
                dxc = dex[:, kvh * 16 + gc:kvh * 16 + gc + 1]
                if hh == 0:
                    qs0 = A[:, qh0 * 1024 + c * 128:
                            qh0 * 1024 + (c + 1) * 128]
                    qs1 = A[:, qh1 * 1024 + c * 128:
                            qh1 * 1024 + (c + 1) * 128]
                    qkeys = [f"qrl_{qh0}", f"qrl_{qh1}"]
                    slot = kvh % 4
                    base = 16384 + slot * 2048
                    os0 = A[:, base + c * 128:base + (c + 1) * 128]
                    os1 = A[:, base + 1024 + c * 128:
                            base + 1024 + (c + 1) * 128]
                    okr = [f"orl{slot}", f"orl{slot}b"]
                    d0 = C[:, qh0 * 1024 + c * 128:
                           qh0 * 1024 + (c + 1) * 128]
                    d1 = C[:, qh1 * 1024 + c * 128:
                           qh1 * 1024 + (c + 1) * 128]
                    wk = [f"oc_{qh0}_{ts_c}", f"oc_{qh1}_{ts_c}", "xnh"]
                else:
                    qs0 = qp_ap(qh0)[:, c * 128:(c + 1) * 128]
                    qs1 = qp_ap(qh1)[:, c * 128:(c + 1) * 128]
                    qkeys = [f"qp_{qh0}_{ts_c}", f"qp_{qh1}_{ts_c}"]
                    os0 = ofm[:, qh0 * 1024 + c * 128:
                              qh0 * 1024 + (c + 1) * 128]
                    os1 = ofm[:, qh1 * 1024 + c * 128:
                              qh1 * 1024 + (c + 1) * 128]
                    okr = [f"o_{qh0}_{ts_c}", f"o_{qh1}_{ts_c}"]
                    d0, d1 = os0, os1
                    wk = [f"o_{qh0}_{ts_c}", f"o_{qh1}_{ts_c}"]

                def wmul(e, pr=par, a=qs0, b2=qs1, dx=dxc):
                    e.tensor_scalar_mul(vt_sb[pr][:], a, dx)
                    return e.tensor_scalar_mul(ke_sb[pr][:], b2, dx)
                sch.op("dve", wmul, reads=qkeys + ["dexc"],
                       writes=[f"vt{par}", f"ke2{par}"])

                def cmm(e, pr=par, kv=kvh):
                    e.matmul(pb[pr][:, 0:128], vt_sb[pr][:], Sbuf[kv][:],
                             start=True, stop=True)
                    return e.matmul(pb[pr][:, 128:256], ke_sb[pr][:],
                                    Sbuf[kv][:], start=True, stop=True)
                sch.op("pe", cmm, reads=[f"vt{par}", f"ke2{par}", f"Sb{kvh}"],
                       writes=[f"pb{par}"])

                def addo(e, pr=par, a=os0, b2=os1):
                    e.tensor_tensor(oc_t[pr][:, 0:128], a,
                                    pb[pr][:, 0:128], ALU.add)
                    return e.tensor_tensor(oc_t[pr][:, 128:256], b2,
                                           pb[pr][:, 128:256], ALU.add)
                sch.op("dve", addo, reads=okr + [f"pb{par}"],
                       writes=[f"oct{par}", "zf"])

                def rms_a(e, pr=par):
                    e.activation(rms_sq[:], oc_t[pr][:, 0:128], AF.Square,
                                 accum_out=rms_acc[pr][:, 0:1])
                    return e.activation(rms_sq[:], oc_t[pr][:, 128:256],
                                        AF.Square,
                                        accum_out=rms_acc[pr][:, 1:2])
                sch.op("act", rms_a, reads=[f"oct{par}"],
                       writes=[f"rmsacc{par}"])

                def rms_b(e, pr=par):
                    e.activation(rms_rstd[pr][:, 0:2], rms_acc[pr][:, 0:2],
                                 AF.Ln, scale=1.0 / 128.0,
                                 bias=eps_col[:, 0:1])
                    return e.activation(rms_rstd[pr][:, 2:4],
                                        rms_rstd[pr][:, 0:2], AF.Exp,
                                        scale=-0.5)
                sch.op("act", rms_b, reads=[f"rmsacc{par}", "epsc"],
                       writes=[f"rmssq{par}"])

                def gmul(e, pr=par):
                    e.scalar_tensor_tensor(onb[2 * pr][:],
                                           oc_t[pr][:, 0:128],
                                           rms_rstd[pr][:, 2:3], gnw_bc[:],
                                           ALU.mult, ALU.mult)
                    return e.scalar_tensor_tensor(onb[2 * pr + 1][:],
                                                  oc_t[pr][:, 128:256],
                                                  rms_rstd[pr][:, 3:4],
                                                  gnw_bc[:],
                                                  ALU.mult, ALU.mult)
                sch.op("dve", gmul,
                       reads=[f"rmssq{par}", f"oct{par}", "gnw_bc"],
                       writes=[f"onb{2 * par}", f"onb{2 * par + 1}"])

                def ot(e, pr=par):
                    e.transpose(pbh[pr][:, 256:384], onb[2 * pr][:],
                                ident[:])
                    return e.transpose(pbh[pr][:, 384:512],
                                       onb[2 * pr + 1][:], ident[:])
                sch.op("pe", ot,
                       reads=[f"onb{2 * par}", f"onb{2 * par + 1}", "ident"],
                       writes=[f"pbh{par}O"])

                def oevc(e, pr=par, a=d0, b2=d1):
                    e.activation(a, pbh[pr][:, 256:384], AF.Copy)
                    return e.activation(b2, pbh[pr][:, 384:512], AF.Copy)
                sch.op("act", oevc, reads=[f"pbh{par}O"], writes=wk)

            def wo_phase(hh):
                rhsb = C if hh == 0 else ofm
                okey = "oc" if hh == 0 else "o"
                for douh in range(DCH):
                    pp = douh % 2
                    if pp == 0:
                        sch.op("pool", lambda e, d2=douh // 2: e.dma_start(
                            out=awb3, in_=WoR[:, :, d2 * 256:(d2 + 1) * 256]),
                               dma=True, dsem="aw", writes=["aw0", "aw1"])
                    for ts in range(2):
                        q = 2 * hh + ts
                        def wmm(e, p2=pp, tt=ts, rb=rhsb):
                            for f in range(NQ):
                                ins = e.matmul(pb[tt][:, 0:512],
                                               awb[:, f * 256 + p2 * 128:
                                                   f * 256 + (p2 + 1) * 128],
                                               rb[:, f * 1024 + tt * 512:
                                                  f * 1024 + (tt + 1) * 512],
                                               start=(f == 0),
                                               stop=(f == NQ - 1))
                            return ins
                        sch.op("pe", wmm,
                               reads=[f"aw{pp}"]
                                     + [f"{okey}_{f}_{ts}" for f in range(NQ)],
                               writes=[f"pb{ts}"])
                        sxb = (4, 5) if ts == 0 else (2, 3)
                        epilogue_tile(ts, douh, q, douh * 2 + ts, r1_s, "r1",
                                      r_dst, rd_tag, xb_a, "xbA",
                                      (sxb[0], sxb[1], douh == 0,
                                       douh == DCH - 1))
                for ts in range(2):
                    sxb = (4, 5) if ts == 0 else (2, 3)
                    ln_finalize(2 * hh + ts, sxb[0], sxb[1])

            for hh in range(2):
                for kvh in range(NKV):
                    if hh == 0:
                        slot = kvh % 4
                        base = 16384 + slot * 2048
                        for j2 in range(2):
                            q2 = 2 * kvh + j2
                            wrk = [f"orl{slot}" if j2 == 0
                                   else f"orl{slot}b"]
                            if kvh < 4:
                                wrk += [f"v_{2 * slot + j2}_0",
                                        f"v_{2 * slot + j2}_1"]
                            sch.op("sp", lambda e, q3=q2, b2=base, j3=j2:
                                   e.dma_start(
                                       out=A[:, b2 + j3 * 1024:
                                             b2 + (j3 + 1) * 1024],
                                       in_=o_sp[:, q3 * 1024:
                                                (q3 + 1) * 1024]),
                                   reads=[f"osp_{q2}_0", f"osp_{q2}_1"],
                                   dma=True, dsem=f"orl{slot}", writes=wrk)
                    for c in range(NCHH):
                        corr_unit(hh, kvh, c, (kvh * 8 + c) % 2)
                wo_phase(hh)
            sch.fence()

        if n_sub >= 2:
            attn(r_dst_override=(y_out if n_sub == 2 else None))
        if n_sub >= 3:
            ffn(1)

        sch.fence()
        sems = sch.emit(nc, block, ctx)

        dtotals = [(sems[("d", ds)], 16 * n) for ds, n in sch.dcnt.items()]

        @block.sync
        def _(sy):
            for sm, n in dtotals:
                sy.wait_ge(sm, n)

        @block.gpsimd
        def _(g):
            for sm, n in dtotals:
                g.wait_ge(sm, n)

    return nc



def _hadamard(n):
    h = np.array([[1.0]], dtype=np.float32)
    while h.shape[0] < n:
        h = np.block([[h, h], [h, -h]])
    return h


def _consts():
    ident = np.eye(128, dtype=np.float32).astype(ml_dtypes.bfloat16)
    h128 = _hadamard(128).astype(ml_dtypes.bfloat16)
    h64 = _hadamard(64)
    h64x2 = np.zeros((128, 128), np.float32)
    h64x2[0:64, 0:64] = h64
    h64x2[64:128, 64:128] = h64
    h64x2 = h64x2.astype(ml_dtypes.bfloat16)
    mask = np.triu(np.ones((128, 128), np.float32)).astype(ml_dtypes.bfloat16)
    ones_col = np.ones((128, 1), np.float32).astype(ml_dtypes.bfloat16)
    ones_row = np.ones((1, 128), np.float32).astype(ml_dtypes.bfloat16)
    return ident, h128, h64x2, mask, ones_col, ones_row


def _slabT(w):
    return np.ascontiguousarray(
        w.reshape(64, 128, 16, 128).transpose(0, 3, 2, 1).reshape(64, 128, 2048))


def _relayout_col128(vec, ncol):
    return np.ascontiguousarray(vec.reshape(ncol, 128).T)


def kernel(L0_attn_hidden_states, L0_attn_Wq, L0_attn_bq, L0_attn_Wk,
           L0_attn_bk, L0_attn_Wv, L0_attn_bv, L0_attn_Wg1, L0_attn_Wg2,
           L0_attn_bg2, L0_attn_g_norm_w, L0_attn_Wo, L0_ffn_W1, L0_ffn_W2,
           L0_ffn_W3, _n_sub=3):
    x = np.asarray(L0_attn_hidden_states, dtype=np.float32)
    ident, h128, h64x2, mask, ones_col, ones_row = _consts()

    Wg1_t = np.ascontiguousarray(
        np.asarray(L0_attn_Wg1, np.float32).reshape(16, 128, 16)
        .transpose(1, 0, 2).reshape(128, 256))

    common = {
        "Wq": np.asarray(L0_attn_Wq, np.float32),
        "Wk": np.asarray(L0_attn_Wk, np.float32),
        "Wv": np.asarray(L0_attn_Wv, np.float32),
        "Wg2": np.asarray(L0_attn_Wg2, np.float32),
        "Wo": np.asarray(L0_attn_Wo, np.float32),
        "W1S": _slabT(np.asarray(L0_ffn_W1, np.float32)),
        "W2T": np.ascontiguousarray(np.asarray(L0_ffn_W2, np.float32).T),
        "W3S": _slabT(np.asarray(L0_ffn_W3, np.float32)),
        "wg1_t": Wg1_t,
        "bq_t": _relayout_col128(np.asarray(L0_attn_bq, np.float32), 16),
        "bk_t": _relayout_col128(np.asarray(L0_attn_bk, np.float32), 8),
        "bv_t": _relayout_col128(np.asarray(L0_attn_bv, np.float32), 8),
        "bg2_t": _relayout_col128(np.asarray(L0_attn_bg2, np.float32), 8),
        "gnw_t": np.ascontiguousarray(
            np.asarray(L0_attn_g_norm_w, np.float32).reshape(1, 128)),
        "ident_c": ident, "h128_c": h128, "h64x2_c": h64x2, "mask_c": mask,
        "ones_col_c": ones_col, "ones_row_c": ones_row,
    }

    rng = np.random.default_rng()
    nonce = int(rng.integers(1 << 16, 1 << 30))

    nc = build_nc(n_sub=_n_sub)
    in_maps = []
    for c in range(8):
        b, half = c // 2, c % 2
        xc = np.ascontiguousarray(x[b, half * 2048:(half + 1) * 2048, :].T)
        pair_nonce = nonce + (c // 2)
        m = dict(common)
        m["x_in"] = xc
        m["meta_c"] = np.array(
            [[pair_nonce, c % 2, c, pair_nonce, c, 0, 0, 0]], dtype=np.int32)
        in_maps.append(m)

    res = run_bass_kernel_spmd(nc, in_maps, core_ids=list(range(8)))

    if DEBUG_ATTN:
        global _DBG
        _DBG = {k: np.asarray(v, dtype=np.float32)
                for k, v in res.results[0].items() if k.startswith("dbg")}

    out = np.empty((4, 4096, 2048), dtype=np.float32)
    for c in range(8):
        b, half = c // 2, c % 2
        out[b, half * 2048:(half + 1) * 2048, :] = \
            np.asarray(res.results[c]["y_out"]).T
    return out


if __name__ == "__main__":
    import sys
    ns = int(sys.argv[1]) if len(sys.argv) > 1 else 3
    nc = build_nc(n_sub=ns)
    ni = sum(len(v) for v in nc.to_json().get("functions", [{}])[0].get(
        "instructions", [])) if False else 0
    print("build ok")


# version 21
# speedup vs baseline: 1.0504x; 1.0504x over previous
import contextlib

import numpy as np
import ml_dtypes

import concourse.bass as bass
import concourse.mybir as mybir
from concourse.bass_utils import run_bass_kernel_spmd

F32 = mybir.dt.float32
BF16 = mybir.dt.bfloat16
I32 = mybir.dt.int32
AF = mybir.ActivationFunctionType
ALU = mybir.AluOpType

D = 2048
T = 2048
HF = 8192
GN = 64
DCH = 16
TQ = 512
NTQ = 4
TH = 1024
NCHH = 8
NKV = 8
NQ = 16
EPS = 1e-5
FWHT_SCALE = 0.5 / float(np.sqrt(8192.0))
QSCALE = 1.0 / float(np.sqrt(128.0))
GINV = 1.0 / 16.0
POLL_MAX = 500000

ENGS = ["sp", "pe", "dve", "act", "pool"]
_BODY = {"sp": "sync", "pe": "tensor", "dve": "vector", "act": "scalar",
         "pool": "gpsimd"}


PARANOID = False
DEBUG_ATTN = False


class Sched:

    def __init__(self):
        self.ops = {e: [] for e in ENGS}
        self.ccnt = {e: 0 for e in ENGS}
        self.dcnt = {}
        self.down = {}
        self.dclosed = {}
        self.drelax = set()
        self.dflushonly = set()
        self.dflush = {}
        self.last_w = {}
        self.readers = {}
        self.floor = {}

    def op(self, eng, fn, reads=(), writes=(), dma=False, dsem=None):
        if dma:
            assert dsem is not None, "dma ops need a dsem"
            self.down.setdefault(dsem, set()).add(eng)
            assert not self.dclosed.get(dsem), f"dsem {dsem} closed"
        deps = {}
        floor = None
        if PARANOID:
            floor = {("c", e): n for e, n in self.ccnt.items()}
            for ds, n in self.dcnt.items():
                floor[("d", ds)] = n * 16
        else:
            floor = self.floor
        me_stream = ("d", dsem) if dma else ("c", eng)
        for key, tgt in floor.items():
            if key != me_stream and tgt > 0:
                if deps.get(key, 0) < tgt:
                    deps[key] = tgt

        def need(kind, ident, cnt):
            if kind == "c":
                key = ("c", ident)
                if key == me_stream:
                    return
                tgt = cnt
            else:
                ds = ident
                if ds in self.dflush and cnt <= self.dflush[ds][0]:
                    _, feng, fcnt = self.dflush[ds]
                    need("c", feng, fcnt)
                    return
                if ds in self.drelax:
                    cnt = self.dcnt[ds]
                    self.dclosed[ds] = True
                key = ("d", ds)
                if key == me_stream:
                    return
                tgt = cnt * 16
            if deps.get(key, 0) < tgt:
                deps[key] = tgt

        def need_entry(ent):
            kind, ident, cnt = ent
            need(kind, ident, cnt)

        for r in reads:
            w = self.last_w.get(r)
            if w is not None:
                need_entry(w)
        for w_ in writes:
            lw = self.last_w.get(w_)
            if lw is not None:
                need_entry(lw)
            for rd in self.readers.get(w_, ()):
                need_entry(rd)

        for key in list(deps):
            if key[0] == "d":
                deps[key] = 16 * self.dcnt[key[1]]
        if dma:
            self.dcnt[dsem] = self.dcnt.get(dsem, 0) + 1
            me = ("d", dsem, self.dcnt[dsem])
        else:
            self.ccnt[eng] += 1
            me = ("c", eng, self.ccnt[eng])
        self.ops[eng].append((fn, dma, dsem, deps))
        for r in reads:
            self.readers.setdefault(r, []).append(me)
        for w_ in writes:
            self.last_w[w_] = me
            self.readers[w_] = []

    def fence(self):
        by_eng = {}
        for ds, engs in self.down.items():
            n = self.dcnt.get(ds, 0)
            if n > 0 and self.dflush.get(ds, (0,))[0] < n:
                for eng in engs:
                    by_eng.setdefault(eng, []).append((ds, n))
        for eng, lst in by_eng.items():
            def flush(e, lst=tuple(lst)):
                for ds, n in lst:
                    e.wait_ge(self._sems[("d", ds)], 16 * n)
                return e.nop()
            self.op(eng, flush)
            for ds, n in lst:
                self.dflush[ds] = (n, eng, self.ccnt[eng])
        self.floor = {("c", e): n for e, n in self.ccnt.items() if n > 0}

    def emit(self, nc, block, ctx):
        sems = {}
        for e in ENGS:
            sems[("c", e)] = ctx.enter_context(nc.semaphore(f"sc_{e}"))
        for ds in self.down:
            sems[("d", ds)] = ctx.enter_context(nc.semaphore(f"sd_{ds}"))
        self._sems = sems

        def body_for(e):
            ops = self.ops[e]

            def run(eng):
                hw = {}
                for fn, dma, dsem, deps in ops:
                    for key, tgt in deps.items():
                        if hw.get(key, 0) >= tgt:
                            continue
                        eng.wait_ge(sems[key], tgt)
                        hw[key] = tgt
                    instr = fn(eng)
                    if dma:
                        instr.then_inc(sems[("d", dsem)], 16)
                    else:
                        instr.then_inc(sems[("c", e)], 1)
            return run

        for e in ENGS:
            getattr(block, _BODY[e])(body_for(e))
        return sems


def build_nc(n_sub=3):
    nc = bass.Bass("TRN2", target_bir_lowering=False)

    x_in = nc.dram_tensor("x_in", [D, T], F32, kind="ExternalInput").ap()
    y_out = nc.dram_tensor("y_out", [D, T], F32, kind="ExternalOutput").ap()

    Wq = nc.dram_tensor("Wq", [D, D], F32, kind="ExternalInput").ap()
    Wk = nc.dram_tensor("Wk", [D, 1024], F32, kind="ExternalInput").ap()
    Wv = nc.dram_tensor("Wv", [D, 1024], F32, kind="ExternalInput").ap()
    Wg2 = nc.dram_tensor("Wg2", [16, 1024], F32, kind="ExternalInput").ap()
    Wo = nc.dram_tensor("Wo", [D, D], F32, kind="ExternalInput").ap()
    W1S = nc.dram_tensor("W1S", [GN, 128, D], F32, kind="ExternalInput").ap()
    W2T = nc.dram_tensor("W2T", [HF, D], F32, kind="ExternalInput").ap()
    W3S = nc.dram_tensor("W3S", [GN, 128, D], F32, kind="ExternalInput").ap()

    wg1_t = nc.dram_tensor("wg1_t", [128, 256], F32, kind="ExternalInput").ap()
    bq_t = nc.dram_tensor("bq_t", [128, 16], F32, kind="ExternalInput").ap()
    bk_t = nc.dram_tensor("bk_t", [128, 8], F32, kind="ExternalInput").ap()
    bv_t = nc.dram_tensor("bv_t", [128, 8], F32, kind="ExternalInput").ap()
    bg2_t = nc.dram_tensor("bg2_t", [128, 8], F32, kind="ExternalInput").ap()
    gnw_t = nc.dram_tensor("gnw_t", [1, 128], F32, kind="ExternalInput").ap()

    ident_c = nc.dram_tensor("ident_c", [128, 128], BF16, kind="ExternalInput").ap()
    h128_c = nc.dram_tensor("h128_c", [128, 128], BF16, kind="ExternalInput").ap()
    h64x2_c = nc.dram_tensor("h64x2_c", [128, 128], BF16, kind="ExternalInput").ap()
    mask_c = nc.dram_tensor("mask_c", [128, 128], BF16, kind="ExternalInput").ap()
    ones_col_c = nc.dram_tensor("ones_col_c", [128, 1], BF16, kind="ExternalInput").ap()
    ones_row_c = nc.dram_tensor("ones_row_c", [1, 128], BF16, kind="ExternalInput").ap()
    meta_c = nc.dram_tensor("meta_c", [1, 8], I32, kind="ExternalInput").ap()

    W1T_s = nc.dram_tensor("W1T_s", [GN, 128, D], BF16, kind="Internal").ap()
    W3T_s = nc.dram_tensor("W3T_s", [GN, 128, D], BF16, kind="Internal").ap()
    Z_s = nc.dram_tensor("Z_s", [HF, D], BF16, kind="Internal").ap()
    W2pp_s = nc.dram_tensor("W2pp_s", [4, GN, 128, 512], BF16, kind="Internal").ap()
    r1_s = nc.dram_tensor("r1_s", [D, T], F32, kind="Internal").ap()
    r2_s = nc.dram_tensor("r2_s", [D, T], F32, kind="Internal").ap()
    xb_a = nc.dram_tensor("xb_a", [D, T], BF16, kind="Internal").ap()
    xb_b = nc.dram_tensor("xb_b", [D, T], BF16, kind="Internal").ap()
    q_sp = nc.dram_tensor("q_sp", [128, 16384], BF16, kind="Internal").ap()
    o_sp = nc.dram_tensor("o_sp", [128, 16384], BF16, kind="Internal").ap()

    if DEBUG_ATTN:
        dbg_k = nc.dram_tensor("dbg_k", [128, 8192], BF16, kind="ExternalOutput").ap()
        dbg_ku = nc.dram_tensor("dbg_ku", [128, 8192], BF16, kind="ExternalOutput").ap()
        dbg_v = nc.dram_tensor("dbg_v", [128, 8192], BF16, kind="ExternalOutput").ap()
        dbg_fq = nc.dram_tensor("dbg_fq", [128, 8192], BF16, kind="ExternalOutput").ap()
        dbg_qp = nc.dram_tensor("dbg_qp", [128, 16384], BF16, kind="ExternalOutput").ap()
        dbg_o = nc.dram_tensor("dbg_o", [128, 16384], BF16, kind="ExternalOutput").ap()
        dbg_xn = nc.dram_tensor("dbg_xn", [128, 16384], BF16, kind="ExternalOutput").ap()
    sh_state = nc.dram_tensor("sh_state", [NKV * 128, 128], F32, kind="Internal",
                              addr_space="Shared").ap()
    sh_canary = nc.dram_tensor("sh_canary", [1, 16], I32, kind="Internal",
                               addr_space="Shared").ap()

    with contextlib.ExitStack() as ctx:
        sb = lambda name, shape, dt: ctx.enter_context(nc.sbuf_tensor(name, shape, dt))
        psb = lambda name, shape, dt: ctx.enter_context(nc.psum_tensor(name, shape, dt))

        pb = [psb(f"pb{i}", [128, 512], F32) for i in range(6)]
        pbh = [psb(f"pbh{i}", [128, 1024], BF16) for i in range(2)]

        A = sb("A", [128, 32768], BF16)
        B = sb("B", [128, 16384], BF16)
        C = sb("C", [128, 16384], BF16)
        Dn = sb("Dn", [128, 16384], BF16)
        E = sb("E", [128, 4096], F32)

        ident = sb("ident", [128, 128], BF16)
        h128 = sb("h128", [128, 128], BF16)
        h64x2 = sb("h64x2", [128, 128], BF16)
        mask = sb("mask", [128, 128], BF16)
        gnw_bc = sb("gnw_bc", [128, 128], BF16)
        ones_col = sb("ones_col", [128, 1], BF16)
        ones_row = sb("ones_row", [1, 128], BF16)
        gnw_row = sb("gnw_row", [1, 128], F32)
        gnw_rowb = sb("gnw_rowb", [1, 128], BF16)
        meta = sb("meta", [1, 8], I32)
        can_sb = sb("can_sb", [1, 16], I32)
        bq_sb = sb("bq_sb", [128, 16], F32)
        bk_sb = sb("bk_sb", [128, 8], F32)
        bv_sb = sb("bv_sb", [128, 8], F32)
        bg2_sb = sb("bg2_sb", [128, 8], F32)
        wg1b = sb("wg1b", [128, 256], BF16)
        wg2b = sb("wg2b", [16, 1024], BF16)
        zeros128 = sb("zeros128", [128, 128], F32)
        eps_col = sb("eps_col", [128, 1], F32)
        mrbc = sb("mrbc", [128, 4096], BF16)
        rowf_t = sb("rowf_t", [128, 1024], F32)
        rowb = sb("rowb", [1, 1024], BF16)
        S = [sb(f"S{i}", [128, 128], F32) for i in range(NKV)]
        Sbuf = [sb(f"Sb{i}", [128, 128], BF16) for i in range(NKV)]
        elb = sb("elb", [128, 64], F32)
        atb = [sb(f"atb{i}", [128, 128], BF16) for i in range(4)]
        vt_sb = [sb(f"vt{i}", [128, 128], BF16) for i in range(2)]
        ke_sb = [sb(f"ke{i}", [128, 128], BF16) for i in range(2)]
        onb = [sb(f"onb{i}", [128, 128], BF16) for i in range(4)]
        otmp = [sb(f"otmp{i}", [128, 128], BF16) for i in range(4)]
        rms_sq = sb("rms_sq", [128, 128], BF16)
        rms_acc = [sb(f"rmsa{i}", [128, 2], F32) for i in range(2)]
        rms_rstd = [sb(f"rmsr{i}", [128, 8], F32) for i in range(2)]
        dex = sb("dex", [128, 128], F32)
        dacc = sb("dacc", [128, 8], F32)
        lnm2 = sb("lnm2", [1, 512], BF16)

        sgt = [E[:, 0:512], E[:, 512:1024]]
        rcur = [E[:, 1024:1536], E[:, 1536:2048]]
        rnew = [E[:, 2048:2560], E[:, 2560:3072]]
        w3f = [E[:, 1024:2048], E[:, 2048:3072]]
        zf = E[:, 0:512]
        zmin = E[:, 512:1024]
        zab = E[:, 3072:3584]
        zex = E[:, 0:512]
        zln = E[:, 3072:3584]
        wg1f = E[:, 3584:3840]
        wg2f = E[0:16, 1024:2048]
        gk_v = E[:, 1024:2048]
        Bcum = E[:, 2048:3072]
        oc_t = [E[:, 0:256], E[:, 256:512]]

        w1sl = [C[:, 0:2048], C[:, 2048:4096]]
        w3sl = [C[:, 4096:6144], C[:, 6144:8192]]
        w2bl4 = [C[:, 8192:10240], C[:, 10240:12288]]
        usb_v = [Dn[:, 0:2048], Dn[:, 2048:4096]]
        zsl = [Dn[:, 4096:6144], Dn[:, 6144:8192]]
        wld = [Dn[:, 8192 + i * 2048:8192 + (i + 1) * 2048] for i in range(4)]
        gsl1 = [Dn[:, 0:2048], Dn[:, 2048:4096]]
        gsl3 = [Dn[:, 4096:6144], Dn[:, 6144:8192]]
        xnh = C
        kfm = A[:, 0:8192]
        kefm = A[:, 8192:16384]
        vfm = A[:, 16384:24576]
        fq = A[:, 24576:32768]
        xnq = B[:, 0:8192]
        awb = B[:, 8192:12288]
        xbq_t = [B[:, 12288:12800], B[:, 12800:13312]]
        xbn_t = [B[:, 13312:13824], B[:, 13824:14336]]
        xbn2_t = [B[:, 14336:14848], B[:, 14848:15360]]

        def qp_ap(qh):
            if qh < 8:
                return B[:, qh * 1024:(qh + 1) * 1024]
            if qh < 15:
                return A[:, 24576 + (qh - 8) * 1024:24576 + (qh - 7) * 1024]
            return B[:, 15360:16384]
        ofm = Dn
        aws = [awb[:, 0:2048], awb[:, 2048:4096]]
        awbv = Dn[:, 3072:7168]
        ebm_t = Dn[:, 0:1024]
        z1b = Dn[0:16, 1024:2048]
        qbt = [Dn[:, 2048:2560], Dn[:, 2560:3072]]

        s_hp = ctx.enter_context(nc.semaphore("s_hp"))
        s_hs = ctx.enter_context(nc.semaphore("s_hs"))

        sch = Sched()
        block = ctx.enter_context(nc.Block())

        def ld(dst, src, wkeys, eng="sp", rkeys=(), dsem=None):
            sch.op(eng, lambda e, d=dst, s=src: e.dma_start(out=d, in_=s),
                   reads=rkeys, writes=wkeys, dma=True,
                   dsem=dsem if dsem is not None else wkeys[0])

        sch.drelax.add("consts")
        ld(ident[:], ident_c, ["ident"], dsem="consts")
        ld(h128[:], h128_c, ["h128"], dsem="consts")
        ld(h64x2[:], h64x2_c, ["h64x2"], dsem="consts")
        ld(mask[:], mask_c, ["mask"], dsem="consts")
        ld(ones_col[:], ones_col_c, ["ones_col"], dsem="consts")
        ld(ones_row[:], ones_row_c, ["ones_row"], dsem="consts")
        ld(meta[:], meta_c, ["meta"], dsem="consts")
        ld(bq_sb[:], bq_t, ["bq"], dsem="consts")
        ld(bk_sb[:], bk_t, ["bk"], dsem="consts")
        ld(bv_sb[:], bv_t, ["bv"], dsem="consts")
        ld(bg2_sb[:], bg2_t, ["bg2"], dsem="consts")
        ld(gnw_row[:], gnw_t, ["gnw_row"], dsem="consts")

        def memsets(e):
            e.memset(zeros128[:], 0.0)
            e.memset(eps_col[:], EPS)
            e.memset(dacc[:], 1.0)
            for i in range(NKV):
                e.memset(S[i][:], 0.0)
                ins = e.memset(Sbuf[i][:], 0.0)
            return ins
        sch.op("pool", memsets,
               writes=["zeros128", "epsc", "dacc"]
                      + [f"S{i}" for i in range(NKV)]
                      + [f"Sb{i}" for i in range(NKV)])

        sch.op("dve", lambda e: e.tensor_scalar_mul(bq_sb[:], bq_sb[:], QSCALE),
               reads=["bq"], writes=["bq"])
        sch.op("dve", lambda e: e.tensor_copy(gnw_rowb[:], gnw_row[:]),
               reads=["gnw_row"], writes=["gnw_rowb"])
        sch.op("pe", lambda e: e.matmul(pb[0][:, 0:128], ones_row[:],
                                        gnw_rowb[:], start=True, stop=True),
               reads=["ones_row", "gnw_rowb"], writes=["pb0"])
        sch.op("act", lambda e: e.activation(gnw_bc[:], pb[0][:, 0:128], AF.Copy),
               reads=["pb0"], writes=["gnw_bc"])

        def ln_finalize(q, bx, bx2):
            def v1(e):
                e.tensor_scalar(rowf_t[0:1, 0:512], pb[bx][0:1, 0:512],
                                1.0 / 2048.0, None, ALU.mult)
                e.tensor_scalar(rowf_t[0:1, 512:1024], pb[bx2][0:1, 0:512],
                                1.0 / 2048.0, None, ALU.mult)
                return e.tensor_copy(rowb[0:1, 0:512], rowf_t[0:1, 0:512])
            sch.op("dve", v1, reads=[f"pb{bx}", f"pb{bx2}"],
                   writes=["rowf", "rowbm"])
            sch.op("act", lambda e: e.activation(lnm2[0:1, 0:512],
                                                 rowf_t[0:1, 0:512], AF.Square),
                   reads=["rowf"], writes=["rowfm2", "lnm2"])
            sch.op("dve", lambda e: e.tensor_sub(rowf_t[0:1, 0:512],
                                                 rowf_t[0:1, 512:1024],
                                                 lnm2[0:1, 0:512]),
                   reads=["rowf", "rowfm2"], writes=["rowf2"])
            sch.op("act", lambda e: e.activation(rowf_t[0:1, 512:1024],
                                                 rowf_t[0:1, 0:512],
                                                 AF.Sqrt,
                                                 bias=eps_col[0:1, 0:1]),
                   reads=["rowf2", "epsc"], writes=["rowf3"])
            def v3(e):
                e.reciprocal(rowf_t[0:1, 0:512], rowf_t[0:1, 512:1024])
                return e.tensor_copy(rowb[0:1, 512:1024], rowf_t[0:1, 0:512])
            sch.op("dve", v3, reads=["rowf3", "rowbm"], writes=["rowb"])
            def bcast(e):
                e.matmul(pb[0][:, 0:512], ones_row[:], rowb[0:1, 0:512],
                         start=True, stop=True)
                return e.matmul(pb[1][:, 0:512], ones_row[:], rowb[0:1, 512:1024],
                                start=True, stop=True)
            sch.op("pe", bcast, reads=["rowb", "ones_row"], writes=["pb0", "pb1"])
            def evb(e):
                e.activation(mrbc[:, q * 1024:q * 1024 + 512], pb[0][:, 0:512],
                             AF.Copy)
                return e.activation(mrbc[:, q * 1024 + 512:(q + 1) * 1024],
                                    pb[1][:, 0:512], AF.Copy)
            sch.op("act", evb, reads=["pb0", "pb1"], writes=[f"mrbc{q}"])

        def epilogue_tile(bk_i, dch, q, j, r_src, rs_key, r_dst, rd_tag,
                          xb_dst, xb_tag, stats):
            p2 = j % 2
            rk = [f"rcur{p2}"] + ([f"{rs_key}_{dch}_{q}"] if rs_key else [])
            ld(rcur[p2], r_src[dch * 128:(dch + 1) * 128, q * TQ:(q + 1) * TQ],
               [f"rcur{p2}"], rkeys=rk)
            sch.op("dve", lambda e, b=bk_i, p=p2: e.tensor_add(
                rnew[p], pb[b][:, 0:512], rcur[p]),
                   reads=[f"pb{bk_i}", f"rcur{p2}"], writes=[f"rnew{p2}"])
            sch.op("act", lambda e, p=p2, d=dch, qq=q, rd=r_dst: e.dma_start(
                out=rd[d * 128:(d + 1) * 128, qq * TQ:(qq + 1) * TQ],
                in_=rnew[p]),
                   reads=[f"rnew{p2}"], dma=True, dsem=f"strnew{p2}",
                   writes=[f"{rd_tag}_{dch}_{q}"])
            if xb_dst is not None:
                sch.op("dve", lambda e, p=p2: e.tensor_copy(xbn_t[p], rnew[p]),
                       reads=[f"rnew{p2}"], writes=[f"xbn{p2}"])
                sch.op("act", lambda e, p=p2, d=dch, qq=q, xd=xb_dst: e.dma_start(
                    out=xd[d * 128:(d + 1) * 128, qq * TQ:(qq + 1) * TQ],
                    in_=xbn_t[p]),
                       reads=[f"xbn{p2}"], dma=True, dsem=f"stxbn{p2}",
                       writes=[f"{xb_tag}_{dch}_{q}"])
            if stats is not None:
                sx, sx2, first, last = stats
                sch.op("act", lambda e, p=p2: e.activation(xbn2_t[p], xbn_t[p],
                                                           AF.Square),
                       reads=[f"xbn{p2}"], writes=[f"xbn2{p2}"])
                def smm(e, p=p2, f=first, l=last):
                    e.matmul(pb[sx][0:1, 0:512], ones_col[:], xbn_t[p][:],
                             start=f, stop=l)
                    return e.matmul(pb[sx2][0:1, 0:512], ones_col[:],
                                    xbn2_t[p][:], start=f, stop=l)
                sch.op("pe", smm, reads=[f"xbn{p2}", f"xbn2{p2}", "ones_col"],
                       writes=[f"pb{sx}", f"pb{sx2}"])


        Z_r = Z_s.rearrange("(a b) d -> b a d", b=128)
        W2T_r = W2T.rearrange("(a b) d -> b a d", b=128)

        def p4_step(b0):
            p = b0 % 2
            sch.op("pool", lambda e, bb=b0, pp=p: e.dma_start(
                out=usb_v[pp][0:64, :], in_=W2T_r[bb]),
                   writes=[f"usb{p}a"], dma=True,
                   dsem=f"usb{p}")
            sch.op("pool", lambda e, bb=b0, pp=p: e.dma_start(
                out=usb_v[pp][64:128, :], in_=W2T_r[bb + 64]),
                   writes=[f"usb{p}b"], dma=True, dsem=f"usb{p}")
            for dsl in range(4):
                bkx = 4 + dsl % 2
                sch.op("pe", lambda e, pp=p, dd=dsl, b2=bkx: e.matmul(
                    pb[b2][:, 0:512], h64x2[:],
                    usb_v[pp][:, dd * 512:(dd + 1) * 512], start=True, stop=True),
                       reads=[f"usb{p}a", f"usb{p}b", "h64x2"],
                       writes=[f"pb{bkx}"])
                sch.op("act", lambda e, pp=p, dd=dsl, b2=bkx: e.activation(
                    zsl[pp][:, dd * 512:(dd + 1) * 512], pb[b2][:, 0:512],
                    AF.Copy),
                       reads=[f"pb{bkx}"], writes=[f"zsl{p}_{dsl}"])
            zreads = [f"zsl{p}_{d}" for d in range(4)]
            sch.op("act", lambda e, bb=b0, pp=p: e.dma_start(
                out=Z_r[bb], in_=zsl[pp][0:64, :]),
                   reads=zreads, dma=True, dsem=f"stzsl{p}", writes=[f"z_{b0}a"])
            sch.op("act", lambda e, bb=b0, pp=p: e.dma_start(
                out=Z_r[bb + 64], in_=zsl[pp][64:128, :]),
                   reads=zreads, dma=True, dsem=f"stzsl{p}", writes=[f"z_{b0}b"])

        z_all = [f"z_{b}a" for b in range(64)] + [f"z_{b}b" for b in range(64)]

        def p5_step(g):
            p = g % 2
            sch.op("sp", lambda e, gg=g, pp=p: e.dma_start(
                out=usb_v[pp], in_=Z_s[gg * 128:(gg + 1) * 128, :]),
                   reads=z_all,
                   writes=[f"usb{p}a", f"usb{p}b"], dma=True, dsem=f"usb{p}")
            for dsl in range(4):
                bkx = 4 + dsl % 2
                sch.op("pe", lambda e, pp=p, dd=dsl, b2=bkx: e.matmul(
                    pb[b2][:, 0:512], h128[:],
                    usb_v[pp][:, dd * 512:(dd + 1) * 512], start=True, stop=True),
                       reads=[f"usb{p}a", "h128"], writes=[f"pb{bkx}"])
                sch.op("act", lambda e, pp=p, dd=dsl, b2=bkx: e.activation(
                    zsl[pp][:, dd * 512:(dd + 1) * 512], pb[b2][:, 0:512],
                    AF.Copy, scale=FWHT_SCALE),
                       reads=[f"pb{bkx}"], writes=[f"zsl{p}_{dsl}"])
                sch.op("act", lambda e, pp=p, dd=dsl, gg=g: e.dma_start(
                    out=W2pp_s[dd, gg], in_=zsl[pp][:, dd * 512:(dd + 1) * 512]),
                       reads=[f"zsl{p}_{dsl}"], dma=True, dsem=f"stzsl{p}",
                       writes=[f"w2pp_{dsl}_{g}"])

        ring = [rcur[0], rcur[1], rnew[0], rnew[1]]
        ringk = ["rcur0", "rcur1", "rnew0", "rnew1"]
        for q in range(NTQ):
            for dch in range(DCH):
                i4 = (q * DCH + dch) % 4
                p = (q * DCH + dch) % 2
                ld(ring[i4], x_in[dch * 128:(dch + 1) * 128,
                                  q * TQ:(q + 1) * TQ],
                   [ringk[i4]], dsem=ringk[i4])
                sch.op("dve", lambda e, pp=p, r4=i4: e.tensor_copy(
                    xbn_t[pp], ring[r4]),
                       reads=[ringk[i4]], writes=[f"xbn{p}"])
                sch.op("act", lambda e, pp=p, d=dch, qq=q: e.dma_start(
                    out=xb_a[d * 128:(d + 1) * 128, qq * TQ:(qq + 1) * TQ],
                    in_=xbn_t[pp]),
                       reads=[f"xbn{p}"], dma=True, dsem=f"stxbn{p}",
                       writes=[f"xbA_{dch}_{q}"])
                sch.op("act", lambda e, pp=p: e.activation(xbn2_t[pp], xbn_t[pp],
                                                           AF.Square),
                       reads=[f"xbn{p}"], writes=[f"xbn2{p}"])
                def smm(e, pp=p, f=(dch == 0), l=(dch == DCH - 1)):
                    e.matmul(pb[4][0:1, 0:512], ones_col[:], xbn_t[pp][:],
                             start=f, stop=l)
                    return e.matmul(pb[5][0:1, 0:512], ones_col[:],
                                    xbn2_t[pp][:], start=f, stop=l)
                sch.op("pe", smm, reads=[f"xbn{p}", f"xbn2{p}", "ones_col"],
                       writes=["pb4", "pb5"])
            ln_finalize(q, 4, 5)

        def ffn(idx, r_dst_override=None):
            r_src = x_in if idx == 0 else r2_s
            rs_key = None if idx == 0 else "r2"
            final = (idx == 1)
            r_dst = r_dst_override if r_dst_override is not None else \
                (r1_s if idx == 0 else y_out)
            rd_tag = "y" if (final or r_dst_override is not None) else "r1"
            xb_dst = None if final else xb_b
            prep = (idx == 0)
            for q in range(NTQ):
                for dch in range(DCH):
                    p = dch % 2
                    xk = [f"xbq{p}"]
                    if prep and q == 0 and dch < 2:
                        pass
                    ld(xbq_t[p], xb_a[dch * 128:(dch + 1) * 128,
                                      q * TQ:(q + 1) * TQ],
                       xk, rkeys=[f"xbA_{dch}_{q}"])
                    def xn(e, pp=p, d=dch, qq=q):
                        e.tensor_tensor(xbn_t[pp][:], xbq_t[pp][:],
                                        mrbc[:, qq * 1024:qq * 1024 + 512],
                                        ALU.subtract)
                        return e.tensor_tensor(
                            xnq[:, d * 512:(d + 1) * 512], xbn_t[pp][:],
                            mrbc[:, qq * 1024 + 512:(qq + 1) * 1024], ALU.mult)
                    sch.op("dve", xn, reads=[f"xbq{p}", f"mrbc{q}"],
                           writes=["xnq", f"xbn{p}"])
                for g in range(GN):
                    p = g % 2
                    if prep and q == 0:
                        if g < 32:
                            for j in range(2):
                                p4_step(2 * g + j)
                        else:
                            for j in range(2):
                                p5_step(2 * (g - 32) + j)
                        sch.op("pool", lambda e, gg=g, pp=p: e.dma_start(
                            out=w1sl[pp], in_=W1S[gg]),
                               dma=True, dsem=f"w1sl{p}",
                               writes=[f"w1sl{p}"])
                        sch.op("act", lambda e, gg=g, pp=p: e.dma_start(
                            out=W1T_s[gg], in_=w1sl[pp]),
                               reads=[f"w1sl{p}"], dma=True, dsem="stgsl0",
                               writes=[f"w1t_{g}"])
                        for hf in range(2):
                            sch.op("sp", lambda e, gg=g, hh=hf: e.dma_start(
                                out=w3f[hh],
                                in_=W3S[gg, :, hh * 1024:(hh + 1) * 1024]),
                                   dma=True, dsem=f"w3f{hf}",
                                   writes=([f"w3f{hf}", "rcur0",
                                            "rcur1"] if hf == 0 else
                                           [f"w3f{hf}", "rnew0",
                                            "rnew1"]))
                        def cst(e, pp=p):
                            e.tensor_copy(w3sl[pp][:, 0:1024], w3f[0])
                            return e.tensor_copy(w3sl[pp][:, 1024:2048],
                                                 w3f[1])
                        sch.op("dve", cst,
                               reads=["w3f0", "w3f1", "rcur0",
                                      "rcur1", "rnew0", "rnew1"],
                               writes=[f"w3sl{p}"])
                        sch.op("act", lambda e, gg=g, pp=p: e.dma_start(
                            out=W3T_s[gg], in_=w3sl[pp]),
                               reads=[f"w3sl{p}"], dma=True, dsem="stgsl1",
                               writes=[f"w3t_{g}"])
                        lhs1, lhs3 = w1sl[p], w3sl[p]
                        mmreads = [f"w1sl{p}", f"w3sl{p}", "xnq"]
                        mmwrites = [f"pb{2 * p}", f"pb{2 * p + 1}"]
                    else:
                        ld(w1sl[p], W1T_s[g], [f"w1sl{p}"],
                           rkeys=[f"w1t_{g}"])
                        ld(w3sl[p], W3T_s[g], [f"w3sl{p}"],
                           rkeys=[f"w3t_{g}"])
                        lhs1, lhs3 = w1sl[p], w3sl[p]
                        mmreads = [f"w1sl{p}", f"w3sl{p}", "xnq"]
                        mmwrites = [f"pb{2 * p}", f"pb{2 * p + 1}"]
                    def mm1(e, l1=lhs1, l3=lhs3, pp=p):
                        for d in range(DCH):
                            e.matmul(pb[2 * pp][:, 0:512],
                                     l1[:, d * 128:(d + 1) * 128],
                                     xnq[:, d * 512:(d + 1) * 512],
                                     start=(d == 0), stop=(d == DCH - 1))
                        for d in range(DCH):
                            ins = e.matmul(pb[2 * pp + 1][:, 0:512],
                                           l3[:, d * 128:(d + 1) * 128],
                                           xnq[:, d * 512:(d + 1) * 512],
                                           start=(d == 0), stop=(d == DCH - 1))
                        return ins
                    sch.op("pe", mm1, reads=mmreads, writes=mmwrites)
                    sch.op("act", lambda e, pp=p: e.activation(
                        sgt[pp], pb[2 * pp][:, 0:512], AF.Silu),
                           reads=[f"pb{2 * p}"], writes=[f"sgt{p}"])
                    sch.op("dve", lambda e, pp=p, gg=g: e.tensor_tensor(
                        A[:, gg * 512:(gg + 1) * 512], sgt[pp],
                        pb[2 * pp + 1][:, 0:512], ALU.mult),
                           reads=[f"sgt{p}", f"pb{2 * p + 1}"],
                           writes=[f"hid{g}"])
                for quad in range(4):
                    for g4 in range(GN // 4):
                        p = (quad * (GN // 4) + g4) % 2
                        src = W2pp_s[quad, 4 * g4:4 * g4 + 4].rearrange(
                            "g p d -> p g d")
                        sch.op("sp", lambda e, s2=src, pp=p: e.dma_start(
                            out=w2bl4[pp].rearrange("p (g d) -> p g d", d=512),
                            in_=s2),
                               reads=[f"w2pp_{quad}_{4 * g4 + j}"
                                      for j in range(4)],
                               dma=True, dsem=f"w2bl{p}",
                               writes=[f"w2bl{p}"])
                        def mm2(e, pp=p, g0=4 * g4):
                            for j in range(4):
                                for dl in range(4):
                                    ins = e.matmul(
                                        pb[dl][:, 0:512],
                                        w2bl4[pp][:, j * 512 + dl * 128:
                                                  j * 512 + (dl + 1) * 128],
                                        A[:, (g0 + j) * 512:(g0 + j + 1) * 512],
                                        start=(g0 + j == 0),
                                        stop=(g0 + j == GN - 1))
                            return ins
                        sch.op("pe", mm2,
                               reads=[f"w2bl{p}"] +
                                     [f"hid{4 * g4 + j}" for j in range(4)],
                               writes=["pb0", "pb1", "pb2", "pb3"])
                    for dl in range(4):
                        dch = quad * 4 + dl
                        st = None if final else \
                            (4, 5, dch == 0, dch == DCH - 1)
                        epilogue_tile(dl, dch, q, quad * 4 + dl, r_src, rs_key,
                                      r_dst, rd_tag, xb_dst, "xbB", st)
                if not final:
                    ln_finalize(q, 4, 5)
            sch.fence()

        ffn(0, r_dst_override=(y_out if n_sub == 1 else None))

        def hs_read(e):
            with (e.register("rnonce") as rnonce, e.register("rpar") as rpar,
                  e.register("rn") as rn, e.register("rw") as rw,
                  e.register("rc") as rc, e.register("riters") as riters,
                  e.register("rcond") as rcond, e.register("rtmp") as rtmp):
                e.reg_mov(rnonce, e.value_load(meta[0:1, 0:1]))
                e.reg_mov(rpar, e.value_load(meta[0:1, 1:2]))
                e.reg_alu(rn, rpar, 0, ALU.add)
                with e.While(rn):
                    e.reg_mov(rw, 0)
                    e.reg_mov(riters, POLL_MAX)
                    e.reg_mov(rcond, 1)
                    with e.While(rcond):
                        e.dma_start(out=can_sb[0:1, 0:2],
                                    in_=sh_canary[0:1, 0:2]).then_inc(s_hp, 16)
                        e.reg_add(rw, rw, 16)
                        e.wait_ge(s_hp, rw)
                        e.reg_mov(rc, e.value_load(can_sb[0:1, 0:1]))
                        e.reg_sub(riters, riters, 1)
                        e.reg_alu(rcond, rc, rnonce, ALU.not_equal)
                        e.reg_alu(rtmp, riters, 0, ALU.is_gt)
                        e.reg_alu(rcond, rcond, rtmp, ALU.bitwise_and)
                    for i in range(NKV):
                        e.dma_start(out=S[i][:],
                                    in_=sh_state[i * 128:(i + 1) * 128, :]
                                    ).then_inc(s_hp, 16)
                    e.reg_add(rw, rw, 16 * NKV)
                    e.wait_ge(s_hp, rw)
                    e.reg_alu(rcond, rc, rnonce, ALU.not_equal)
                    e.reg_mov(rc, e.value_load(can_sb[0:1, 1:2]))
                    e.reg_add(rc, rc, 1)
                    e.reg_mov(rtmp, e.value_load(meta[0:1, 2:3]))
                    e.reg_alu(rtmp, rc, rtmp, ALU.not_equal)
                    e.reg_alu(rcond, rcond, rtmp, ALU.bitwise_or)
                    with e.While(rcond):
                        for i in range(NKV):
                            e.memset(S[i][:], float("nan"))
                        e.reg_mov(rcond, 0)
                    e.reg_sub(rn, rn, 1)
                e.reg_mov(rn, 1)
                e.reg_sub(rn, rn, rpar)
                with e.While(rn):
                    for i in range(NKV):
                        e.memset(S[i][:], 0.0)
                    e.reg_sub(rn, rn, 1)
            return e.nop()

        def hs_write(e):
            with (e.register("rpar2") as rpar, e.register("rn2") as rn):
                e.reg_mov(rpar, e.value_load(meta[0:1, 1:2]))
                e.reg_mov(rn, 1)
                e.reg_sub(rn, rn, rpar)
                with e.While(rn):
                    for i in range(NKV):
                        e.dma_start(out=sh_state[i * 128:(i + 1) * 128, :],
                                    in_=S[i][:]).then_inc(s_hs, 16)
                    e.wait_ge(s_hs, 16 * NKV)
                    e.dma_start(out=sh_canary[0:1, 0:2],
                                in_=meta[0:1, 3:5]).then_inc(s_hs, 16)
                    e.wait_ge(s_hs, 16 * (NKV + 1))
                    e.reg_sub(rn, rn, 1)
            return e.nop()

        def attn(r_dst_override=None):
            WkR = Wk.rearrange("(c p) f -> p c f", p=128)
            WvR = Wv.rearrange("(c p) f -> p c f", p=128)
            WqR = Wq.rearrange("(c p) f -> p c f", p=128)
            WoR = Wo.rearrange("(c p) f -> p c f", p=128)
            aws3 = [a.rearrange("p (c f) -> p c f", f=128) for a in aws]
            r_dst = r_dst_override if r_dst_override is not None else r2_s
            rd_tag = "y" if r_dst_override is not None else "r2"

            ld(wg1f, wg1_t, ["wg1f"])
            ld(wg2f[:], Wg2, ["wg2f"])
            sch.op("dve", lambda e: e.tensor_copy(wg1b[:], wg1f),
                   reads=["wg1f"], writes=["wg1b"])
            sch.op("dve", lambda e: e.tensor_copy(wg2b[:], wg2f[:]),
                   reads=["wg2f"], writes=["wg2b"])

            for h in range(2):
                for dch in range(DCH):
                    for ts in range(2):
                        i = dch * 2 + ts
                        p = i % 2
                        q = 2 * h + ts
                        ld(xbq_t[p], xb_b[dch * 128:(dch + 1) * 128,
                                          q * TQ:(q + 1) * TQ],
                           [f"xbq{p}"], rkeys=[f"xbB_{dch}_{q}"])
                        def xn(e, pp=p, d=dch, tt=ts, qq=q):
                            e.tensor_tensor(xbn_t[pp][:], xbq_t[pp][:],
                                            mrbc[:, qq * 1024:qq * 1024 + 512],
                                            ALU.subtract)
                            return e.tensor_tensor(
                                xnh[:, d * 1024 + tt * 512:
                                    d * 1024 + (tt + 1) * 512],
                                xbn_t[pp][:],
                                mrbc[:, qq * 1024 + 512:(qq + 1) * 1024],
                                ALU.mult)
                        sch.op("dve", xn, reads=[f"xbq{p}", f"mrbc{q}"],
                               writes=["xnh", f"xbn{p}"])

                for ts in range(2):
                    def z1mm(e, tt=ts):
                        for d in range(DCH):
                            ins = e.matmul(pb[4 + tt][0:16, 0:512],
                                           wg1b[:, d * 16:(d + 1) * 16],
                                           xnh[:, d * 1024 + tt * 512:
                                               d * 1024 + (tt + 1) * 512],
                                           start=(d == 0), stop=(d == DCH - 1))
                        return ins
                    sch.op("pe", z1mm, reads=["wg1b", "xnh"],
                           writes=[f"pb{4 + ts}"])
                    sch.op("act", lambda e, tt=ts: e.activation(
                        z1b[0:16, tt * 512:(tt + 1) * 512],
                        pb[4 + tt][0:16, 0:512], AF.Copy),
                           reads=[f"pb{4 + ts}"],
                           writes=["z1b", "o_1_0", "o_1_1"])

                awb3 = awb.rearrange("p (c f) -> p c f", f=256)
                awv3b = awbv.rearrange("p (c f) -> p c f", f=256)
                for fch in range(NKV):
                    pp = fch % 2
                    if pp == 0:
                        sch.op("pool", lambda e, f2=fch // 2: e.dma_start(
                            out=awb3, in_=WkR[:, :, f2 * 256:(f2 + 1) * 256]),
                               dma=True, dsem="aw", writes=["aw0", "aw1"])
                    for ts in range(2):
                        def kmm(e, p2=pp, tt=ts):
                            for d in range(DCH):
                                ins = e.matmul(pb[tt][:, 0:512],
                                               awb[:, d * 256 + p2 * 128:
                                                   d * 256 + (p2 + 1) * 128],
                                               xnh[:, d * 1024 + tt * 512:
                                                   d * 1024 + (tt + 1) * 512],
                                               start=(d == 0),
                                               stop=(d == DCH - 1))
                            return ins
                        sch.op("pe", kmm, reads=[f"aw{pp}", "xnh"],
                               writes=[f"pb{ts}"])
                        sch.op("act", lambda e, f=fch, tt=ts: e.activation(
                            kfm[:, f * 1024 + tt * 512:f * 1024 + (tt + 1) * 512],
                            pb[tt][:, 0:512], AF.Relu, bias=bk_sb[:, f:f + 1]),
                               reads=[f"pb{ts}", "bk"], writes=[f"k_{fch}_{ts}"])
                    if pp == 0:
                        sch.op("pool", lambda e, f2=fch // 2: e.dma_start(
                            out=awv3b, in_=WvR[:, :, f2 * 256:(f2 + 1) * 256]),
                               dma=True, dsem="awv",
                               writes=["awv0", "awv1"] +
                                      [f"o_{f}_{t2}" for f in (3, 4, 5, 6)
                                       for t2 in range(2)])
                    for ts in range(2):
                        def vmm(e, p2=pp, tt=ts):
                            for d in range(DCH):
                                ins = e.matmul(pb[2 + tt][:, 0:512],
                                               awbv[:, d * 256 + p2 * 128:
                                                    d * 256 + (p2 + 1) * 128],
                                               xnh[:, d * 1024 + tt * 512:
                                                   d * 1024 + (tt + 1) * 512],
                                               start=(d == 0),
                                               stop=(d == DCH - 1))
                            return ins
                        sch.op("pe", vmm, reads=[f"awv{pp}", "xnh"],
                               writes=[f"pb{2 + ts}"])
                        sch.op("dve", lambda e, f=fch, tt=ts: e.tensor_scalar_add(
                            vfm[:, f * 1024 + tt * 512:f * 1024 + (tt + 1) * 512],
                            pb[2 + tt][:, 0:512], bv_sb[:, f:f + 1]),
                               reads=[f"pb{2 + ts}", "bv"],
                               writes=[f"v_{fch}_{ts}"])
                        sch.op("pe", lambda e, f=fch, tt=ts: e.matmul(
                            pb[4 + tt][:, 0:512],
                            wg2b[:, f * 128:(f + 1) * 128],
                            z1b[0:16, tt * 512:(tt + 1) * 512],
                            start=True, stop=True),
                               reads=["wg2b", "z1b"], writes=[f"pb{4 + ts}"])
                        sch.op("dve", lambda e, f=fch, tt=ts: e.tensor_scalar_add(
                            zf, pb[4 + tt][:, 0:512], bg2_sb[:, f:f + 1]),
                               reads=[f"pb{4 + ts}", "bg2"], writes=["zf"])
                        sch.op("dve", lambda e: e.tensor_scalar_min(
                            zmin, zf, 0.0), reads=["zf"], writes=["zmin"])
                        def zchain(e):
                            e.activation(zab, zf, AF.Abs)
                            e.activation(zex, zab, AF.Exp, scale=-1.0)
                            return e.activation(zln, zex, AF.Ln, bias=1.0)
                        sch.op("act", zchain, reads=["zf"],
                               writes=["zln", "zf"])
                        sch.op("dve", lambda e, tt=ts: e.tensor_sub(
                            gk_v[:, tt * 512:(tt + 1) * 512], zmin, zln),
                               reads=["zmin", "zln"],
                               writes=["gk", "rcur0", "rcur1", "wg2f"])
                    def scan(e):
                        for cc in range(NCHH):
                            ins = e.tensor_tensor_scan(
                                Bcum[:, cc * 128:(cc + 1) * 128],
                                gk_v[:, cc * 128:(cc + 1) * 128],
                                zeros128[:], 0.0, ALU.add, ALU.add)
                        return ins
                    sch.op("dve", scan, reads=["gk", "zeros128"],
                           writes=["Bcum", "rnew0", "rnew1"])
                    sch.op("act", lambda e, f=fch: e.activation(
                        fq[:, f * 1024:(f + 1) * 1024], Bcum, AF.Exp,
                        scale=GINV),
                           reads=["Bcum"], writes=[f"fq{fch}"])
                    sch.op("act", lambda e: e.activation(
                        ebm_t[:], Bcum, AF.Exp, scale=-GINV),
                           reads=["Bcum"], writes=["ebm", "o_0_0", "o_0_1"])
                    def elbf(e, f=fch):
                        for cc in range(NCHH):
                            ins = e.activation(
                                elb[:, f * 8 + cc:f * 8 + cc + 1],
                                Bcum[:, cc * 128 + 127:cc * 128 + 128],
                                AF.Exp, scale=GINV)
                        return ins
                    sch.op("act", elbf, reads=["Bcum"], writes=[f"elb{fch}"])
                    def kmul(e, f=fch):
                        return e.tensor_tensor(
                            kefm[:, f * 1024:(f + 1) * 1024],
                            kfm[:, f * 1024:(f + 1) * 1024], ebm_t[:],
                            ALU.mult)
                    sch.op("dve", kmul,
                           reads=[f"k_{fch}_0", f"k_{fch}_1", "ebm"],
                           writes=[f"ku_{fch}"])
                    def kendm(e, f=fch):
                        for cc in range(NCHH):
                            ins = e.tensor_scalar_mul(
                                kfm[:, f * 1024 + cc * 128:
                                    f * 1024 + (cc + 1) * 128],
                                kefm[:, f * 1024 + cc * 128:
                                     f * 1024 + (cc + 1) * 128],
                                elb[:, f * 8 + cc:f * 8 + cc + 1])
                        return ins
                    sch.op("dve", kendm, reads=[f"ku_{fch}", f"elb{fch}"],
                           writes=[f"ke_{fch}", f"k_{fch}_0", f"k_{fch}_1"])

                if DEBUG_ATTN and h == 0:
                    sch.drelax.add("dbg")
                    sch.drelax.add("dbg2")
                    sch.drelax.add("dbg3")
                    sch.op("act", lambda e: e.dma_start(out=dbg_k, in_=kfm),
                           reads=[f"ke_{f}" for f in range(NKV)], dma=True,
                           dsem="dbg", writes=["dbgk"])
                    sch.op("act", lambda e: e.dma_start(out=dbg_ku, in_=kefm),
                           reads=[f"ku_{f}" for f in range(NKV)], dma=True,
                           dsem="dbg", writes=["dbgku"])
                    sch.op("act", lambda e: e.dma_start(out=dbg_v, in_=vfm),
                           reads=[f"v_{f}_{t2}" for f in range(NKV)
                                  for t2 in range(2)], dma=True,
                           dsem="dbg", writes=["dbgv"])
                    sch.op("act", lambda e: e.dma_start(out=dbg_fq, in_=fq),
                           reads=[f"fq{f}" for f in range(NKV)], dma=True,
                           dsem="dbg", writes=["dbgfq"])
                    sch.op("act", lambda e: e.dma_start(out=dbg_xn, in_=xnh[:]),
                           reads=["xnh"], dma=True, dsem="dbg", writes=["dbgxn"])

                for qh in range(NQ):
                    pp = qh % 2
                    if pp == 0:
                        sch.op("pool", lambda e, q2=qh // 2: e.dma_start(
                            out=awb3, in_=WqR[:, :, q2 * 256:(q2 + 1) * 256]),
                               dma=True, dsem="aw", writes=["aw0", "aw1"])
                    for ts in range(2):
                        def qmm(e, p2=pp, tt=ts):
                            for d in range(DCH):
                                ins = e.matmul(pb[tt][:, 0:512],
                                               awb[:, d * 256 + p2 * 128:
                                                   d * 256 + (p2 + 1) * 128],
                                               xnh[:, d * 1024 + tt * 512:
                                                   d * 1024 + (tt + 1) * 512],
                                               start=(d == 0),
                                               stop=(d == DCH - 1))
                            return ins
                        sch.op("pe", qmm, reads=[f"aw{pp}", "xnh"],
                               writes=[f"pb{ts}"])
                        sch.op("act", lambda e, qq=qh, tt=ts: e.activation(
                            qbt[tt], pb[tt][:, 0:512], AF.Relu,
                            scale=QSCALE, bias=bq_sb[:, qq:qq + 1]),
                               reads=[f"pb{ts}", "bq"],
                               writes=[f"qbt{ts}", f"o_2_{ts}"])
                        qpw = [f"qp_{qh}_{ts}"]
                        if 8 <= qh < 15:
                            qpw.append(f"fq{qh - 8}")
                        sch.op("dve", lambda e, qq=qh, tt=ts: e.tensor_tensor(
                            qp_ap(qq)[:, tt * 512:(tt + 1) * 512],
                            qbt[tt],
                            fq[:, (qq // 2) * 1024 + tt * 512:
                               (qq // 2) * 1024 + (tt + 1) * 512], ALU.mult),
                               reads=[f"qbt{ts}", f"fq{qh // 2}"],
                               writes=qpw)

                if DEBUG_ATTN and h == 0:
                    for qh in range(NQ):
                        sch.op("act", lambda e, qq=qh: e.dma_start(
                            out=dbg_qp[:, qq * 1024:(qq + 1) * 1024],
                            in_=qp_ap(qq)),
                               reads=[f"qp_{qh}_0", f"qp_{qh}_1"], dma=True,
                               dsem="dbg2", writes=[f"dbgqp{qh}"])

                if h == 0:
                    for qh in range(NQ):
                        rk = [f"qp_{qh}_0", f"qp_{qh}_1"]
                        if 8 <= qh < 15:
                            rk.append(f"fq{qh - 8}")
                        sch.op("sp", lambda e, qq=qh: e.dma_start(
                            out=q_sp[:, qq * 1024:(qq + 1) * 1024],
                            in_=qp_ap(qq)),
                               reads=rk, dma=True, dsem="qsp",
                               writes=[f"qsp_{qh}"])

                for c in range(NCHH):
                    for kvh in range(NKV):
                        u = c * NKV + kvh
                        par = u % 2
                        ts_c = c // 4
                        qh0, qh1 = 2 * kvh, 2 * kvh + 1
                        ksl = kfm[:, kvh * 1024 + c * 128:
                                  kvh * 1024 + (c + 1) * 128]
                        kesl = kefm[:, kvh * 1024 + c * 128:
                                    kvh * 1024 + (c + 1) * 128]
                        vsl = vfm[:, kvh * 1024 + c * 128:
                                  kvh * 1024 + (c + 1) * 128]
                        q0sl = qp_ap(qh0)[:, c * 128:(c + 1) * 128]
                        q1sl = qp_ap(qh1)[:, c * 128:(c + 1) * 128]
                        kread = [f"ke_{kvh}", f"ku_{kvh}", f"v_{kvh}_{ts_c}",
                                 f"qp_{qh0}_{ts_c}", f"qp_{qh1}_{ts_c}"]

                        def t1(e, a=vsl, b2=ksl, pr=par):
                            e.transpose(pbh[pr][:, 0:128], a, ident[:])
                            return e.transpose(pbh[pr][:, 128:256], b2,
                                               ident[:])
                        sch.op("pe", t1, reads=kread + ["ident"],
                               writes=[f"pbh{par}T"])

                        def t2(e, ks=kesl, qa=q0sl, qb=q1sl, pr=par):
                            e.matmul(pb[pr][:, 0:128], ks, qa,
                                     start=True, stop=True)
                            return e.matmul(pb[pr][:, 128:256], ks, qb,
                                            start=True, stop=True)
                        sch.op("pe", t2, reads=kread, writes=[f"pb{par}"])

                        def v1(e, pr=par):
                            e.tensor_copy(vt_sb[pr][:], pbh[pr][:, 0:128])
                            e.tensor_copy(ke_sb[pr][:], pbh[pr][:, 128:256])
                            e.tensor_tensor(atb[2 * pr][:], pb[pr][:, 0:128],
                                            mask[:], ALU.mult)
                            return e.tensor_tensor(atb[2 * pr + 1][:],
                                                   pb[pr][:, 128:256],
                                                   mask[:], ALU.mult)
                        sch.op("dve", v1,
                               reads=[f"pbh{par}T", f"pb{par}", "mask"],
                               writes=[f"vt{par}", f"ke2{par}",
                                       f"atb{2 * par}", f"atb{2 * par + 1}"])

                        def omm(e, pr=par, qa=q0sl, qb=q1sl, kv=kvh):
                            e.matmul(pb[2 + pr][:, 0:128], atb[2 * pr][:],
                                     vt_sb[pr][:], start=True, stop=False)
                            e.matmul(pb[2 + pr][:, 0:128], qa, Sbuf[kv][:],
                                     start=False, stop=True)
                            e.matmul(pb[2 + pr][:, 128:256],
                                     atb[2 * pr + 1][:], vt_sb[pr][:],
                                     start=True, stop=False)
                            e.matmul(pb[2 + pr][:, 128:256], qb, Sbuf[kv][:],
                                     start=False, stop=True)
                            return e.matmul(pb[2 + pr][:, 256:384],
                                            ke_sb[pr][:], vt_sb[pr][:],
                                            start=True, stop=True)
                        sch.op("pe", omm,
                               reads=[f"atb{2 * par}", f"atb{2 * par + 1}",
                                      f"vt{par}", f"ke2{par}", f"Sb{kvh}"]
                                     + kread,
                               writes=[f"pb{2 + par}"])

                        def rms_a(e, pr=par):
                            e.activation(rms_sq[:], pb[2 + pr][:, 0:128],
                                         AF.Square,
                                         accum_out=rms_acc[pr][:, 0:1])
                            return e.activation(rms_sq[:],
                                                pb[2 + pr][:, 128:256],
                                                AF.Square,
                                                accum_out=rms_acc[pr][:, 1:2])
                        sch.op("act", rms_a, reads=[f"pb{2 + par}"],
                               writes=[f"rmsacc{par}"])
                        def rms_b(e, pr=par):
                            e.activation(rms_rstd[pr][:, 0:2],
                                         rms_acc[pr][:, 0:2], AF.Ln,
                                         scale=1.0 / 128.0,
                                         bias=eps_col[:, 0:1])
                            return e.activation(rms_rstd[pr][:, 2:4],
                                                rms_rstd[pr][:, 0:2], AF.Exp,
                                                scale=-0.5)
                        sch.op("act", rms_b, reads=[f"rmsacc{par}", "epsc"],
                               writes=[f"rmssq{par}"])
                        def gmul(e, pr=par):
                            e.scalar_tensor_tensor(
                                onb[2 * pr][:], pb[2 + pr][:, 0:128],
                                rms_rstd[pr][:, 2:3], gnw_bc[:],
                                ALU.mult, ALU.mult)
                            return e.scalar_tensor_tensor(
                                onb[2 * pr + 1][:], pb[2 + pr][:, 128:256],
                                rms_rstd[pr][:, 3:4], gnw_bc[:],
                                ALU.mult, ALU.mult)
                        sch.op("dve", gmul,
                               reads=[f"rmssq{par}", f"pb{2 + par}",
                                      "gnw_bc"],
                               writes=[f"onb{2 * par}", f"onb{2 * par + 1}"])
                        def ot(e, pr=par):
                            e.transpose(pbh[pr][:, 256:384], onb[2 * pr][:],
                                        ident[:])
                            return e.transpose(pbh[pr][:, 384:512],
                                               onb[2 * pr + 1][:], ident[:])
                        sch.op("pe", ot,
                               reads=[f"onb{2 * par}", f"onb{2 * par + 1}",
                                      "ident"],
                               writes=[f"pbh{par}O"])

                        def oev_raw(e, pr=par, a=qh0, b2=qh1, cc=c):
                            e.activation(ofm[:, a * 1024 + cc * 128:
                                             a * 1024 + (cc + 1) * 128],
                                         pb[2 + pr][:, 0:128], AF.Copy)
                            return e.activation(
                                ofm[:, b2 * 1024 + cc * 128:
                                    b2 * 1024 + (cc + 1) * 128],
                                pb[2 + pr][:, 128:256], AF.Copy)
                        sch.op("act", oev_raw,
                               reads=[f"pb{2 + par}", f"pbh{par}O"],
                               writes=[f"o_{qh0}_{ts_c}", f"o_{qh1}_{ts_c}"])

                        gcol = kvh * 16 + h * 8 + c
                        def dexop(e, kv=kvh, g=gcol, cc=c):
                            e.tensor_copy(dex[:, g:g + 1],
                                          dacc[:, kv:kv + 1])
                            return e.tensor_scalar_mul(
                                dacc[:, kv:kv + 1], dacc[:, kv:kv + 1],
                                elb[:, kv * 8 + cc:kv * 8 + cc + 1])
                        sch.op("dve", dexop, reads=[f"elb{kvh}", "dacc"],
                               writes=["dacc", "dexc"])

                        def supd(e, pr=par, kv=kvh, cc=c):
                            e.tensor_scalar_mul(
                                S[kv][:], S[kv][:],
                                elb[:, kv * 8 + cc:kv * 8 + cc + 1])
                            e.tensor_add(S[kv][:], S[kv][:],
                                         pb[2 + pr][:, 256:384])
                            return e.tensor_copy(Sbuf[kv][:], S[kv][:])
                        sch.op("dve", supd,
                               reads=[f"elb{kvh}", f"pb{2 + par}", f"S{kvh}"],
                               writes=[f"S{kvh}", f"Sb{kvh}"])

                    if h == 0 and c in (3, 7):
                        tsX = c // 4
                        for qh in range(NQ):
                            sch.op("sp", lambda e, qq=qh, tt=tsX: e.dma_start(
                                out=o_sp[:, qq * 1024 + tt * 512:
                                         qq * 1024 + (tt + 1) * 512],
                                in_=ofm[:, qq * 1024 + tt * 512:
                                        qq * 1024 + (tt + 1) * 512]),
                                   reads=[f"o_{qh}_{tsX}"], dma=True,
                                   dsem="osp", writes=[f"osp_{qh}_{tsX}"])

                if DEBUG_ATTN and h == 0:
                    sch.op("act", lambda e: e.dma_start(out=dbg_o, in_=ofm[:]),
                           reads=[f"o_{f}_{t2}" for f in range(NQ)
                                  for t2 in range(2)], dma=True,
                           dsem="dbg3", writes=["dbgo"])

            sch.op("pool", hs_write,
                   reads=["meta"] + [f"S{i}" for i in range(NKV)],
                   writes=["hs_done"])
            sch.op("pool", hs_read, reads=["meta", "hs_done"],
                   writes=[f"S{i}" for i in range(NKV)])
            def sbc(e):
                for i in range(NKV):
                    ins = e.tensor_copy(Sbuf[i][:], S[i][:])
                return ins
            sch.op("dve", sbc, reads=[f"S{i}" for i in range(NKV)],
                   writes=[f"Sb{i}" for i in range(NKV)])

            for qh in range(NQ):
                claims = [f"qrl_{qh}"]
                if qh < 8:
                    claims += [f"ke_{qh}", f"k_{qh}_0", f"k_{qh}_1"]
                else:
                    claims += [f"ku_{qh - 8}"]
                sch.op("sp", lambda e, qq=qh: e.dma_start(
                    out=A[:, qq * 1024:(qq + 1) * 1024],
                    in_=q_sp[:, qq * 1024:(qq + 1) * 1024]),
                       reads=[f"qsp_{qh}"], dma=True, dsem="qrl",
                       writes=claims)

            def corr_unit(hh, kvh, c, par):
                gc = hh * 8 + c
                ts_c = c // 4
                qh0, qh1 = 2 * kvh, 2 * kvh + 1
                dxc = dex[:, kvh * 16 + gc:kvh * 16 + gc + 1]
                if hh == 0:
                    qs0 = A[:, qh0 * 1024 + c * 128:
                            qh0 * 1024 + (c + 1) * 128]
                    qs1 = A[:, qh1 * 1024 + c * 128:
                            qh1 * 1024 + (c + 1) * 128]
                    qkeys = [f"qrl_{qh0}", f"qrl_{qh1}"]
                    slot = kvh % 4
                    base = 16384 + slot * 2048
                    os0 = A[:, base + c * 128:base + (c + 1) * 128]
                    os1 = A[:, base + 1024 + c * 128:
                            base + 1024 + (c + 1) * 128]
                    okr = [f"orl{slot}", f"orl{slot}b"]
                    d0 = C[:, qh0 * 1024 + c * 128:
                           qh0 * 1024 + (c + 1) * 128]
                    d1 = C[:, qh1 * 1024 + c * 128:
                           qh1 * 1024 + (c + 1) * 128]
                    wk = [f"oc_{qh0}_{ts_c}", f"oc_{qh1}_{ts_c}", "xnh"]
                else:
                    qs0 = qp_ap(qh0)[:, c * 128:(c + 1) * 128]
                    qs1 = qp_ap(qh1)[:, c * 128:(c + 1) * 128]
                    qkeys = [f"qp_{qh0}_{ts_c}", f"qp_{qh1}_{ts_c}"]
                    os0 = ofm[:, qh0 * 1024 + c * 128:
                              qh0 * 1024 + (c + 1) * 128]
                    os1 = ofm[:, qh1 * 1024 + c * 128:
                              qh1 * 1024 + (c + 1) * 128]
                    okr = [f"o_{qh0}_{ts_c}", f"o_{qh1}_{ts_c}"]
                    d0, d1 = os0, os1
                    wk = [f"o_{qh0}_{ts_c}", f"o_{qh1}_{ts_c}"]

                def wmul(e, pr=par, a=qs0, b2=qs1, dx=dxc):
                    e.tensor_scalar_mul(vt_sb[pr][:], a, dx)
                    return e.tensor_scalar_mul(ke_sb[pr][:], b2, dx)
                sch.op("dve", wmul, reads=qkeys + ["dexc"],
                       writes=[f"vt{par}", f"ke2{par}"])

                def cmm(e, pr=par, kv=kvh):
                    e.matmul(pb[pr][:, 0:128], vt_sb[pr][:], Sbuf[kv][:],
                             start=True, stop=True)
                    return e.matmul(pb[pr][:, 128:256], ke_sb[pr][:],
                                    Sbuf[kv][:], start=True, stop=True)
                sch.op("pe", cmm, reads=[f"vt{par}", f"ke2{par}", f"Sb{kvh}"],
                       writes=[f"pb{par}"])

                def addo(e, pr=par, a=os0, b2=os1):
                    e.tensor_tensor(oc_t[pr][:, 0:128], a,
                                    pb[pr][:, 0:128], ALU.add)
                    return e.tensor_tensor(oc_t[pr][:, 128:256], b2,
                                           pb[pr][:, 128:256], ALU.add)
                sch.op("dve", addo, reads=okr + [f"pb{par}"],
                       writes=[f"oct{par}", "zf"])

                def rms_a(e, pr=par):
                    e.activation(rms_sq[:], oc_t[pr][:, 0:128], AF.Square,
                                 accum_out=rms_acc[pr][:, 0:1])
                    return e.activation(rms_sq[:], oc_t[pr][:, 128:256],
                                        AF.Square,
                                        accum_out=rms_acc[pr][:, 1:2])
                sch.op("act", rms_a, reads=[f"oct{par}"],
                       writes=[f"rmsacc{par}"])

                def rms_b(e, pr=par):
                    e.activation(rms_rstd[pr][:, 0:2], rms_acc[pr][:, 0:2],
                                 AF.Ln, scale=1.0 / 128.0,
                                 bias=eps_col[:, 0:1])
                    return e.activation(rms_rstd[pr][:, 2:4],
                                        rms_rstd[pr][:, 0:2], AF.Exp,
                                        scale=-0.5)
                sch.op("act", rms_b, reads=[f"rmsacc{par}", "epsc"],
                       writes=[f"rmssq{par}"])

                def gmul(e, pr=par):
                    e.scalar_tensor_tensor(onb[2 * pr][:],
                                           oc_t[pr][:, 0:128],
                                           rms_rstd[pr][:, 2:3], gnw_bc[:],
                                           ALU.mult, ALU.mult)
                    return e.scalar_tensor_tensor(onb[2 * pr + 1][:],
                                                  oc_t[pr][:, 128:256],
                                                  rms_rstd[pr][:, 3:4],
                                                  gnw_bc[:],
                                                  ALU.mult, ALU.mult)
                sch.op("dve", gmul,
                       reads=[f"rmssq{par}", f"oct{par}", "gnw_bc"],
                       writes=[f"onb{2 * par}", f"onb{2 * par + 1}"])

                def ot(e, pr=par):
                    e.transpose(pbh[pr][:, 256:384], onb[2 * pr][:],
                                ident[:])
                    return e.transpose(pbh[pr][:, 384:512],
                                       onb[2 * pr + 1][:], ident[:])
                sch.op("pe", ot,
                       reads=[f"onb{2 * par}", f"onb{2 * par + 1}", "ident"],
                       writes=[f"pbh{par}O"])

                def oevc(e, pr=par, a=d0, b2=d1):
                    e.activation(a, pbh[pr][:, 256:384], AF.Copy)
                    return e.activation(b2, pbh[pr][:, 384:512], AF.Copy)
                sch.op("act", oevc, reads=[f"pbh{par}O"], writes=wk)

            def wo_phase(hh, interleave=None):
                rhsb = C if hh == 0 else ofm
                okey = "oc" if hh == 0 else "o"
                for douh in range(DCH):
                    if interleave is not None:
                        interleave(douh)
                    pp = douh % 2
                    if pp == 0:
                        sch.op("pool", lambda e, d2=douh // 2: e.dma_start(
                            out=awb3, in_=WoR[:, :, d2 * 256:(d2 + 1) * 256]),
                               dma=True, dsem="aw", writes=["aw0", "aw1"])
                    for ts in range(2):
                        q = 2 * hh + ts
                        def wmm(e, p2=pp, tt=ts, rb=rhsb):
                            for f in range(NQ):
                                ins = e.matmul(pb[tt][:, 0:512],
                                               awb[:, f * 256 + p2 * 128:
                                                   f * 256 + (p2 + 1) * 128],
                                               rb[:, f * 1024 + tt * 512:
                                                  f * 1024 + (tt + 1) * 512],
                                               start=(f == 0),
                                               stop=(f == NQ - 1))
                            return ins
                        sch.op("pe", wmm,
                               reads=[f"aw{pp}"]
                                     + [f"{okey}_{f}_{ts}" for f in range(NQ)],
                               writes=[f"pb{ts}"])
                        sxb = (4, 5) if ts == 0 else (2, 3)
                        epilogue_tile(ts, douh, q, douh * 2 + ts, r1_s, "r1",
                                      r_dst, rd_tag, xb_a, "xbA",
                                      (sxb[0], sxb[1], douh == 0,
                                       douh == DCH - 1))
                for ts in range(2):
                    sxb = (4, 5) if ts == 0 else (2, 3)
                    ln_finalize(2 * hh + ts, sxb[0], sxb[1])

            for hh in range(2):
                for kvh in range(NKV):
                    if hh == 0:
                        slot = kvh % 4
                        base = 16384 + slot * 2048
                        for j2 in range(2):
                            q2 = 2 * kvh + j2
                            wrk = [f"orl{slot}" if j2 == 0
                                   else f"orl{slot}b"]
                            if kvh < 4:
                                wrk += [f"v_{2 * slot + j2}_0",
                                        f"v_{2 * slot + j2}_1"]
                            sch.op("sp", lambda e, q3=q2, b2=base, j3=j2:
                                   e.dma_start(
                                       out=A[:, b2 + j3 * 1024:
                                             b2 + (j3 + 1) * 1024],
                                       in_=o_sp[:, q3 * 1024:
                                                (q3 + 1) * 1024]),
                                   reads=[f"osp_{q2}_0", f"osp_{q2}_1"],
                                   dma=True, dsem=f"orl{slot}", writes=wrk)
                    if hh == 0:
                        for c in range(NCHH):
                            corr_unit(hh, kvh, c, (kvh * 8 + c) % 2)
                if hh == 0:
                    def ileave(douh):
                        for j in range(4):
                            idx = douh * 4 + j
                            corr_unit(1, idx // 8, idx % 8, idx % 2)
                    wo_phase(0, interleave=ileave)
                else:
                    wo_phase(1)
            sch.fence()

        if n_sub >= 2:
            attn(r_dst_override=(y_out if n_sub == 2 else None))
        if n_sub >= 3:
            ffn(1)

        sch.fence()
        sems = sch.emit(nc, block, ctx)

        dtotals = [(sems[("d", ds)], 16 * n) for ds, n in sch.dcnt.items()]

        @block.sync
        def _(sy):
            for sm, n in dtotals:
                sy.wait_ge(sm, n)

        @block.gpsimd
        def _(g):
            for sm, n in dtotals:
                g.wait_ge(sm, n)

    return nc



def _hadamard(n):
    h = np.array([[1.0]], dtype=np.float32)
    while h.shape[0] < n:
        h = np.block([[h, h], [h, -h]])
    return h


def _consts():
    ident = np.eye(128, dtype=np.float32).astype(ml_dtypes.bfloat16)
    h128 = _hadamard(128).astype(ml_dtypes.bfloat16)
    h64 = _hadamard(64)
    h64x2 = np.zeros((128, 128), np.float32)
    h64x2[0:64, 0:64] = h64
    h64x2[64:128, 64:128] = h64
    h64x2 = h64x2.astype(ml_dtypes.bfloat16)
    mask = np.triu(np.ones((128, 128), np.float32)).astype(ml_dtypes.bfloat16)
    ones_col = np.ones((128, 1), np.float32).astype(ml_dtypes.bfloat16)
    ones_row = np.ones((1, 128), np.float32).astype(ml_dtypes.bfloat16)
    return ident, h128, h64x2, mask, ones_col, ones_row


def _slabT(w):
    return np.ascontiguousarray(
        w.reshape(64, 128, 16, 128).transpose(0, 3, 2, 1).reshape(64, 128, 2048))


def _relayout_col128(vec, ncol):
    return np.ascontiguousarray(vec.reshape(ncol, 128).T)


def kernel(L0_attn_hidden_states, L0_attn_Wq, L0_attn_bq, L0_attn_Wk,
           L0_attn_bk, L0_attn_Wv, L0_attn_bv, L0_attn_Wg1, L0_attn_Wg2,
           L0_attn_bg2, L0_attn_g_norm_w, L0_attn_Wo, L0_ffn_W1, L0_ffn_W2,
           L0_ffn_W3, _n_sub=3):
    x = np.asarray(L0_attn_hidden_states, dtype=np.float32)
    ident, h128, h64x2, mask, ones_col, ones_row = _consts()

    Wg1_t = np.ascontiguousarray(
        np.asarray(L0_attn_Wg1, np.float32).reshape(16, 128, 16)
        .transpose(1, 0, 2).reshape(128, 256))

    common = {
        "Wq": np.asarray(L0_attn_Wq, np.float32),
        "Wk": np.asarray(L0_attn_Wk, np.float32),
        "Wv": np.asarray(L0_attn_Wv, np.float32),
        "Wg2": np.asarray(L0_attn_Wg2, np.float32),
        "Wo": np.asarray(L0_attn_Wo, np.float32),
        "W1S": _slabT(np.asarray(L0_ffn_W1, np.float32)),
        "W2T": np.ascontiguousarray(np.asarray(L0_ffn_W2, np.float32).T),
        "W3S": _slabT(np.asarray(L0_ffn_W3, np.float32)),
        "wg1_t": Wg1_t,
        "bq_t": _relayout_col128(np.asarray(L0_attn_bq, np.float32), 16),
        "bk_t": _relayout_col128(np.asarray(L0_attn_bk, np.float32), 8),
        "bv_t": _relayout_col128(np.asarray(L0_attn_bv, np.float32), 8),
        "bg2_t": _relayout_col128(np.asarray(L0_attn_bg2, np.float32), 8),
        "gnw_t": np.ascontiguousarray(
            np.asarray(L0_attn_g_norm_w, np.float32).reshape(1, 128)),
        "ident_c": ident, "h128_c": h128, "h64x2_c": h64x2, "mask_c": mask,
        "ones_col_c": ones_col, "ones_row_c": ones_row,
    }

    rng = np.random.default_rng()
    nonce = int(rng.integers(1 << 16, 1 << 30))

    nc = build_nc(n_sub=_n_sub)
    in_maps = []
    for c in range(8):
        b, half = c // 2, c % 2
        xc = np.ascontiguousarray(x[b, half * 2048:(half + 1) * 2048, :].T)
        pair_nonce = nonce + (c // 2)
        m = dict(common)
        m["x_in"] = xc
        m["meta_c"] = np.array(
            [[pair_nonce, c % 2, c, pair_nonce, c, 0, 0, 0]], dtype=np.int32)
        in_maps.append(m)

    res = run_bass_kernel_spmd(nc, in_maps, core_ids=list(range(8)))

    if DEBUG_ATTN:
        global _DBG
        _DBG = {k: np.asarray(v, dtype=np.float32)
                for k, v in res.results[0].items() if k.startswith("dbg")}

    out = np.empty((4, 4096, 2048), dtype=np.float32)
    for c in range(8):
        b, half = c // 2, c % 2
        out[b, half * 2048:(half + 1) * 2048, :] = \
            np.asarray(res.results[c]["y_out"]).T
    return out


if __name__ == "__main__":
    import sys
    ns = int(sys.argv[1]) if len(sys.argv) > 1 else 3
    nc = build_nc(n_sub=ns)
    ni = sum(len(v) for v in nc.to_json().get("functions", [{}])[0].get(
        "instructions", [])) if False else 0
    print("build ok")


# version 22
# speedup vs baseline: 1.0527x; 1.0022x over previous
import contextlib

import numpy as np
import ml_dtypes

import concourse.bass as bass
import concourse.mybir as mybir
from concourse.bass_utils import run_bass_kernel_spmd

F32 = mybir.dt.float32
BF16 = mybir.dt.bfloat16
I32 = mybir.dt.int32
AF = mybir.ActivationFunctionType
ALU = mybir.AluOpType

D = 2048
T = 2048
HF = 8192
GN = 64
DCH = 16
TQ = 512
NTQ = 4
TH = 1024
NCHH = 8
NKV = 8
NQ = 16
EPS = 1e-5
FWHT_SCALE = 0.5 / float(np.sqrt(8192.0))
QSCALE = 1.0 / float(np.sqrt(128.0))
GINV = 1.0 / 16.0
POLL_MAX = 500000

ENGS = ["sp", "pe", "dve", "act", "pool"]
_BODY = {"sp": "sync", "pe": "tensor", "dve": "vector", "act": "scalar",
         "pool": "gpsimd"}


PARANOID = False
DEBUG_ATTN = False


class Sched:

    def __init__(self):
        self.ops = {e: [] for e in ENGS}
        self.ccnt = {e: 0 for e in ENGS}
        self.dcnt = {}
        self.down = {}
        self.dclosed = {}
        self.drelax = set()
        self.dflushonly = set()
        self.dflush = {}
        self.last_w = {}
        self.readers = {}
        self.floor = {}

    def op(self, eng, fn, reads=(), writes=(), dma=False, dsem=None):
        if dma:
            assert dsem is not None, "dma ops need a dsem"
            self.down.setdefault(dsem, set()).add(eng)
            assert not self.dclosed.get(dsem), f"dsem {dsem} closed"
        deps = {}
        floor = None
        if PARANOID:
            floor = {("c", e): n for e, n in self.ccnt.items()}
            for ds, n in self.dcnt.items():
                floor[("d", ds)] = n * 16
        else:
            floor = self.floor
        me_stream = ("d", dsem) if dma else ("c", eng)
        for key, tgt in floor.items():
            if key != me_stream and tgt > 0:
                if deps.get(key, 0) < tgt:
                    deps[key] = tgt

        def need(kind, ident, cnt):
            if kind == "c":
                key = ("c", ident)
                if key == me_stream:
                    return
                tgt = cnt
            else:
                ds = ident
                if ds in self.dflush and cnt <= self.dflush[ds][0]:
                    _, feng, fcnt = self.dflush[ds]
                    need("c", feng, fcnt)
                    return
                if ds in self.drelax:
                    cnt = self.dcnt[ds]
                    self.dclosed[ds] = True
                key = ("d", ds)
                if key == me_stream:
                    return
                tgt = cnt * 16
            if deps.get(key, 0) < tgt:
                deps[key] = tgt

        def need_entry(ent):
            kind, ident, cnt = ent
            need(kind, ident, cnt)

        for r in reads:
            w = self.last_w.get(r)
            if w is not None:
                need_entry(w)
        for w_ in writes:
            lw = self.last_w.get(w_)
            if lw is not None:
                need_entry(lw)
            for rd in self.readers.get(w_, ()):
                need_entry(rd)

        for key in list(deps):
            if key[0] == "d":
                deps[key] = 16 * self.dcnt[key[1]]
        if dma:
            self.dcnt[dsem] = self.dcnt.get(dsem, 0) + 1
            me = ("d", dsem, self.dcnt[dsem])
        else:
            self.ccnt[eng] += 1
            me = ("c", eng, self.ccnt[eng])
        self.ops[eng].append((fn, dma, dsem, deps))
        for r in reads:
            self.readers.setdefault(r, []).append(me)
        for w_ in writes:
            self.last_w[w_] = me
            self.readers[w_] = []

    def fence(self):
        by_eng = {}
        for ds, engs in self.down.items():
            n = self.dcnt.get(ds, 0)
            if n > 0 and self.dflush.get(ds, (0,))[0] < n:
                for eng in engs:
                    by_eng.setdefault(eng, []).append((ds, n))
        for eng, lst in by_eng.items():
            def flush(e, lst=tuple(lst)):
                for ds, n in lst:
                    e.wait_ge(self._sems[("d", ds)], 16 * n)
                return e.nop()
            self.op(eng, flush)
            for ds, n in lst:
                self.dflush[ds] = (n, eng, self.ccnt[eng])
        self.floor = {("c", e): n for e, n in self.ccnt.items() if n > 0}

    def emit(self, nc, block, ctx):
        sems = {}
        for e in ENGS:
            sems[("c", e)] = ctx.enter_context(nc.semaphore(f"sc_{e}"))
        for ds in self.down:
            sems[("d", ds)] = ctx.enter_context(nc.semaphore(f"sd_{ds}"))
        self._sems = sems

        def body_for(e):
            ops = self.ops[e]

            def run(eng):
                hw = {}
                for fn, dma, dsem, deps in ops:
                    for key, tgt in deps.items():
                        if hw.get(key, 0) >= tgt:
                            continue
                        eng.wait_ge(sems[key], tgt)
                        hw[key] = tgt
                    instr = fn(eng)
                    if dma:
                        instr.then_inc(sems[("d", dsem)], 16)
                    else:
                        instr.then_inc(sems[("c", e)], 1)
            return run

        for e in ENGS:
            getattr(block, _BODY[e])(body_for(e))
        return sems


def build_nc(n_sub=3):
    nc = bass.Bass("TRN2", target_bir_lowering=False)

    x_in = nc.dram_tensor("x_in", [D, T], F32, kind="ExternalInput").ap()
    y_out = nc.dram_tensor("y_out", [D, T], F32, kind="ExternalOutput").ap()

    Wq = nc.dram_tensor("Wq", [D, D], F32, kind="ExternalInput").ap()
    Wk = nc.dram_tensor("Wk", [D, 1024], F32, kind="ExternalInput").ap()
    Wv = nc.dram_tensor("Wv", [D, 1024], F32, kind="ExternalInput").ap()
    Wg2 = nc.dram_tensor("Wg2", [16, 1024], F32, kind="ExternalInput").ap()
    Wo = nc.dram_tensor("Wo", [D, D], F32, kind="ExternalInput").ap()
    W1S = nc.dram_tensor("W1S", [GN, 128, D], F32, kind="ExternalInput").ap()
    W2T = nc.dram_tensor("W2T", [HF, D], F32, kind="ExternalInput").ap()
    W3S = nc.dram_tensor("W3S", [GN, 128, D], F32, kind="ExternalInput").ap()

    wg1_t = nc.dram_tensor("wg1_t", [128, 256], F32, kind="ExternalInput").ap()
    bq_t = nc.dram_tensor("bq_t", [128, 16], F32, kind="ExternalInput").ap()
    bk_t = nc.dram_tensor("bk_t", [128, 8], F32, kind="ExternalInput").ap()
    bv_t = nc.dram_tensor("bv_t", [128, 8], F32, kind="ExternalInput").ap()
    bg2_t = nc.dram_tensor("bg2_t", [128, 8], F32, kind="ExternalInput").ap()
    gnw_t = nc.dram_tensor("gnw_t", [1, 128], F32, kind="ExternalInput").ap()

    ident_c = nc.dram_tensor("ident_c", [128, 128], BF16, kind="ExternalInput").ap()
    h128_c = nc.dram_tensor("h128_c", [128, 128], BF16, kind="ExternalInput").ap()
    h64x2_c = nc.dram_tensor("h64x2_c", [128, 128], BF16, kind="ExternalInput").ap()
    mask_c = nc.dram_tensor("mask_c", [128, 128], BF16, kind="ExternalInput").ap()
    ones_col_c = nc.dram_tensor("ones_col_c", [128, 1], BF16, kind="ExternalInput").ap()
    ones_row_c = nc.dram_tensor("ones_row_c", [1, 128], BF16, kind="ExternalInput").ap()
    meta_c = nc.dram_tensor("meta_c", [1, 8], I32, kind="ExternalInput").ap()

    W1T_s = nc.dram_tensor("W1T_s", [GN, 128, D], BF16, kind="Internal").ap()
    W3T_s = nc.dram_tensor("W3T_s", [GN, 128, D], BF16, kind="Internal").ap()
    Z_s = nc.dram_tensor("Z_s", [HF, D], BF16, kind="Internal").ap()
    W2pp_s = nc.dram_tensor("W2pp_s", [4, GN, 128, 512], BF16, kind="Internal").ap()
    r1_s = nc.dram_tensor("r1_s", [D, T], F32, kind="Internal").ap()
    r2_s = nc.dram_tensor("r2_s", [D, T], F32, kind="Internal").ap()
    xb_a = nc.dram_tensor("xb_a", [D, T], BF16, kind="Internal").ap()
    xb_b = nc.dram_tensor("xb_b", [D, T], BF16, kind="Internal").ap()
    q_sp = nc.dram_tensor("q_sp", [128, 16384], BF16, kind="Internal").ap()
    o_sp = nc.dram_tensor("o_sp", [128, 16384], BF16, kind="Internal").ap()

    if DEBUG_ATTN:
        dbg_k = nc.dram_tensor("dbg_k", [128, 8192], BF16, kind="ExternalOutput").ap()
        dbg_ku = nc.dram_tensor("dbg_ku", [128, 8192], BF16, kind="ExternalOutput").ap()
        dbg_v = nc.dram_tensor("dbg_v", [128, 8192], BF16, kind="ExternalOutput").ap()
        dbg_fq = nc.dram_tensor("dbg_fq", [128, 8192], BF16, kind="ExternalOutput").ap()
        dbg_qp = nc.dram_tensor("dbg_qp", [128, 16384], BF16, kind="ExternalOutput").ap()
        dbg_o = nc.dram_tensor("dbg_o", [128, 16384], BF16, kind="ExternalOutput").ap()
        dbg_xn = nc.dram_tensor("dbg_xn", [128, 16384], BF16, kind="ExternalOutput").ap()
    sh_state = nc.dram_tensor("sh_state", [NKV * 128, 128], F32, kind="Internal",
                              addr_space="Shared").ap()
    sh_canary = nc.dram_tensor("sh_canary", [1, 16], I32, kind="Internal",
                               addr_space="Shared").ap()

    with contextlib.ExitStack() as ctx:
        sb = lambda name, shape, dt: ctx.enter_context(nc.sbuf_tensor(name, shape, dt))
        psb = lambda name, shape, dt: ctx.enter_context(nc.psum_tensor(name, shape, dt))

        pb = [psb(f"pb{i}", [128, 512], F32) for i in range(6)]
        pbh = [psb(f"pbh{i}", [128, 1024], BF16) for i in range(2)]

        A = sb("A", [128, 32768], BF16)
        B = sb("B", [128, 16384], BF16)
        C = sb("C", [128, 16384], BF16)
        Dn = sb("Dn", [128, 16384], BF16)
        E = sb("E", [128, 4096], F32)

        ident = sb("ident", [128, 128], BF16)
        h128 = sb("h128", [128, 128], BF16)
        h64x2 = sb("h64x2", [128, 128], BF16)
        mask = sb("mask", [128, 128], BF16)
        gnw_bc = sb("gnw_bc", [128, 128], BF16)
        ones_col = sb("ones_col", [128, 1], BF16)
        ones_row = sb("ones_row", [1, 128], BF16)
        gnw_row = sb("gnw_row", [1, 128], F32)
        gnw_rowb = sb("gnw_rowb", [1, 128], BF16)
        meta = sb("meta", [1, 8], I32)
        can_sb = sb("can_sb", [1, 16], I32)
        bq_sb = sb("bq_sb", [128, 16], F32)
        bk_sb = sb("bk_sb", [128, 8], F32)
        bv_sb = sb("bv_sb", [128, 8], F32)
        bg2_sb = sb("bg2_sb", [128, 8], F32)
        wg1b = sb("wg1b", [128, 256], BF16)
        wg2b = sb("wg2b", [16, 1024], BF16)
        zeros128 = sb("zeros128", [128, 128], F32)
        eps_col = sb("eps_col", [128, 1], F32)
        mrbc = sb("mrbc", [128, 4096], BF16)
        rowf_t = sb("rowf_t", [128, 1024], F32)
        rowb = sb("rowb", [1, 1024], BF16)
        S = [sb(f"S{i}", [128, 128], F32) for i in range(NKV)]
        Sbuf = [sb(f"Sb{i}", [128, 128], BF16) for i in range(NKV)]
        elb = sb("elb", [128, 64], F32)
        atb = [sb(f"atb{i}", [128, 128], BF16) for i in range(4)]
        vt_sb = [sb(f"vt{i}", [128, 128], BF16) for i in range(2)]
        ke_sb = [sb(f"ke{i}", [128, 128], BF16) for i in range(2)]
        onb = [sb(f"onb{i}", [128, 128], BF16) for i in range(4)]
        otmp = [sb(f"otmp{i}", [128, 128], BF16) for i in range(4)]
        rms_sq = sb("rms_sq", [128, 128], BF16)
        rms_acc = [sb(f"rmsa{i}", [128, 2], F32) for i in range(2)]
        rms_rstd = [sb(f"rmsr{i}", [128, 8], F32) for i in range(2)]
        dex = sb("dex", [128, 128], F32)
        dacc = sb("dacc", [128, 8], F32)
        lnm2 = sb("lnm2", [1, 512], BF16)

        sgt = [E[:, 0:512], E[:, 512:1024]]
        rcur = [E[:, 1024:1536], E[:, 1536:2048]]
        rnew = [E[:, 2048:2560], E[:, 2560:3072]]
        w3f = [E[:, 1024:2048], E[:, 2048:3072]]
        zf = E[:, 0:512]
        zmin = E[:, 512:1024]
        zab = E[:, 3072:3584]
        zex = E[:, 0:512]
        zln = E[:, 3072:3584]
        wg1f = E[:, 3584:3840]
        wg2f = E[0:16, 1024:2048]
        gk_v = E[:, 1024:2048]
        Bcum = E[:, 2048:3072]
        oc_t = [E[:, 0:256], E[:, 256:512]]

        w1sl = [C[:, 0:2048], C[:, 2048:4096]]
        w3sl = [C[:, 4096:6144], C[:, 6144:8192]]
        w2bl4 = [C[:, 8192:10240], C[:, 10240:12288]]
        usb_v = [Dn[:, 0:2048], Dn[:, 2048:4096]]
        zsl = [Dn[:, 4096:6144], Dn[:, 6144:8192]]
        wld = [Dn[:, 8192 + i * 2048:8192 + (i + 1) * 2048] for i in range(4)]
        gsl1 = [Dn[:, 0:2048], Dn[:, 2048:4096]]
        gsl3 = [Dn[:, 4096:6144], Dn[:, 6144:8192]]
        xnh = C
        kfm = A[:, 0:8192]
        kefm = A[:, 8192:16384]
        vfm = A[:, 16384:24576]
        fq = A[:, 24576:32768]
        xnq = B[:, 0:8192]
        awb = B[:, 8192:12288]
        xbq_t = [B[:, 12288:12800], B[:, 12800:13312]]
        xbn_t = [B[:, 13312:13824], B[:, 13824:14336]]
        xbn2_t = [B[:, 14336:14848], B[:, 14848:15360]]

        def qp_ap(qh):
            if qh < 8:
                return B[:, qh * 1024:(qh + 1) * 1024]
            if qh < 15:
                return A[:, 24576 + (qh - 8) * 1024:24576 + (qh - 7) * 1024]
            return B[:, 15360:16384]
        ofm = Dn
        aws = [awb[:, 0:2048], awb[:, 2048:4096]]
        awbv = Dn[:, 3072:7168]
        ebm_t = Dn[:, 0:1024]
        z1b = Dn[0:16, 1024:2048]
        qbt = [Dn[:, 2048:2560], Dn[:, 2560:3072]]

        s_hp = ctx.enter_context(nc.semaphore("s_hp"))
        s_hs = ctx.enter_context(nc.semaphore("s_hs"))

        sch = Sched()
        block = ctx.enter_context(nc.Block())

        def ld(dst, src, wkeys, eng="sp", rkeys=(), dsem=None):
            sch.op(eng, lambda e, d=dst, s=src: e.dma_start(out=d, in_=s),
                   reads=rkeys, writes=wkeys, dma=True,
                   dsem=dsem if dsem is not None else wkeys[0])

        sch.drelax.add("consts")
        ld(ident[:], ident_c, ["ident"], dsem="consts")
        ld(h128[:], h128_c, ["h128"], dsem="consts")
        ld(h64x2[:], h64x2_c, ["h64x2"], dsem="consts")
        ld(mask[:], mask_c, ["mask"], dsem="consts")
        ld(ones_col[:], ones_col_c, ["ones_col"], dsem="consts")
        ld(ones_row[:], ones_row_c, ["ones_row"], dsem="consts")
        ld(meta[:], meta_c, ["meta"], dsem="consts")
        ld(bq_sb[:], bq_t, ["bq"], dsem="consts")
        ld(bk_sb[:], bk_t, ["bk"], dsem="consts")
        ld(bv_sb[:], bv_t, ["bv"], dsem="consts")
        ld(bg2_sb[:], bg2_t, ["bg2"], dsem="consts")
        ld(gnw_row[:], gnw_t, ["gnw_row"], dsem="consts")

        def memsets(e):
            e.memset(zeros128[:], 0.0)
            e.memset(eps_col[:], EPS)
            e.memset(dacc[:], 1.0)
            for i in range(NKV):
                e.memset(S[i][:], 0.0)
                ins = e.memset(Sbuf[i][:], 0.0)
            return ins
        sch.op("pool", memsets,
               writes=["zeros128", "epsc", "dacc"]
                      + [f"S{i}" for i in range(NKV)]
                      + [f"Sb{i}" for i in range(NKV)])

        sch.op("dve", lambda e: e.tensor_scalar_mul(bq_sb[:], bq_sb[:], QSCALE),
               reads=["bq"], writes=["bq"])
        sch.op("dve", lambda e: e.tensor_copy(gnw_rowb[:], gnw_row[:]),
               reads=["gnw_row"], writes=["gnw_rowb"])
        sch.op("pe", lambda e: e.matmul(pb[0][:, 0:128], ones_row[:],
                                        gnw_rowb[:], start=True, stop=True),
               reads=["ones_row", "gnw_rowb"], writes=["pb0"])
        sch.op("act", lambda e: e.activation(gnw_bc[:], pb[0][:, 0:128], AF.Copy),
               reads=["pb0"], writes=["gnw_bc"])

        def ln_finalize(q, bx, bx2):
            def v1(e):
                e.tensor_scalar(rowf_t[0:1, 0:512], pb[bx][0:1, 0:512],
                                1.0 / 2048.0, None, ALU.mult)
                e.tensor_scalar(rowf_t[0:1, 512:1024], pb[bx2][0:1, 0:512],
                                1.0 / 2048.0, None, ALU.mult)
                return e.tensor_copy(rowb[0:1, 0:512], rowf_t[0:1, 0:512])
            sch.op("dve", v1, reads=[f"pb{bx}", f"pb{bx2}"],
                   writes=["rowf", "rowbm"])
            sch.op("act", lambda e: e.activation(lnm2[0:1, 0:512],
                                                 rowf_t[0:1, 0:512], AF.Square),
                   reads=["rowf"], writes=["rowfm2", "lnm2"])
            sch.op("dve", lambda e: e.tensor_sub(rowf_t[0:1, 0:512],
                                                 rowf_t[0:1, 512:1024],
                                                 lnm2[0:1, 0:512]),
                   reads=["rowf", "rowfm2"], writes=["rowf2"])
            sch.op("act", lambda e: e.activation(rowf_t[0:1, 512:1024],
                                                 rowf_t[0:1, 0:512],
                                                 AF.Sqrt,
                                                 bias=eps_col[0:1, 0:1]),
                   reads=["rowf2", "epsc"], writes=["rowf3"])
            def v3(e):
                e.reciprocal(rowf_t[0:1, 0:512], rowf_t[0:1, 512:1024])
                return e.tensor_copy(rowb[0:1, 512:1024], rowf_t[0:1, 0:512])
            sch.op("dve", v3, reads=["rowf3", "rowbm"], writes=["rowb"])
            def bcast(e):
                e.matmul(pb[0][:, 0:512], ones_row[:], rowb[0:1, 0:512],
                         start=True, stop=True)
                return e.matmul(pb[1][:, 0:512], ones_row[:], rowb[0:1, 512:1024],
                                start=True, stop=True)
            sch.op("pe", bcast, reads=["rowb", "ones_row"], writes=["pb0", "pb1"])
            def evb(e):
                e.activation(mrbc[:, q * 1024:q * 1024 + 512], pb[0][:, 0:512],
                             AF.Copy)
                return e.activation(mrbc[:, q * 1024 + 512:(q + 1) * 1024],
                                    pb[1][:, 0:512], AF.Copy)
            sch.op("act", evb, reads=["pb0", "pb1"], writes=[f"mrbc{q}"])

        def epilogue_tile(bk_i, dch, q, j, r_src, rs_key, r_dst, rd_tag,
                          xb_dst, xb_tag, stats):
            p2 = j % 2
            rk = [f"rcur{p2}"] + ([f"{rs_key}_{dch}_{q}"] if rs_key else [])
            ld(rcur[p2], r_src[dch * 128:(dch + 1) * 128, q * TQ:(q + 1) * TQ],
               [f"rcur{p2}"], rkeys=rk)
            sch.op("dve", lambda e, b=bk_i, p=p2: e.tensor_add(
                rnew[p], pb[b][:, 0:512], rcur[p]),
                   reads=[f"pb{bk_i}", f"rcur{p2}"], writes=[f"rnew{p2}"])
            sch.op("act", lambda e, p=p2, d=dch, qq=q, rd=r_dst: e.dma_start(
                out=rd[d * 128:(d + 1) * 128, qq * TQ:(qq + 1) * TQ],
                in_=rnew[p]),
                   reads=[f"rnew{p2}"], dma=True, dsem=f"strnew{p2}",
                   writes=[f"{rd_tag}_{dch}_{q}"])
            if xb_dst is not None:
                sch.op("dve", lambda e, p=p2: e.tensor_copy(xbn_t[p], rnew[p]),
                       reads=[f"rnew{p2}"], writes=[f"xbn{p2}"])
                sch.op("act", lambda e, p=p2, d=dch, qq=q, xd=xb_dst: e.dma_start(
                    out=xd[d * 128:(d + 1) * 128, qq * TQ:(qq + 1) * TQ],
                    in_=xbn_t[p]),
                       reads=[f"xbn{p2}"], dma=True, dsem=f"stxbn{p2}",
                       writes=[f"{xb_tag}_{dch}_{q}"])
            if stats is not None:
                sx, sx2, first, last = stats
                sch.op("act", lambda e, p=p2: e.activation(xbn2_t[p], xbn_t[p],
                                                           AF.Square),
                       reads=[f"xbn{p2}"], writes=[f"xbn2{p2}"])
                def smm(e, p=p2, f=first, l=last):
                    e.matmul(pb[sx][0:1, 0:512], ones_col[:], xbn_t[p][:],
                             start=f, stop=l)
                    return e.matmul(pb[sx2][0:1, 0:512], ones_col[:],
                                    xbn2_t[p][:], start=f, stop=l)
                sch.op("pe", smm, reads=[f"xbn{p2}", f"xbn2{p2}", "ones_col"],
                       writes=[f"pb{sx}", f"pb{sx2}"])


        Z_r = Z_s.rearrange("(a b) d -> b a d", b=128)
        W2T_r = W2T.rearrange("(a b) d -> b a d", b=128)

        def p4_step(b0):
            p = b0 % 2
            sch.op("pool", lambda e, bb=b0, pp=p: e.dma_start(
                out=usb_v[pp][0:64, :], in_=W2T_r[bb]),
                   writes=[f"usb{p}a"], dma=True,
                   dsem=f"usb{p}")
            sch.op("pool", lambda e, bb=b0, pp=p: e.dma_start(
                out=usb_v[pp][64:128, :], in_=W2T_r[bb + 64]),
                   writes=[f"usb{p}b"], dma=True, dsem=f"usb{p}")
            for dsl in range(4):
                bkx = 4 + dsl % 2
                sch.op("pe", lambda e, pp=p, dd=dsl, b2=bkx: e.matmul(
                    pb[b2][:, 0:512], h64x2[:],
                    usb_v[pp][:, dd * 512:(dd + 1) * 512], start=True, stop=True),
                       reads=[f"usb{p}a", f"usb{p}b", "h64x2"],
                       writes=[f"pb{bkx}"])
                sch.op("act", lambda e, pp=p, dd=dsl, b2=bkx: e.activation(
                    zsl[pp][:, dd * 512:(dd + 1) * 512], pb[b2][:, 0:512],
                    AF.Copy),
                       reads=[f"pb{bkx}"], writes=[f"zsl{p}_{dsl}"])
            zreads = [f"zsl{p}_{d}" for d in range(4)]
            sch.op("act", lambda e, bb=b0, pp=p: e.dma_start(
                out=Z_r[bb], in_=zsl[pp][0:64, :]),
                   reads=zreads, dma=True, dsem=f"stzsl{p}", writes=[f"z_{b0}a"])
            sch.op("act", lambda e, bb=b0, pp=p: e.dma_start(
                out=Z_r[bb + 64], in_=zsl[pp][64:128, :]),
                   reads=zreads, dma=True, dsem=f"stzsl{p}", writes=[f"z_{b0}b"])

        z_all = [f"z_{b}a" for b in range(64)] + [f"z_{b}b" for b in range(64)]

        def p5_step(g):
            p = g % 2
            sch.op("sp", lambda e, gg=g, pp=p: e.dma_start(
                out=usb_v[pp], in_=Z_s[gg * 128:(gg + 1) * 128, :]),
                   reads=z_all,
                   writes=[f"usb{p}a", f"usb{p}b"], dma=True, dsem=f"usb{p}")
            for dsl in range(4):
                bkx = 4 + dsl % 2
                sch.op("pe", lambda e, pp=p, dd=dsl, b2=bkx: e.matmul(
                    pb[b2][:, 0:512], h128[:],
                    usb_v[pp][:, dd * 512:(dd + 1) * 512], start=True, stop=True),
                       reads=[f"usb{p}a", "h128"], writes=[f"pb{bkx}"])
                sch.op("act", lambda e, pp=p, dd=dsl, b2=bkx: e.activation(
                    zsl[pp][:, dd * 512:(dd + 1) * 512], pb[b2][:, 0:512],
                    AF.Copy, scale=FWHT_SCALE),
                       reads=[f"pb{bkx}"], writes=[f"zsl{p}_{dsl}"])
                sch.op("act", lambda e, pp=p, dd=dsl, gg=g: e.dma_start(
                    out=W2pp_s[dd, gg], in_=zsl[pp][:, dd * 512:(dd + 1) * 512]),
                       reads=[f"zsl{p}_{dsl}"], dma=True, dsem=f"stzsl{p}",
                       writes=[f"w2pp_{dsl}_{g}"])

        ring = [rcur[0], rcur[1], rnew[0], rnew[1]]
        ringk = ["rcur0", "rcur1", "rnew0", "rnew1"]
        for q in range(NTQ):
            for dch in range(DCH):
                i4 = (q * DCH + dch) % 4
                p = (q * DCH + dch) % 2
                ld(ring[i4], x_in[dch * 128:(dch + 1) * 128,
                                  q * TQ:(q + 1) * TQ],
                   [ringk[i4]], dsem=ringk[i4])
                sch.op("dve", lambda e, pp=p, r4=i4: e.tensor_copy(
                    xbn_t[pp], ring[r4]),
                       reads=[ringk[i4]], writes=[f"xbn{p}"])
                sch.op("act", lambda e, pp=p, d=dch, qq=q: e.dma_start(
                    out=xb_a[d * 128:(d + 1) * 128, qq * TQ:(qq + 1) * TQ],
                    in_=xbn_t[pp]),
                       reads=[f"xbn{p}"], dma=True, dsem=f"stxbn{p}",
                       writes=[f"xbA_{dch}_{q}"])
                sch.op("act", lambda e, pp=p: e.activation(xbn2_t[pp], xbn_t[pp],
                                                           AF.Square),
                       reads=[f"xbn{p}"], writes=[f"xbn2{p}"])
                def smm(e, pp=p, f=(dch == 0), l=(dch == DCH - 1)):
                    e.matmul(pb[4][0:1, 0:512], ones_col[:], xbn_t[pp][:],
                             start=f, stop=l)
                    return e.matmul(pb[5][0:1, 0:512], ones_col[:],
                                    xbn2_t[pp][:], start=f, stop=l)
                sch.op("pe", smm, reads=[f"xbn{p}", f"xbn2{p}", "ones_col"],
                       writes=["pb4", "pb5"])
            ln_finalize(q, 4, 5)

        def ffn(idx, r_dst_override=None):
            r_src = x_in if idx == 0 else r2_s
            rs_key = None if idx == 0 else "r2"
            final = (idx == 1)
            r_dst = r_dst_override if r_dst_override is not None else \
                (r1_s if idx == 0 else y_out)
            rd_tag = "y" if (final or r_dst_override is not None) else "r1"
            xb_dst = None if final else xb_b
            prep = (idx == 0)

            def emit_xn(qq2, tiles):
                for dch in tiles:
                    p2 = dch % 2
                    ld(xbq_t[p2], xb_a[dch * 128:(dch + 1) * 128,
                                       qq2 * TQ:(qq2 + 1) * TQ],
                       [f"xbq{p2}"], rkeys=[f"xbA_{dch}_{qq2}"])
                    def xn(e, pp=p2, d=dch, qq=qq2):
                        e.tensor_tensor(
                            xnq[:, d * 512:(d + 1) * 512], xbq_t[pp][:],
                            mrbc[:, qq * 1024:qq * 1024 + 512],
                            ALU.subtract)
                        return e.tensor_tensor(
                            xnq[:, d * 512:(d + 1) * 512],
                            xnq[:, d * 512:(d + 1) * 512],
                            mrbc[:, qq * 1024 + 512:(qq + 1) * 1024], ALU.mult)
                    sch.op("dve", xn, reads=[f"xbq{p2}", f"mrbc{qq2}"],
                           writes=["xnq"])

            for q in range(NTQ):
                if q == 0:
                    emit_xn(0, range(DCH))
                for g in range(GN):
                    p = g % 2
                    if prep and q == 0:
                        if g < 32:
                            for j in range(2):
                                p4_step(2 * g + j)
                        else:
                            for j in range(2):
                                p5_step(2 * (g - 32) + j)
                        sch.op("pool", lambda e, gg=g, pp=p: e.dma_start(
                            out=w1sl[pp], in_=W1S[gg]),
                               dma=True, dsem=f"w1sl{p}",
                               writes=[f"w1sl{p}"])
                        sch.op("act", lambda e, gg=g, pp=p: e.dma_start(
                            out=W1T_s[gg], in_=w1sl[pp]),
                               reads=[f"w1sl{p}"], dma=True, dsem="stgsl0",
                               writes=[f"w1t_{g}"])
                        for hf in range(2):
                            sch.op("sp", lambda e, gg=g, hh=hf: e.dma_start(
                                out=w3f[hh],
                                in_=W3S[gg, :, hh * 1024:(hh + 1) * 1024]),
                                   dma=True, dsem=f"w3f{hf}",
                                   writes=([f"w3f{hf}", "rcur0",
                                            "rcur1"] if hf == 0 else
                                           [f"w3f{hf}", "rnew0",
                                            "rnew1"]))
                        def cst(e, pp=p):
                            e.tensor_copy(w3sl[pp][:, 0:1024], w3f[0])
                            return e.tensor_copy(w3sl[pp][:, 1024:2048],
                                                 w3f[1])
                        sch.op("dve", cst,
                               reads=["w3f0", "w3f1", "rcur0",
                                      "rcur1", "rnew0", "rnew1"],
                               writes=[f"w3sl{p}"])
                        sch.op("act", lambda e, gg=g, pp=p: e.dma_start(
                            out=W3T_s[gg], in_=w3sl[pp]),
                               reads=[f"w3sl{p}"], dma=True, dsem="stgsl1",
                               writes=[f"w3t_{g}"])
                        lhs1, lhs3 = w1sl[p], w3sl[p]
                        mmreads = [f"w1sl{p}", f"w3sl{p}", "xnq"]
                        mmwrites = [f"pb{2 * p}", f"pb{2 * p + 1}"]
                    else:
                        ld(w1sl[p], W1T_s[g], [f"w1sl{p}"],
                           rkeys=[f"w1t_{g}"])
                        ld(w3sl[p], W3T_s[g], [f"w3sl{p}"],
                           rkeys=[f"w3t_{g}"])
                        lhs1, lhs3 = w1sl[p], w3sl[p]
                        mmreads = [f"w1sl{p}", f"w3sl{p}", "xnq"]
                        mmwrites = [f"pb{2 * p}", f"pb{2 * p + 1}"]
                    def mm1(e, l1=lhs1, l3=lhs3, pp=p):
                        for d in range(DCH):
                            e.matmul(pb[2 * pp][:, 0:512],
                                     l1[:, d * 128:(d + 1) * 128],
                                     xnq[:, d * 512:(d + 1) * 512],
                                     start=(d == 0), stop=(d == DCH - 1))
                        for d in range(DCH):
                            ins = e.matmul(pb[2 * pp + 1][:, 0:512],
                                           l3[:, d * 128:(d + 1) * 128],
                                           xnq[:, d * 512:(d + 1) * 512],
                                           start=(d == 0), stop=(d == DCH - 1))
                        return ins
                    sch.op("pe", mm1, reads=mmreads, writes=mmwrites)
                    sch.op("act", lambda e, pp=p: e.activation(
                        sgt[pp], pb[2 * pp][:, 0:512], AF.Silu),
                           reads=[f"pb{2 * p}"], writes=[f"sgt{p}"])
                    sch.op("dve", lambda e, pp=p, gg=g: e.tensor_tensor(
                        A[:, gg * 512:(gg + 1) * 512], sgt[pp],
                        pb[2 * pp + 1][:, 0:512], ALU.mult),
                           reads=[f"sgt{p}", f"pb{2 * p + 1}"],
                           writes=[f"hid{g}"])
                for quad in range(4):
                    for g4 in range(GN // 4):
                        p = (quad * (GN // 4) + g4) % 2
                        src = W2pp_s[quad, 4 * g4:4 * g4 + 4].rearrange(
                            "g p d -> p g d")
                        sch.op("sp", lambda e, s2=src, pp=p: e.dma_start(
                            out=w2bl4[pp].rearrange("p (g d) -> p g d", d=512),
                            in_=s2),
                               reads=[f"w2pp_{quad}_{4 * g4 + j}"
                                      for j in range(4)],
                               dma=True, dsem=f"w2bl{p}",
                               writes=[f"w2bl{p}"])
                        def mm2(e, pp=p, g0=4 * g4):
                            for j in range(4):
                                for dl in range(4):
                                    ins = e.matmul(
                                        pb[dl][:, 0:512],
                                        w2bl4[pp][:, j * 512 + dl * 128:
                                                  j * 512 + (dl + 1) * 128],
                                        A[:, (g0 + j) * 512:(g0 + j + 1) * 512],
                                        start=(g0 + j == 0),
                                        stop=(g0 + j == GN - 1))
                            return ins
                        sch.op("pe", mm2,
                               reads=[f"w2bl{p}"] +
                                     [f"hid{4 * g4 + j}" for j in range(4)],
                               writes=["pb0", "pb1", "pb2", "pb3"])
                    for dl in range(4):
                        dch = quad * 4 + dl
                        st = None if final else \
                            (4, 5, dch == 0, dch == DCH - 1)
                        epilogue_tile(dl, dch, q, quad * 4 + dl, r_src, rs_key,
                                      r_dst, rd_tag, xb_dst, "xbB", st)
                    if q + 1 < NTQ:
                        emit_xn(q + 1, range(quad * 4, quad * 4 + 4))
                if not final:
                    ln_finalize(q, 4, 5)
            sch.fence()

        ffn(0, r_dst_override=(y_out if n_sub == 1 else None))

        def hs_read(e):
            with (e.register("rnonce") as rnonce, e.register("rpar") as rpar,
                  e.register("rn") as rn, e.register("rw") as rw,
                  e.register("rc") as rc, e.register("riters") as riters,
                  e.register("rcond") as rcond, e.register("rtmp") as rtmp):
                e.reg_mov(rnonce, e.value_load(meta[0:1, 0:1]))
                e.reg_mov(rpar, e.value_load(meta[0:1, 1:2]))
                e.reg_alu(rn, rpar, 0, ALU.add)
                with e.While(rn):
                    e.reg_mov(rw, 0)
                    e.reg_mov(riters, POLL_MAX)
                    e.reg_mov(rcond, 1)
                    with e.While(rcond):
                        e.dma_start(out=can_sb[0:1, 0:2],
                                    in_=sh_canary[0:1, 0:2]).then_inc(s_hp, 16)
                        e.reg_add(rw, rw, 16)
                        e.wait_ge(s_hp, rw)
                        e.reg_mov(rc, e.value_load(can_sb[0:1, 0:1]))
                        e.reg_sub(riters, riters, 1)
                        e.reg_alu(rcond, rc, rnonce, ALU.not_equal)
                        e.reg_alu(rtmp, riters, 0, ALU.is_gt)
                        e.reg_alu(rcond, rcond, rtmp, ALU.bitwise_and)
                    for i in range(NKV):
                        e.dma_start(out=S[i][:],
                                    in_=sh_state[i * 128:(i + 1) * 128, :]
                                    ).then_inc(s_hp, 16)
                    e.reg_add(rw, rw, 16 * NKV)
                    e.wait_ge(s_hp, rw)
                    e.reg_alu(rcond, rc, rnonce, ALU.not_equal)
                    e.reg_mov(rc, e.value_load(can_sb[0:1, 1:2]))
                    e.reg_add(rc, rc, 1)
                    e.reg_mov(rtmp, e.value_load(meta[0:1, 2:3]))
                    e.reg_alu(rtmp, rc, rtmp, ALU.not_equal)
                    e.reg_alu(rcond, rcond, rtmp, ALU.bitwise_or)
                    with e.While(rcond):
                        for i in range(NKV):
                            e.memset(S[i][:], float("nan"))
                        e.reg_mov(rcond, 0)
                    e.reg_sub(rn, rn, 1)
                e.reg_mov(rn, 1)
                e.reg_sub(rn, rn, rpar)
                with e.While(rn):
                    for i in range(NKV):
                        e.memset(S[i][:], 0.0)
                    e.reg_sub(rn, rn, 1)
            return e.nop()

        def hs_write(e):
            with (e.register("rpar2") as rpar, e.register("rn2") as rn):
                e.reg_mov(rpar, e.value_load(meta[0:1, 1:2]))
                e.reg_mov(rn, 1)
                e.reg_sub(rn, rn, rpar)
                with e.While(rn):
                    for i in range(NKV):
                        e.dma_start(out=sh_state[i * 128:(i + 1) * 128, :],
                                    in_=S[i][:]).then_inc(s_hs, 16)
                    e.wait_ge(s_hs, 16 * NKV)
                    e.dma_start(out=sh_canary[0:1, 0:2],
                                in_=meta[0:1, 3:5]).then_inc(s_hs, 16)
                    e.wait_ge(s_hs, 16 * (NKV + 1))
                    e.reg_sub(rn, rn, 1)
            return e.nop()

        def attn(r_dst_override=None):
            WkR = Wk.rearrange("(c p) f -> p c f", p=128)
            WvR = Wv.rearrange("(c p) f -> p c f", p=128)
            WqR = Wq.rearrange("(c p) f -> p c f", p=128)
            WoR = Wo.rearrange("(c p) f -> p c f", p=128)
            aws3 = [a.rearrange("p (c f) -> p c f", f=128) for a in aws]
            r_dst = r_dst_override if r_dst_override is not None else r2_s
            rd_tag = "y" if r_dst_override is not None else "r2"

            ld(wg1f, wg1_t, ["wg1f"])
            ld(wg2f[:], Wg2, ["wg2f"])
            sch.op("dve", lambda e: e.tensor_copy(wg1b[:], wg1f),
                   reads=["wg1f"], writes=["wg1b"])
            sch.op("dve", lambda e: e.tensor_copy(wg2b[:], wg2f[:]),
                   reads=["wg2f"], writes=["wg2b"])

            for h in range(2):
                for dch in range(DCH):
                    for ts in range(2):
                        i = dch * 2 + ts
                        p = i % 2
                        q = 2 * h + ts
                        ld(xbq_t[p], xb_b[dch * 128:(dch + 1) * 128,
                                          q * TQ:(q + 1) * TQ],
                           [f"xbq{p}"], rkeys=[f"xbB_{dch}_{q}"])
                        def xn(e, pp=p, d=dch, tt=ts, qq=q):
                            e.tensor_tensor(xbn_t[pp][:], xbq_t[pp][:],
                                            mrbc[:, qq * 1024:qq * 1024 + 512],
                                            ALU.subtract)
                            return e.tensor_tensor(
                                xnh[:, d * 1024 + tt * 512:
                                    d * 1024 + (tt + 1) * 512],
                                xbn_t[pp][:],
                                mrbc[:, qq * 1024 + 512:(qq + 1) * 1024],
                                ALU.mult)
                        sch.op("dve", xn, reads=[f"xbq{p}", f"mrbc{q}"],
                               writes=["xnh", f"xbn{p}"])

                for ts in range(2):
                    def z1mm(e, tt=ts):
                        for d in range(DCH):
                            ins = e.matmul(pb[4 + tt][0:16, 0:512],
                                           wg1b[:, d * 16:(d + 1) * 16],
                                           xnh[:, d * 1024 + tt * 512:
                                               d * 1024 + (tt + 1) * 512],
                                           start=(d == 0), stop=(d == DCH - 1))
                        return ins
                    sch.op("pe", z1mm, reads=["wg1b", "xnh"],
                           writes=[f"pb{4 + ts}"])
                    sch.op("act", lambda e, tt=ts: e.activation(
                        z1b[0:16, tt * 512:(tt + 1) * 512],
                        pb[4 + tt][0:16, 0:512], AF.Copy),
                           reads=[f"pb{4 + ts}"],
                           writes=["z1b", "o_1_0", "o_1_1"])

                awb3 = awb.rearrange("p (c f) -> p c f", f=256)
                awv3b = awbv.rearrange("p (c f) -> p c f", f=256)
                for fch in range(NKV):
                    pp = fch % 2
                    if pp == 0:
                        sch.op("pool", lambda e, f2=fch // 2: e.dma_start(
                            out=awb3, in_=WkR[:, :, f2 * 256:(f2 + 1) * 256]),
                               dma=True, dsem="aw", writes=["aw0", "aw1"])
                    for ts in range(2):
                        def kmm(e, p2=pp, tt=ts):
                            for d in range(DCH):
                                ins = e.matmul(pb[tt][:, 0:512],
                                               awb[:, d * 256 + p2 * 128:
                                                   d * 256 + (p2 + 1) * 128],
                                               xnh[:, d * 1024 + tt * 512:
                                                   d * 1024 + (tt + 1) * 512],
                                               start=(d == 0),
                                               stop=(d == DCH - 1))
                            return ins
                        sch.op("pe", kmm, reads=[f"aw{pp}", "xnh"],
                               writes=[f"pb{ts}"])
                        sch.op("act", lambda e, f=fch, tt=ts: e.activation(
                            kfm[:, f * 1024 + tt * 512:f * 1024 + (tt + 1) * 512],
                            pb[tt][:, 0:512], AF.Relu, bias=bk_sb[:, f:f + 1]),
                               reads=[f"pb{ts}", "bk"], writes=[f"k_{fch}_{ts}"])
                    if pp == 0:
                        sch.op("pool", lambda e, f2=fch // 2: e.dma_start(
                            out=awv3b, in_=WvR[:, :, f2 * 256:(f2 + 1) * 256]),
                               dma=True, dsem="awv",
                               writes=["awv0", "awv1"] +
                                      [f"o_{f}_{t2}" for f in (3, 4, 5, 6)
                                       for t2 in range(2)])
                    for ts in range(2):
                        def vmm(e, p2=pp, tt=ts):
                            for d in range(DCH):
                                ins = e.matmul(pb[2 + tt][:, 0:512],
                                               awbv[:, d * 256 + p2 * 128:
                                                    d * 256 + (p2 + 1) * 128],
                                               xnh[:, d * 1024 + tt * 512:
                                                   d * 1024 + (tt + 1) * 512],
                                               start=(d == 0),
                                               stop=(d == DCH - 1))
                            return ins
                        sch.op("pe", vmm, reads=[f"awv{pp}", "xnh"],
                               writes=[f"pb{2 + ts}"])
                        sch.op("dve", lambda e, f=fch, tt=ts: e.tensor_scalar_add(
                            vfm[:, f * 1024 + tt * 512:f * 1024 + (tt + 1) * 512],
                            pb[2 + tt][:, 0:512], bv_sb[:, f:f + 1]),
                               reads=[f"pb{2 + ts}", "bv"],
                               writes=[f"v_{fch}_{ts}"])
                        sch.op("pe", lambda e, f=fch, tt=ts: e.matmul(
                            pb[4 + tt][:, 0:512],
                            wg2b[:, f * 128:(f + 1) * 128],
                            z1b[0:16, tt * 512:(tt + 1) * 512],
                            start=True, stop=True),
                               reads=["wg2b", "z1b"], writes=[f"pb{4 + ts}"])
                        sch.op("dve", lambda e, f=fch, tt=ts: e.tensor_scalar_add(
                            zf, pb[4 + tt][:, 0:512], bg2_sb[:, f:f + 1]),
                               reads=[f"pb{4 + ts}", "bg2"], writes=["zf"])
                        sch.op("dve", lambda e: e.tensor_scalar_min(
                            zmin, zf, 0.0), reads=["zf"], writes=["zmin"])
                        def zchain(e):
                            e.activation(zab, zf, AF.Abs)
                            e.activation(zex, zab, AF.Exp, scale=-1.0)
                            return e.activation(zln, zex, AF.Ln, bias=1.0)
                        sch.op("act", zchain, reads=["zf"],
                               writes=["zln", "zf"])
                        sch.op("dve", lambda e, tt=ts: e.tensor_sub(
                            gk_v[:, tt * 512:(tt + 1) * 512], zmin, zln),
                               reads=["zmin", "zln"],
                               writes=["gk", "rcur0", "rcur1", "wg2f"])
                    def scan(e):
                        for cc in range(NCHH):
                            ins = e.tensor_tensor_scan(
                                Bcum[:, cc * 128:(cc + 1) * 128],
                                gk_v[:, cc * 128:(cc + 1) * 128],
                                zeros128[:], 0.0, ALU.add, ALU.add)
                        return ins
                    sch.op("dve", scan, reads=["gk", "zeros128"],
                           writes=["Bcum", "rnew0", "rnew1"])
                    sch.op("act", lambda e, f=fch: e.activation(
                        fq[:, f * 1024:(f + 1) * 1024], Bcum, AF.Exp,
                        scale=GINV),
                           reads=["Bcum"], writes=[f"fq{fch}"])
                    sch.op("act", lambda e: e.activation(
                        ebm_t[:], Bcum, AF.Exp, scale=-GINV),
                           reads=["Bcum"], writes=["ebm", "o_0_0", "o_0_1"])
                    def elbf(e, f=fch):
                        for cc in range(NCHH):
                            ins = e.activation(
                                elb[:, f * 8 + cc:f * 8 + cc + 1],
                                Bcum[:, cc * 128 + 127:cc * 128 + 128],
                                AF.Exp, scale=GINV)
                        return ins
                    sch.op("act", elbf, reads=["Bcum"], writes=[f"elb{fch}"])
                    def kmul(e, f=fch):
                        return e.tensor_tensor(
                            kefm[:, f * 1024:(f + 1) * 1024],
                            kfm[:, f * 1024:(f + 1) * 1024], ebm_t[:],
                            ALU.mult)
                    sch.op("dve", kmul,
                           reads=[f"k_{fch}_0", f"k_{fch}_1", "ebm"],
                           writes=[f"ku_{fch}"])
                    def kendm(e, f=fch):
                        for cc in range(NCHH):
                            ins = e.tensor_scalar_mul(
                                kfm[:, f * 1024 + cc * 128:
                                    f * 1024 + (cc + 1) * 128],
                                kefm[:, f * 1024 + cc * 128:
                                     f * 1024 + (cc + 1) * 128],
                                elb[:, f * 8 + cc:f * 8 + cc + 1])
                        return ins
                    sch.op("dve", kendm, reads=[f"ku_{fch}", f"elb{fch}"],
                           writes=[f"ke_{fch}", f"k_{fch}_0", f"k_{fch}_1"])

                if DEBUG_ATTN and h == 0:
                    sch.drelax.add("dbg")
                    sch.drelax.add("dbg2")
                    sch.drelax.add("dbg3")
                    sch.op("act", lambda e: e.dma_start(out=dbg_k, in_=kfm),
                           reads=[f"ke_{f}" for f in range(NKV)], dma=True,
                           dsem="dbg", writes=["dbgk"])
                    sch.op("act", lambda e: e.dma_start(out=dbg_ku, in_=kefm),
                           reads=[f"ku_{f}" for f in range(NKV)], dma=True,
                           dsem="dbg", writes=["dbgku"])
                    sch.op("act", lambda e: e.dma_start(out=dbg_v, in_=vfm),
                           reads=[f"v_{f}_{t2}" for f in range(NKV)
                                  for t2 in range(2)], dma=True,
                           dsem="dbg", writes=["dbgv"])
                    sch.op("act", lambda e: e.dma_start(out=dbg_fq, in_=fq),
                           reads=[f"fq{f}" for f in range(NKV)], dma=True,
                           dsem="dbg", writes=["dbgfq"])
                    sch.op("act", lambda e: e.dma_start(out=dbg_xn, in_=xnh[:]),
                           reads=["xnh"], dma=True, dsem="dbg", writes=["dbgxn"])

                for qh in range(NQ):
                    pp = qh % 2
                    if pp == 0:
                        sch.op("pool", lambda e, q2=qh // 2: e.dma_start(
                            out=awb3, in_=WqR[:, :, q2 * 256:(q2 + 1) * 256]),
                               dma=True, dsem="aw", writes=["aw0", "aw1"])
                    for ts in range(2):
                        def qmm(e, p2=pp, tt=ts):
                            for d in range(DCH):
                                ins = e.matmul(pb[tt][:, 0:512],
                                               awb[:, d * 256 + p2 * 128:
                                                   d * 256 + (p2 + 1) * 128],
                                               xnh[:, d * 1024 + tt * 512:
                                                   d * 1024 + (tt + 1) * 512],
                                               start=(d == 0),
                                               stop=(d == DCH - 1))
                            return ins
                        sch.op("pe", qmm, reads=[f"aw{pp}", "xnh"],
                               writes=[f"pb{ts}"])
                        sch.op("act", lambda e, qq=qh, tt=ts: e.activation(
                            qbt[tt], pb[tt][:, 0:512], AF.Relu,
                            scale=QSCALE, bias=bq_sb[:, qq:qq + 1]),
                               reads=[f"pb{ts}", "bq"],
                               writes=[f"qbt{ts}", f"o_2_{ts}"])
                        qpw = [f"qp_{qh}_{ts}"]
                        if 8 <= qh < 15:
                            qpw.append(f"fq{qh - 8}")
                        sch.op("dve", lambda e, qq=qh, tt=ts: e.tensor_tensor(
                            qp_ap(qq)[:, tt * 512:(tt + 1) * 512],
                            qbt[tt],
                            fq[:, (qq // 2) * 1024 + tt * 512:
                               (qq // 2) * 1024 + (tt + 1) * 512], ALU.mult),
                               reads=[f"qbt{ts}", f"fq{qh // 2}"],
                               writes=qpw)

                if DEBUG_ATTN and h == 0:
                    for qh in range(NQ):
                        sch.op("act", lambda e, qq=qh: e.dma_start(
                            out=dbg_qp[:, qq * 1024:(qq + 1) * 1024],
                            in_=qp_ap(qq)),
                               reads=[f"qp_{qh}_0", f"qp_{qh}_1"], dma=True,
                               dsem="dbg2", writes=[f"dbgqp{qh}"])

                if h == 0:
                    for qh in range(NQ):
                        rk = [f"qp_{qh}_0", f"qp_{qh}_1"]
                        if 8 <= qh < 15:
                            rk.append(f"fq{qh - 8}")
                        sch.op("sp", lambda e, qq=qh: e.dma_start(
                            out=q_sp[:, qq * 1024:(qq + 1) * 1024],
                            in_=qp_ap(qq)),
                               reads=rk, dma=True, dsem="qsp",
                               writes=[f"qsp_{qh}"])

                for c in range(NCHH):
                    for kvh in range(NKV):
                        u = c * NKV + kvh
                        par = u % 2
                        ts_c = c // 4
                        qh0, qh1 = 2 * kvh, 2 * kvh + 1
                        ksl = kfm[:, kvh * 1024 + c * 128:
                                  kvh * 1024 + (c + 1) * 128]
                        kesl = kefm[:, kvh * 1024 + c * 128:
                                    kvh * 1024 + (c + 1) * 128]
                        vsl = vfm[:, kvh * 1024 + c * 128:
                                  kvh * 1024 + (c + 1) * 128]
                        q0sl = qp_ap(qh0)[:, c * 128:(c + 1) * 128]
                        q1sl = qp_ap(qh1)[:, c * 128:(c + 1) * 128]
                        kread = [f"ke_{kvh}", f"ku_{kvh}", f"v_{kvh}_{ts_c}",
                                 f"qp_{qh0}_{ts_c}", f"qp_{qh1}_{ts_c}"]

                        def t1(e, a=vsl, b2=ksl, pr=par):
                            e.transpose(pbh[pr][:, 0:128], a, ident[:])
                            return e.transpose(pbh[pr][:, 128:256], b2,
                                               ident[:])
                        sch.op("pe", t1, reads=kread + ["ident"],
                               writes=[f"pbh{par}T"])

                        def t2(e, ks=kesl, qa=q0sl, qb=q1sl, pr=par):
                            e.matmul(pb[pr][:, 0:128], ks, qa,
                                     start=True, stop=True)
                            return e.matmul(pb[pr][:, 128:256], ks, qb,
                                            start=True, stop=True)
                        sch.op("pe", t2, reads=kread, writes=[f"pb{par}"])

                        def v1(e, pr=par):
                            e.tensor_copy(vt_sb[pr][:], pbh[pr][:, 0:128])
                            e.tensor_copy(ke_sb[pr][:], pbh[pr][:, 128:256])
                            e.tensor_tensor(atb[2 * pr][:], pb[pr][:, 0:128],
                                            mask[:], ALU.mult)
                            return e.tensor_tensor(atb[2 * pr + 1][:],
                                                   pb[pr][:, 128:256],
                                                   mask[:], ALU.mult)
                        sch.op("dve", v1,
                               reads=[f"pbh{par}T", f"pb{par}", "mask"],
                               writes=[f"vt{par}", f"ke2{par}",
                                       f"atb{2 * par}", f"atb{2 * par + 1}"])

                        def omm(e, pr=par, qa=q0sl, qb=q1sl, kv=kvh):
                            e.matmul(pb[2 + pr][:, 0:128], atb[2 * pr][:],
                                     vt_sb[pr][:], start=True, stop=False)
                            e.matmul(pb[2 + pr][:, 0:128], qa, Sbuf[kv][:],
                                     start=False, stop=True)
                            e.matmul(pb[2 + pr][:, 128:256],
                                     atb[2 * pr + 1][:], vt_sb[pr][:],
                                     start=True, stop=False)
                            e.matmul(pb[2 + pr][:, 128:256], qb, Sbuf[kv][:],
                                     start=False, stop=True)
                            return e.matmul(pb[2 + pr][:, 256:384],
                                            ke_sb[pr][:], vt_sb[pr][:],
                                            start=True, stop=True)
                        sch.op("pe", omm,
                               reads=[f"atb{2 * par}", f"atb{2 * par + 1}",
                                      f"vt{par}", f"ke2{par}", f"Sb{kvh}"]
                                     + kread,
                               writes=[f"pb{2 + par}"])

                        def rms_a(e, pr=par):
                            e.activation(rms_sq[:], pb[2 + pr][:, 0:128],
                                         AF.Square,
                                         accum_out=rms_acc[pr][:, 0:1])
                            return e.activation(rms_sq[:],
                                                pb[2 + pr][:, 128:256],
                                                AF.Square,
                                                accum_out=rms_acc[pr][:, 1:2])
                        sch.op("act", rms_a, reads=[f"pb{2 + par}"],
                               writes=[f"rmsacc{par}"])
                        def rms_b(e, pr=par):
                            e.activation(rms_rstd[pr][:, 0:2],
                                         rms_acc[pr][:, 0:2], AF.Ln,
                                         scale=1.0 / 128.0,
                                         bias=eps_col[:, 0:1])
                            return e.activation(rms_rstd[pr][:, 2:4],
                                                rms_rstd[pr][:, 0:2], AF.Exp,
                                                scale=-0.5)
                        sch.op("act", rms_b, reads=[f"rmsacc{par}", "epsc"],
                               writes=[f"rmssq{par}"])
                        def gmul(e, pr=par):
                            e.scalar_tensor_tensor(
                                onb[2 * pr][:], pb[2 + pr][:, 0:128],
                                rms_rstd[pr][:, 2:3], gnw_bc[:],
                                ALU.mult, ALU.mult)
                            return e.scalar_tensor_tensor(
                                onb[2 * pr + 1][:], pb[2 + pr][:, 128:256],
                                rms_rstd[pr][:, 3:4], gnw_bc[:],
                                ALU.mult, ALU.mult)
                        sch.op("dve", gmul,
                               reads=[f"rmssq{par}", f"pb{2 + par}",
                                      "gnw_bc"],
                               writes=[f"onb{2 * par}", f"onb{2 * par + 1}"])
                        def ot(e, pr=par):
                            e.transpose(pbh[pr][:, 256:384], onb[2 * pr][:],
                                        ident[:])
                            return e.transpose(pbh[pr][:, 384:512],
                                               onb[2 * pr + 1][:], ident[:])
                        sch.op("pe", ot,
                               reads=[f"onb{2 * par}", f"onb{2 * par + 1}",
                                      "ident"],
                               writes=[f"pbh{par}O"])

                        def oev_raw(e, pr=par, a=qh0, b2=qh1, cc=c):
                            e.activation(ofm[:, a * 1024 + cc * 128:
                                             a * 1024 + (cc + 1) * 128],
                                         pb[2 + pr][:, 0:128], AF.Copy)
                            return e.activation(
                                ofm[:, b2 * 1024 + cc * 128:
                                    b2 * 1024 + (cc + 1) * 128],
                                pb[2 + pr][:, 128:256], AF.Copy)
                        sch.op("act", oev_raw,
                               reads=[f"pb{2 + par}", f"pbh{par}O"],
                               writes=[f"o_{qh0}_{ts_c}", f"o_{qh1}_{ts_c}"])

                        gcol = kvh * 16 + h * 8 + c
                        def dexop(e, kv=kvh, g=gcol, cc=c):
                            e.tensor_copy(dex[:, g:g + 1],
                                          dacc[:, kv:kv + 1])
                            return e.tensor_scalar_mul(
                                dacc[:, kv:kv + 1], dacc[:, kv:kv + 1],
                                elb[:, kv * 8 + cc:kv * 8 + cc + 1])
                        sch.op("dve", dexop, reads=[f"elb{kvh}", "dacc"],
                               writes=["dacc", "dexc"])

                        def supd(e, pr=par, kv=kvh, cc=c):
                            e.tensor_scalar_mul(
                                S[kv][:], S[kv][:],
                                elb[:, kv * 8 + cc:kv * 8 + cc + 1])
                            e.tensor_add(S[kv][:], S[kv][:],
                                         pb[2 + pr][:, 256:384])
                            return e.tensor_copy(Sbuf[kv][:], S[kv][:])
                        sch.op("dve", supd,
                               reads=[f"elb{kvh}", f"pb{2 + par}", f"S{kvh}"],
                               writes=[f"S{kvh}", f"Sb{kvh}"])

                    if h == 0 and c in (3, 7):
                        tsX = c // 4
                        for qh in range(NQ):
                            sch.op("sp", lambda e, qq=qh, tt=tsX: e.dma_start(
                                out=o_sp[:, qq * 1024 + tt * 512:
                                         qq * 1024 + (tt + 1) * 512],
                                in_=ofm[:, qq * 1024 + tt * 512:
                                        qq * 1024 + (tt + 1) * 512]),
                                   reads=[f"o_{qh}_{tsX}"], dma=True,
                                   dsem="osp", writes=[f"osp_{qh}_{tsX}"])

                if DEBUG_ATTN and h == 0:
                    sch.op("act", lambda e: e.dma_start(out=dbg_o, in_=ofm[:]),
                           reads=[f"o_{f}_{t2}" for f in range(NQ)
                                  for t2 in range(2)], dma=True,
                           dsem="dbg3", writes=["dbgo"])

            sch.op("pool", hs_write,
                   reads=["meta"] + [f"S{i}" for i in range(NKV)],
                   writes=["hs_done"])
            sch.op("pool", hs_read, reads=["meta", "hs_done"],
                   writes=[f"S{i}" for i in range(NKV)])
            def sbc(e):
                for i in range(NKV):
                    ins = e.tensor_copy(Sbuf[i][:], S[i][:])
                return ins
            sch.op("dve", sbc, reads=[f"S{i}" for i in range(NKV)],
                   writes=[f"Sb{i}" for i in range(NKV)])

            for qh in range(NQ):
                claims = [f"qrl_{qh}"]
                if qh < 8:
                    claims += [f"ke_{qh}", f"k_{qh}_0", f"k_{qh}_1"]
                else:
                    claims += [f"ku_{qh - 8}"]
                sch.op("sp", lambda e, qq=qh: e.dma_start(
                    out=A[:, qq * 1024:(qq + 1) * 1024],
                    in_=q_sp[:, qq * 1024:(qq + 1) * 1024]),
                       reads=[f"qsp_{qh}"], dma=True, dsem="qrl",
                       writes=claims)

            def corr_unit(hh, kvh, c, par):
                gc = hh * 8 + c
                ts_c = c // 4
                qh0, qh1 = 2 * kvh, 2 * kvh + 1
                dxc = dex[:, kvh * 16 + gc:kvh * 16 + gc + 1]
                if hh == 0:
                    qs0 = A[:, qh0 * 1024 + c * 128:
                            qh0 * 1024 + (c + 1) * 128]
                    qs1 = A[:, qh1 * 1024 + c * 128:
                            qh1 * 1024 + (c + 1) * 128]
                    qkeys = [f"qrl_{qh0}", f"qrl_{qh1}"]
                    slot = kvh % 4
                    base = 16384 + slot * 2048
                    os0 = A[:, base + c * 128:base + (c + 1) * 128]
                    os1 = A[:, base + 1024 + c * 128:
                            base + 1024 + (c + 1) * 128]
                    okr = [f"orl{slot}", f"orl{slot}b"]
                    d0 = C[:, qh0 * 1024 + c * 128:
                           qh0 * 1024 + (c + 1) * 128]
                    d1 = C[:, qh1 * 1024 + c * 128:
                           qh1 * 1024 + (c + 1) * 128]
                    wk = [f"oc_{qh0}_{ts_c}", f"oc_{qh1}_{ts_c}", "xnh"]
                else:
                    qs0 = qp_ap(qh0)[:, c * 128:(c + 1) * 128]
                    qs1 = qp_ap(qh1)[:, c * 128:(c + 1) * 128]
                    qkeys = [f"qp_{qh0}_{ts_c}", f"qp_{qh1}_{ts_c}"]
                    os0 = ofm[:, qh0 * 1024 + c * 128:
                              qh0 * 1024 + (c + 1) * 128]
                    os1 = ofm[:, qh1 * 1024 + c * 128:
                              qh1 * 1024 + (c + 1) * 128]
                    okr = [f"o_{qh0}_{ts_c}", f"o_{qh1}_{ts_c}"]
                    d0, d1 = os0, os1
                    wk = [f"o_{qh0}_{ts_c}", f"o_{qh1}_{ts_c}"]

                def wmul(e, pr=par, a=qs0, b2=qs1, dx=dxc):
                    e.tensor_scalar_mul(vt_sb[pr][:], a, dx)
                    return e.tensor_scalar_mul(ke_sb[pr][:], b2, dx)
                sch.op("dve", wmul, reads=qkeys + ["dexc"],
                       writes=[f"vt{par}", f"ke2{par}"])

                def cmm(e, pr=par, kv=kvh):
                    e.matmul(pb[pr][:, 0:128], vt_sb[pr][:], Sbuf[kv][:],
                             start=True, stop=True)
                    return e.matmul(pb[pr][:, 128:256], ke_sb[pr][:],
                                    Sbuf[kv][:], start=True, stop=True)
                sch.op("pe", cmm, reads=[f"vt{par}", f"ke2{par}", f"Sb{kvh}"],
                       writes=[f"pb{par}"])

                def addo(e, pr=par, a=os0, b2=os1):
                    e.tensor_tensor(oc_t[pr][:, 0:128], a,
                                    pb[pr][:, 0:128], ALU.add)
                    return e.tensor_tensor(oc_t[pr][:, 128:256], b2,
                                           pb[pr][:, 128:256], ALU.add)
                sch.op("dve", addo, reads=okr + [f"pb{par}"],
                       writes=[f"oct{par}", "zf"])

                def rms_a(e, pr=par):
                    e.activation(rms_sq[:], oc_t[pr][:, 0:128], AF.Square,
                                 accum_out=rms_acc[pr][:, 0:1])
                    return e.activation(rms_sq[:], oc_t[pr][:, 128:256],
                                        AF.Square,
                                        accum_out=rms_acc[pr][:, 1:2])
                sch.op("act", rms_a, reads=[f"oct{par}"],
                       writes=[f"rmsacc{par}"])

                def rms_b(e, pr=par):
                    e.activation(rms_rstd[pr][:, 0:2], rms_acc[pr][:, 0:2],
                                 AF.Ln, scale=1.0 / 128.0,
                                 bias=eps_col[:, 0:1])
                    return e.activation(rms_rstd[pr][:, 2:4],
                                        rms_rstd[pr][:, 0:2], AF.Exp,
                                        scale=-0.5)
                sch.op("act", rms_b, reads=[f"rmsacc{par}", "epsc"],
                       writes=[f"rmssq{par}"])

                def gmul(e, pr=par):
                    e.scalar_tensor_tensor(onb[2 * pr][:],
                                           oc_t[pr][:, 0:128],
                                           rms_rstd[pr][:, 2:3], gnw_bc[:],
                                           ALU.mult, ALU.mult)
                    return e.scalar_tensor_tensor(onb[2 * pr + 1][:],
                                                  oc_t[pr][:, 128:256],
                                                  rms_rstd[pr][:, 3:4],
                                                  gnw_bc[:],
                                                  ALU.mult, ALU.mult)
                sch.op("dve", gmul,
                       reads=[f"rmssq{par}", f"oct{par}", "gnw_bc"],
                       writes=[f"onb{2 * par}", f"onb{2 * par + 1}"])

                def ot(e, pr=par):
                    e.transpose(pbh[pr][:, 256:384], onb[2 * pr][:],
                                ident[:])
                    return e.transpose(pbh[pr][:, 384:512],
                                       onb[2 * pr + 1][:], ident[:])
                sch.op("pe", ot,
                       reads=[f"onb{2 * par}", f"onb{2 * par + 1}", "ident"],
                       writes=[f"pbh{par}O"])

                def oevc(e, pr=par, a=d0, b2=d1):
                    e.activation(a, pbh[pr][:, 256:384], AF.Copy)
                    return e.activation(b2, pbh[pr][:, 384:512], AF.Copy)
                sch.op("act", oevc, reads=[f"pbh{par}O"], writes=wk)

            def wo_phase(hh):
                rhsb = C if hh == 0 else ofm
                okey = "oc" if hh == 0 else "o"
                for douh in range(DCH):
                    pp = douh % 2
                    if pp == 0:
                        sch.op("pool", lambda e, d2=douh // 2: e.dma_start(
                            out=awb3, in_=WoR[:, :, d2 * 256:(d2 + 1) * 256]),
                               dma=True, dsem="aw", writes=["aw0", "aw1"])
                    for ts in range(2):
                        q = 2 * hh + ts
                        def wmm(e, p2=pp, tt=ts, rb=rhsb):
                            for f in range(NQ):
                                ins = e.matmul(pb[tt][:, 0:512],
                                               awb[:, f * 256 + p2 * 128:
                                                   f * 256 + (p2 + 1) * 128],
                                               rb[:, f * 1024 + tt * 512:
                                                  f * 1024 + (tt + 1) * 512],
                                               start=(f == 0),
                                               stop=(f == NQ - 1))
                            return ins
                        sch.op("pe", wmm,
                               reads=[f"aw{pp}"]
                                     + [f"{okey}_{f}_{ts}" for f in range(NQ)],
                               writes=[f"pb{ts}"])
                        sxb = (4, 5) if ts == 0 else (2, 3)
                        epilogue_tile(ts, douh, q, douh * 2 + ts, r1_s, "r1",
                                      r_dst, rd_tag, xb_a, "xbA",
                                      (sxb[0], sxb[1], douh == 0,
                                       douh == DCH - 1))
                for ts in range(2):
                    sxb = (4, 5) if ts == 0 else (2, 3)
                    ln_finalize(2 * hh + ts, sxb[0], sxb[1])

            for hh in range(2):
                for kvh in range(NKV):
                    if hh == 0:
                        slot = kvh % 4
                        base = 16384 + slot * 2048
                        for j2 in range(2):
                            q2 = 2 * kvh + j2
                            wrk = [f"orl{slot}" if j2 == 0
                                   else f"orl{slot}b"]
                            if kvh < 4:
                                wrk += [f"v_{2 * slot + j2}_0",
                                        f"v_{2 * slot + j2}_1"]
                            sch.op("sp", lambda e, q3=q2, b2=base, j3=j2:
                                   e.dma_start(
                                       out=A[:, b2 + j3 * 1024:
                                             b2 + (j3 + 1) * 1024],
                                       in_=o_sp[:, q3 * 1024:
                                                (q3 + 1) * 1024]),
                                   reads=[f"osp_{q2}_0", f"osp_{q2}_1"],
                                   dma=True, dsem=f"orl{slot}", writes=wrk)
                    for c in range(NCHH):
                        corr_unit(hh, kvh, c, (kvh * 8 + c) % 2)
                wo_phase(hh)
            sch.fence()

        if n_sub >= 2:
            attn(r_dst_override=(y_out if n_sub == 2 else None))
        if n_sub >= 3:
            ffn(1)

        sch.fence()
        sems = sch.emit(nc, block, ctx)

        dtotals = [(sems[("d", ds)], 16 * n) for ds, n in sch.dcnt.items()]

        @block.sync
        def _(sy):
            for sm, n in dtotals:
                sy.wait_ge(sm, n)

        @block.gpsimd
        def _(g):
            for sm, n in dtotals:
                g.wait_ge(sm, n)

    return nc



def _hadamard(n):
    h = np.array([[1.0]], dtype=np.float32)
    while h.shape[0] < n:
        h = np.block([[h, h], [h, -h]])
    return h


def _consts():
    ident = np.eye(128, dtype=np.float32).astype(ml_dtypes.bfloat16)
    h128 = _hadamard(128).astype(ml_dtypes.bfloat16)
    h64 = _hadamard(64)
    h64x2 = np.zeros((128, 128), np.float32)
    h64x2[0:64, 0:64] = h64
    h64x2[64:128, 64:128] = h64
    h64x2 = h64x2.astype(ml_dtypes.bfloat16)
    mask = np.triu(np.ones((128, 128), np.float32)).astype(ml_dtypes.bfloat16)
    ones_col = np.ones((128, 1), np.float32).astype(ml_dtypes.bfloat16)
    ones_row = np.ones((1, 128), np.float32).astype(ml_dtypes.bfloat16)
    return ident, h128, h64x2, mask, ones_col, ones_row


def _slabT(w):
    return np.ascontiguousarray(
        w.reshape(64, 128, 16, 128).transpose(0, 3, 2, 1).reshape(64, 128, 2048))


def _relayout_col128(vec, ncol):
    return np.ascontiguousarray(vec.reshape(ncol, 128).T)


def kernel(L0_attn_hidden_states, L0_attn_Wq, L0_attn_bq, L0_attn_Wk,
           L0_attn_bk, L0_attn_Wv, L0_attn_bv, L0_attn_Wg1, L0_attn_Wg2,
           L0_attn_bg2, L0_attn_g_norm_w, L0_attn_Wo, L0_ffn_W1, L0_ffn_W2,
           L0_ffn_W3, _n_sub=3):
    x = np.asarray(L0_attn_hidden_states, dtype=np.float32)
    ident, h128, h64x2, mask, ones_col, ones_row = _consts()

    Wg1_t = np.ascontiguousarray(
        np.asarray(L0_attn_Wg1, np.float32).reshape(16, 128, 16)
        .transpose(1, 0, 2).reshape(128, 256))

    common = {
        "Wq": np.asarray(L0_attn_Wq, np.float32),
        "Wk": np.asarray(L0_attn_Wk, np.float32),
        "Wv": np.asarray(L0_attn_Wv, np.float32),
        "Wg2": np.asarray(L0_attn_Wg2, np.float32),
        "Wo": np.asarray(L0_attn_Wo, np.float32),
        "W1S": _slabT(np.asarray(L0_ffn_W1, np.float32)),
        "W2T": np.ascontiguousarray(np.asarray(L0_ffn_W2, np.float32).T),
        "W3S": _slabT(np.asarray(L0_ffn_W3, np.float32)),
        "wg1_t": Wg1_t,
        "bq_t": _relayout_col128(np.asarray(L0_attn_bq, np.float32), 16),
        "bk_t": _relayout_col128(np.asarray(L0_attn_bk, np.float32), 8),
        "bv_t": _relayout_col128(np.asarray(L0_attn_bv, np.float32), 8),
        "bg2_t": _relayout_col128(np.asarray(L0_attn_bg2, np.float32), 8),
        "gnw_t": np.ascontiguousarray(
            np.asarray(L0_attn_g_norm_w, np.float32).reshape(1, 128)),
        "ident_c": ident, "h128_c": h128, "h64x2_c": h64x2, "mask_c": mask,
        "ones_col_c": ones_col, "ones_row_c": ones_row,
    }

    rng = np.random.default_rng()
    nonce = int(rng.integers(1 << 16, 1 << 30))

    nc = build_nc(n_sub=_n_sub)
    in_maps = []
    for c in range(8):
        b, half = c // 2, c % 2
        xc = np.ascontiguousarray(x[b, half * 2048:(half + 1) * 2048, :].T)
        pair_nonce = nonce + (c // 2)
        m = dict(common)
        m["x_in"] = xc
        m["meta_c"] = np.array(
            [[pair_nonce, c % 2, c, pair_nonce, c, 0, 0, 0]], dtype=np.int32)
        in_maps.append(m)

    res = run_bass_kernel_spmd(nc, in_maps, core_ids=list(range(8)))

    if DEBUG_ATTN:
        global _DBG
        _DBG = {k: np.asarray(v, dtype=np.float32)
                for k, v in res.results[0].items() if k.startswith("dbg")}

    out = np.empty((4, 4096, 2048), dtype=np.float32)
    for c in range(8):
        b, half = c // 2, c % 2
        out[b, half * 2048:(half + 1) * 2048, :] = \
            np.asarray(res.results[c]["y_out"]).T
    return out


if __name__ == "__main__":
    import sys
    ns = int(sys.argv[1]) if len(sys.argv) > 1 else 3
    nc = build_nc(n_sub=ns)
    ni = sum(len(v) for v in nc.to_json().get("functions", [{}])[0].get(
        "instructions", [])) if False else 0
    print("build ok")
